# version 13
# baseline (speedup 1.0000x reference)
import numpy as np
from contextlib import ExitStack

import concourse.bass as bass
import concourse.mybir as mybir
from concourse.bass import IndirectOffsetOnAxis
from concourse.bass_utils import run_bass_kernel_spmd

F32 = mybir.dt.float32
F32R = mybir.dt.float32r
BF16 = mybir.dt.bfloat16
U16 = mybir.dt.uint16
U32 = mybir.dt.uint32
AF = mybir.ActivationFunctionType
AX = mybir.AxisListType
ALU = mybir.AluOpType

E = 1024
EC = E // 128
S = 2048
CH = 256
NCH = S // CH
NE = 8
CAP = 512
H = 4096
NCORE = 8

ENGINES = ("sync", "scalar", "vector", "tensor", "gpsimd")
A0_STEPS = {"load", "trans", "mm", "evac"}


EPS = 1e-5


def build_program(b_pc=4, debug=False, max_phase=4, sub_phase=9):
    nc = bass.Bass("TRN2", target_bir_lowering=False)

    x_in = nc.dram_tensor("x", [b_pc, S, E], F32, kind="ExternalInput").ap()
    Wq = nc.dram_tensor("Wq", [E, E], F32, kind="ExternalInput").ap()
    Wk = nc.dram_tensor("Wk", [E, E], F32, kind="ExternalInput").ap()
    Wv = nc.dram_tensor("Wv", [E, E], F32, kind="ExternalInput").ap()
    Wo = nc.dram_tensor("Wo", [E, E], F32, kind="ExternalInput").ap()
    bq = nc.dram_tensor("bq", [E], F32, kind="ExternalInput").ap()
    bk = nc.dram_tensor("bk", [E], F32, kind="ExternalInput").ap()
    bv = nc.dram_tensor("bv", [E], F32, kind="ExternalInput").ap()
    bo = nc.dram_tensor("bo", [E], F32, kind="ExternalInput").ap()
    gw = nc.dram_tensor("gw", [E, NE], F32, kind="ExternalInput").ap()
    w1r = nc.dram_tensor("w1r", [NE, 32, 128, 8, 128], F32, kind="ExternalInput").ap()
    w2f = nc.dram_tensor("w2f", [NE, H, E], F32, kind="ExternalInput").ap()
    consts = nc.dram_tensor("consts", [128, 243], F32, kind="ExternalInput").ap()

    out_d = nc.dram_tensor("out", [b_pc, S, E], F32, kind="ExternalOutput").ap()
    dbg_kind = "ExternalOutput" if debug else "Internal"
    xt_d = nc.dram_tensor("xt_d", [b_pc * S, E], F32, kind=dbg_kind).ap()
    acc_d = nc.dram_tensor("acc_d", [b_pc * S, E], F32, kind=dbg_kind).ap()
    raw_d = nc.dram_tensor("raw_d", [b_pc, NE, S], F32, kind="Internal").ap()
    mean_d = nc.dram_tensor("mean_d", [b_pc, NE, S], F32, kind="Internal").ap()
    sdev_d = nc.dram_tensor("sdev_d", [b_pc, S], F32, kind="Internal").ap()
    if debug:
        probs_d = nc.dram_tensor("probs_d", [32, S], F32, kind="ExternalOutput").ap()
        idxt_d = nc.dram_tensor("idxt_d", [128, 128], U32, kind="ExternalOutput").ap()
        wpt_d = nc.dram_tensor("wpt_d", [128, 128], F32, kind="ExternalOutput").ap()

    ctx = ExitStack()
    sb = lambda name, shape, dt=F32: ctx.enter_context(nc.sbuf_tensor(name, shape, dt))
    psum = lambda name, shape: ctx.enter_context(nc.psum_tensor(name, shape, F32))
    mksem = lambda name: ctx.enter_context(nc.semaphore(name))

    with ctx:
        consts_sb = sb("consts_sb", [128, 243])
        bv9_sb = sb("bv9_sb", [128, 8])
        gsumneg = sb("gsumneg", [8, 1])
        eps_t = sb("eps_t", [128, 1])
        half3_t = sb("half3_t", [128, 1])
        kt_ht = sb("kt_ht", [128, 16384])
        wbuf = sb("wbuf", [128, 8192])
        tokbuf = sb("tokbuf", [128, 8192])
        actbuf = sb("actbuf", [128, 16896])
        gstage = sb("gstage", [8, 1024])
        sstage = sb("sstage", [1, 512])
        rstd_buf = sb("rstd_buf", [128, 16])
        rstdT_sb = sb("rstdT_sb", [16, 128])
        idxT = sb("idxT", [128, 128], U32)
        wpT = sb("wpT", [128, 128])
        idx16_t = sb("idx16_t", [32, 512], U16)
        stat = sb("stat", [128, 16])
        gel_t = sb("gel_t", [128, 1024])

        pb = [psum(f"pb{i}", [128, 512]) for i in range(8)]

        sems = {n: mksem(n) for n in ("dS", "dA", "dG", "tE", "vE", "sE")}
        ctr = {n: 0 for n in sems}
        prog = {e: [] for e in ENGINES}
        trace = {e: [] for e in ENGINES}
        nc._trace = trace

        wrappable = {e: False for e in ENGINES}

        def emit(e, fn, op=True):
            prog[e].append(fn)
            wrappable[e] = op

        def wait(e, sem, val):
            if val <= 0:
                return
            trace[e].append(("wait", sem, val))
            emit(e, lambda q, s=sems[sem], v=val: q.wait_ge(s, v), op=False)

        def drain(e):
            emit(e, lambda q: q.drain())

        def inc(e, sem):
            ctr[sem] += 1
            trace[e].append(("inc", sem, 1))
            assert wrappable[e], f"inc({e},{sem}) with no preceding instruction"
            prev = prog[e][-1]
            prog[e][-1] = (lambda q, p=prev, s=sems[sem]: p(q).then_inc(s, 1))
            return ctr[sem]

        def dma(e, out, in_, sem, slow=False):
            ctr[sem] += 16
            trace[e].append(("inc", sem, 16))
            emit(e, lambda q, o=out, i=in_, s=sems[sem], sl=slow:
                 q.dma_start(out=o, in_=i, allow_slow_non_contiguous=sl).then_inc(s, 16))
            return ctr[sem]

        def idma_gather(out, in_, idx_ap):
            ctr["dG"] += 16
            trace["gpsimd"].append(("inc", "dG", 16))
            emit("gpsimd", lambda q, o=out, i=in_, x=idx_ap, s=sems["dG"]:
                 q.indirect_dma_start(out=o, out_offset=None, in_=i,
                                      in_offset=IndirectOffsetOnAxis(x, 0)).then_inc(s, 16))
            return ctr["dG"]

        def idma_scatter_add(out, idx_ap, in_):
            ctr["dG"] += 16
            trace["gpsimd"].append(("inc", "dG", 16))
            emit("gpsimd", lambda q, o=out, i=in_, x=idx_ap, s=sems["dG"]:
                 q.indirect_dma_start(out=o, out_offset=IndirectOffsetOnAxis(x, 0),
                                      in_=i, in_offset=None,
                                      compute_op=ALU.add).then_inc(s, 16))
            return ctr["dG"]

        def op_matmul(o, l, r, st, sp):
            emit("tensor", lambda q, o=o, l=l, r=r, st=st, sp=sp:
                 q.matmul(o, l, r, start=st, stop=sp))

        def op_transpose(o, i, ident):
            emit("tensor", lambda q, o=o, i=i, d=ident: q.transpose(o, i, d))

        def op_copy(e, o, i):
            emit(e, lambda q, o=o, i=i: (q.tensor_copy(o, i) if e == "vector" else q.copy(o, i)))

        def op_act(o, i, func, bias=None, scale=None, accum=None):
            kw = {}
            if bias is not None:
                kw["bias"] = bias
            if scale is not None:
                kw["scale"] = scale
            if accum is not None:
                kw["accum_out"] = accum
            emit("scalar", lambda q, o=o, i=i, f=func, kw=kw: q.activation(o, i, f, **kw))

        def op_ts(o, i, s1, op0, s2=None, op1=None, accum=None):
            kw = {}
            if op1 is not None:
                kw["op1"] = op1
            if accum is not None:
                kw["accum_out"] = accum
            emit("vector", lambda q, o=o, i=i, s1=s1, s2=s2, op0=op0, kw=kw:
                 q.tensor_scalar(o, i, s1, s2, op0=op0, **kw))

        def op_stt(o, i0, s, i1, op0, op1):
            emit("vector", lambda q, o=o, i0=i0, s=s, i1=i1, a=op0, b=op1:
                 q.scalar_tensor_tensor(o, i0, s, i1, a, b))

        def op_tt(o, a, b, op):
            emit("vector", lambda q, o=o, a=a, b=b, op=op: q.tensor_tensor(o, a, b, op=op))

        def op_red(o, i, negate=False, rsum=False):
            emit("vector", lambda q, o=o, i=i, n=negate, rs=rsum:
                 (q.reduce_sum(o, i, axis=AX.X) if rs else q.reduce_max(o, i, axis=AX.X, negate=n)))

        bank_wait = [None] * 8

        def use_banks(banks):
            for i in banks:
                if bank_wait[i]:
                    wait("tensor", *bank_wait[i])
                    bank_wait[i] = None

        def emit_ln(rt, xt_a, store_wait=0):
            op_red(stat[:, 10:11], rt, rsum=True)
            drain("vector")
            op_ts(stat[:, 11:12], stat[:, 10:11], 1.0 / E, ALU.mult)
            drain("vector")
            if store_wait:
                wait("vector", "dA", store_wait)
            op_ts(xt_a, rt, stat[:, 11:12], ALU.subtract)
            v_c = inc("vector", "vE")
            wait("scalar", "vE", v_c)
            op_act(sq_scr[:], xt_a, AF.Square, accum=stat[:, 12:13])
            drain("scalar")
            op_act(stat[:, 14:15], stat[:, 12:13], AF.Ln, bias=eps_t[:], scale=1.0 / E)
            drain("scalar")
            op_act(stat[:, 13:14], stat[:, 14:15], AF.Exp, scale=-0.5)
            drain("scalar")
            s_r0 = inc("scalar", "sE")
            wait("vector", "sE", s_r0)
            op_ts(stat[:, 12:13], stat[:, 12:13], 1.0 / E, ALU.mult, s2=EPS, op1=ALU.add)
            drain("vector")
            op_tt(stat[:, 14:15], stat[:, 12:13], stat[:, 13:14], ALU.mult)
            drain("vector")
            op_tt(stat[:, 14:15], stat[:, 14:15], stat[:, 13:14], ALU.mult)
            drain("vector")
            v_nt = inc("vector", "vE")
            wait("scalar", "vE", v_nt)
            op_act(stat[:, 14:15], stat[:, 14:15], AF.Identity, bias=half3_t[:], scale=-0.5)
            drain("scalar")
            s_nt = inc("scalar", "sE")
            wait("vector", "sE", s_nt)
            op_tt(stat[:, 13:14], stat[:, 13:14], stat[:, 14:15], ALU.mult)
            drain("vector")
            op_ts(xt_a, xt_a, stat[:, 13:14], ALU.mult)
            inc("vector", "vE")

        ident = consts_sb[:, 0:128]
        ones_col = consts_sb[:, 128:129]
        ones_k8 = consts_sb[0:8, 129:137]
        bcrow8 = consts_sb[0:8, 137:145]
        boff32 = consts_sb[0:32, 145:146]

        xt_flat = xt_d
        acc_flat = acc_d

        v_setup = dma("sync", consts_sb[:], consts, "dS")
        bq_sb = consts_sb[:, 147:155]
        bk_sb = consts_sb[:, 155:163]
        bv_sb = consts_sb[:, 163:171]
        bo_sb = consts_sb[:, 171:179]
        gw_sb = consts_sb[:, 179:243]
        wait("vector", "dS", v_setup)
        op_ts(bv9_sb[:], bv_sb[:], 9.0, ALU.mult)
        emit("vector", lambda q: q.memset(eps_t[:], 1e-5))
        emit("vector", lambda q: q.memset(half3_t[:], 1.5))
        setup_v = inc("vector", "vE")
        wait("tensor", "dS", v_setup)
        for c in range(EC):
            op_matmul(pb[0][0:8, 0:8], gw_sb[:, 8 * c:8 * c + 8],
                      consts_sb[:, 129:137], c == 0, c == EC - 1)
        t_gsum = inc("tensor", "tE")
        wait("scalar", "tE", t_gsum)
        op_act(gsumneg[:], pb[0][0:8, 0:1], AF.Copy, scale=-1.0)
        s_gsum = inc("scalar", "sE")
        bank_wait[0] = ("sE", s_gsum)

        xTn = actbuf[:, 0:2048]
        QTn = actbuf[:, 2048:4096]
        Vn = actbuf[:, 4096:6144]
        combT = actbuf[:, 6144:8192]
        resT = actbuf[:, 8192:10240]
        res_tok = actbuf[:, 10240:12288]
        xt_tok = actbuf[:, 12288:14336]
        sq_scr = actbuf[:, 14336:15360]
        exp_sb = actbuf[:, 15360:15872]
        Ptil = actbuf[:, 15872:16384]
        PtilT = actbuf[:, 16384:16896]
        xT_tb = actbuf[:, 0:4096]

        tok_slot_free = [0, 0]
        wslot_free = [0, 0]
        last_scores_tE = 0
        last_AV_tE = 0
        last_O_tE = 0
        last_resTrans_tE = 0
        last_sq_sE = 0
        last_stt_vE = 0
        gstage_free = [0, 0]
        sstage_free = [0, 0]
        last_xt_dA = 0

        def load_weight_block_A(w_ap, k, slot):
            wait("sync", "tE", wslot_free[slot])
            return dma("sync", wbuf[:, 1024 * slot:1024 * slot + 1024],
                       w_ap[128 * k:128 * k + 128, :], "dS")

        for b in range(b_pc if sub_phase >= 0 else 0):
            for tb in range(4 if sub_phase >= 0 else 0):
                slot = tb % 2
                wait("sync", "tE", tok_slot_free[slot])
                d_x = dma("sync", tokbuf[:, 4096 * slot:4096 * slot + 4096]
                          .rearrange("p (a c) -> p a c", a=4),
                          x_in[b, 512 * tb:512 * tb + 512, :]
                          .rearrange("(a p) c -> p a c", p=128), "dS")
                if "trans" in A0_STEPS:
                    wait("tensor", "dS", d_x)
                    if tb == 0 and b == 0:
                        wait("tensor", "vE", setup_v)
                    for a in range(4):
                        for c in range(EC):
                            bk_i = (a * EC + c) % 4
                            use_banks([bk_i])
                            op_transpose(pb[bk_i][:, 0:128],
                                         tokbuf[:, 4096 * slot + 1024 * a + 128 * c:
                                                4096 * slot + 1024 * a + 128 * c + 128],
                                         ident)
                            t_tr = inc("tensor", "tE")
                            wait("vector", "tE", t_tr)
                            op_copy("vector", xT_tb[:, 512 * c + 128 * a:512 * c + 128 * a + 128],
                                    pb[bk_i][:, 0:128])
                            v_e = inc("vector", "vE")
                            bank_wait[bk_i] = ("vE", v_e)
                    tok_slot_free[slot] = ctr["tE"]
                if "mm" in A0_STEPS:
                    use_banks(range(8))
                    wait("tensor", "vE", ctr["vE"])
                    for k in range(EC):
                        slotw = k % 2
                        d_w = load_weight_block_A(Wk, k, slotw)
                        wait("tensor", "dS", d_w)
                        for ep in range(8):
                            op_matmul(pb[ep][:, 0:512],
                                      wbuf[:, 1024 * slotw + 128 * ep:1024 * slotw + 128 * ep + 128],
                                      xT_tb[:, 512 * k:512 * k + 512],
                                      k == 0, k == EC - 1)
                        wslot_free[slotw] = inc("tensor", "tE")
                if "evac" in A0_STEPS:
                    t_kt = ctr["tE"]
                    wait("scalar", "tE", t_kt)
                    if b > 0:
                        wait("scalar", "tE", last_scores_tE)
                    for ep in range(8):
                        op_act(kt_ht[:, 2048 * ep + 512 * tb:2048 * ep + 512 * tb + 512],
                               pb[ep][:, 0:512], AF.Identity, bias=bk_sb[:, ep:ep + 1])
                    s_kt = inc("scalar", "sE")
                    for ep in range(8):
                        bank_wait[ep] = ("sE", s_kt)
            kt_ready_sE = ctr["sE"]

            for n in range(NCH if sub_phase >= 1 else 0):
                slot = n % 2
                wait("sync", "tE", tok_slot_free[slot])
                d_x = dma("sync", tokbuf[:, 4096 * slot:4096 * slot + 2048]
                          .rearrange("p (a c) -> p a c", a=2),
                          x_in[b, 256 * n:256 * n + 256, :]
                          .rearrange("(a p) c -> p a c", p=128), "dS")
                wait("tensor", "dS", d_x)
                for a in range(2):
                    for c in range(EC):
                        bk_i = (a * EC + c) % 4
                        use_banks([bk_i])
                        op_transpose(pb[bk_i][:, 0:128],
                                     tokbuf[:, 4096 * slot + 1024 * a + 128 * c:
                                            4096 * slot + 1024 * a + 128 * c + 128],
                                     ident)
                        t_tr = inc("tensor", "tE")
                        wait("vector", "tE", t_tr)
                        op_copy("vector", xTn[:, 256 * c + 128 * a:256 * c + 128 * a + 128],
                                pb[bk_i][:, 0:128])
                        v_e = inc("vector", "vE")
                        bank_wait[bk_i] = ("vE", v_e)
                tok_slot_free[slot] = ctr["tE"]
                xTn_v = ctr["vE"]

                use_banks(range(8))
                wait("tensor", "vE", xTn_v)
                for k in range(EC):
                    slotw = k % 2
                    d_w = load_weight_block_A(Wq, k, slotw)
                    wait("tensor", "dS", d_w)
                    for ep in range(8):
                        op_matmul(pb[ep][:, 0:256],
                                  wbuf[:, 1024 * slotw + 128 * ep:1024 * slotw + 128 * ep + 128],
                                  xTn[:, 256 * k:256 * k + 256],
                                  k == 0, k == EC - 1)
                    wslot_free[slotw] = inc("tensor", "tE")
                wait("scalar", "tE", ctr["tE"])
                wait("scalar", "tE", last_scores_tE)
                for ep in range(8):
                    op_act(QTn[:, 256 * ep:256 * ep + 256], pb[ep][:, 0:256],
                           AF.Identity, bias=bq_sb[:, ep:ep + 1])
                s_q = inc("scalar", "sE")
                for ep in range(8):
                    bank_wait[ep] = ("sE", s_q)

                use_banks(range(4))
                for k in range(EC):
                    slotw = k % 2
                    d_w = load_weight_block_A(Wv, k, slotw)
                    wait("tensor", "dS", d_w)
                    for ts in range(2):
                        for eb in range(2):
                            op_matmul(pb[2 * ts + eb][:, 0:512],
                                      xTn[:, 256 * k + 128 * ts:256 * k + 128 * ts + 128],
                                      wbuf[:, 1024 * slotw + 512 * eb:1024 * slotw + 512 * eb + 512],
                                      k == 0, k == EC - 1)
                    wslot_free[slotw] = inc("tensor", "tE")
                wait("vector", "tE", ctr["tE"])
                for ts in range(2):
                    for eb in range(2):
                        op_copy("vector", Vn[:, 1024 * ts + 512 * eb:1024 * ts + 512 * eb + 512],
                                pb[2 * ts + eb][:, 0:512])
                v_v = inc("vector", "vE")
                for i in range(4):
                    bank_wait[i] = ("vE", v_v)

                if b == 0 and n == 0:
                    wait("tensor", "sE", kt_ready_sE)
                first_in_qs = [True, True]
                sc_grp_tE = {}
                s2_done = {}
                s4_done = {}
                s5_done = {}

                def sc_cols(m):
                    return 5 * (m % 2)

                def blk_ap(m):
                    qs, kb, mb = m // 8, (m // 2) % 4, m % 2
                    bk_i = 4 + 2 * qs + (kb % 2)
                    return pb[bk_i][:, 256 * mb:256 * mb + 256], bk_i

                for m in range(16 + 2):
                    if 0 <= m - 1 < 16:
                        mm_ = m - 1
                        cb = sc_cols(mm_)
                        blk, bk_i = blk_ap(mm_)
                        wait("vector", "sE", s2_done[mm_])
                        op_tt(stat[:, cb + 4:cb + 5], stat[:, cb + 1:cb + 2],
                              stat[:, cb + 3:cb + 4], ALU.subtract)
                        drain("vector")
                        if 2 * ((mm_ // 2) % 4) + (mm_ % 2) == n:
                            op_ts(stat[:, cb + 4:cb + 5], stat[:, cb + 4:cb + 5],
                                  float(np.log(2.0)), ALU.add)
                            drain("vector")
                        v_b2 = inc("vector", "vE")
                        wait("scalar", "vE", v_b2)
                        op_act(exp_sb[:, 256 * (mm_ % 2):256 * (mm_ % 2) + 256], blk, AF.Exp,
                               bias=stat[:, cb + 4:cb + 5], scale=1.0 / 32.0)
                        drain("scalar")
                        s4_done[mm_] = inc("scalar", "sE")
                        bank_wait[bk_i] = ("sE", s4_done[mm_])
                    if 0 <= m - 2:
                        mm_ = m - 2
                        qs = mm_ // 8
                        wait("vector", "sE", s4_done[mm_])
                        if first_in_qs[qs]:
                            op_copy("vector", Ptil[:, 256 * qs:256 * qs + 256],
                                    exp_sb[:, 256 * (mm_ % 2):256 * (mm_ % 2) + 256])
                            first_in_qs[qs] = False
                        else:
                            op_tt(Ptil[:, 256 * qs:256 * qs + 256],
                                  Ptil[:, 256 * qs:256 * qs + 256],
                                  exp_sb[:, 256 * (mm_ % 2):256 * (mm_ % 2) + 256], ALU.add)
                        drain("vector")
                        s5_done[mm_] = inc("vector", "vE")
                    if m < 16 and m % 2 == 0:
                        qs, kb = m // 8, (m // 2) % 4
                        bk_i = 4 + 2 * qs + (kb % 2)
                        use_banks([bk_i])
                        if kb == 0:
                            wait("tensor", "sE", s_q)
                        for c in range(EC):
                            op_matmul(pb[bk_i][:, 0:512],
                                      QTn[:, 256 * c + 128 * qs:256 * c + 128 * qs + 128],
                                      kt_ht[:, 2048 * c + 512 * kb:2048 * c + 512 * kb + 512],
                                      c == 0, c == EC - 1)
                        sc_grp_tE[m] = sc_grp_tE[m + 1] = inc("tensor", "tE")
                    if m < 16:
                        cb = sc_cols(m)
                        blk, bk_i = blk_ap(m)
                        wait("vector", "tE", sc_grp_tE[m])
                        if m % 2 == 1:
                            wait("vector", "sE", s4_done[m - 1])
                        op_red(stat[:, cb:cb + 1], blk, negate=True)
                        drain("vector")
                        op_ts(stat[:, cb + 1:cb + 2], stat[:, cb:cb + 1], 1.0 / 32.0, ALU.mult)
                        drain("vector")
                        v_m = inc("vector", "vE")
                        wait("scalar", "vE", v_m)
                        if m >= 2:
                            wait("scalar", "vE", s5_done[m - 2])
                        op_act(exp_sb[:, 256 * (m % 2):256 * (m % 2) + 256], blk, AF.Exp,
                               bias=stat[:, cb + 1:cb + 2], scale=1.0 / 32.0,
                               accum=stat[:, cb + 2:cb + 3])
                        drain("scalar")
                        op_act(stat[:, cb + 3:cb + 4], stat[:, cb + 2:cb + 3], AF.Ln)
                        drain("scalar")
                        s2_done[m] = inc("scalar", "sE")
                ptil_v = ctr["vE"]

                if sub_phase < 3:
                    continue
                wait("tensor", "vE", ptil_v)
                for qs in range(2):
                    for ks in range(2):
                        bk_i = (2 * qs + ks) % 4
                        use_banks([bk_i])
                        op_transpose(pb[bk_i][:, 0:128],
                                     Ptil[:, 256 * qs + 128 * ks:256 * qs + 128 * ks + 128],
                                     ident)
                        t_tr = inc("tensor", "tE")
                        wait("vector", "tE", t_tr)
                        op_copy("vector", PtilT[:, 256 * ks + 128 * qs:256 * ks + 128 * qs + 128],
                                pb[bk_i][:, 0:128])
                        v_e = inc("vector", "vE")
                        bank_wait[bk_i] = ("vE", v_e)
                ptilT_v = ctr["vE"]

                wait("tensor", "vE", ptilT_v)
                for ep in range(8):
                    bk_i = ep % 4
                    use_banks([bk_i])
                    for ks in range(2):
                        op_matmul(pb[bk_i][:, 0:256],
                                  Vn[:, 1024 * ks + 128 * ep:1024 * ks + 128 * ep + 128],
                                  PtilT[:, 256 * ks:256 * ks + 256],
                                  ks == 0, ks == 1)
                    last_AV_tE = inc("tensor", "tE")
                    wait("scalar", "tE", last_AV_tE)
                    if ep == 0:
                        wait("scalar", "tE", last_O_tE)
                    op_act(combT[:, 256 * ep:256 * ep + 256], pb[bk_i][:, 0:256],
                           AF.Identity, bias=bv9_sb[:, ep:ep + 1])
                    s_av = inc("scalar", "sE")
                    bank_wait[bk_i] = ("sE", s_av)
                comb_s = ctr["sE"]

                if sub_phase < 4:
                    continue
                use_banks(range(8))
                wait("tensor", "sE", comb_s)
                for c2 in range(EC):
                    slotw = c2 % 2
                    d_w = load_weight_block_A(Wo, c2, slotw)
                    wait("tensor", "dS", d_w)
                    for ep in range(8):
                        op_matmul(pb[ep][:, 0:256],
                                  wbuf[:, 1024 * slotw + 128 * ep:1024 * slotw + 128 * ep + 128],
                                  combT[:, 256 * c2:256 * c2 + 256],
                                  c2 == 0, c2 == EC - 1)
                    wslot_free[slotw] = inc("tensor", "tE")
                last_O_tE = ctr["tE"]
                wait("vector", "tE", last_O_tE)
                wait("vector", "tE", last_resTrans_tE)
                for ep in range(8):
                    op_stt(resT[:, 256 * ep:256 * ep + 256], pb[ep][:, 0:256],
                           bo_sb[:, ep:ep + 1], xTn[:, 256 * ep:256 * ep + 256],
                           ALU.add, ALU.add)
                v_res = inc("vector", "vE")
                for ep in range(8):
                    bank_wait[ep] = ("vE", v_res)

                if sub_phase < 5:
                    continue
                gslot = n % 2
                use_banks([0, 1])
                wait("tensor", "vE", v_res)
                for c in range(EC):
                    op_matmul(pb[0][0:8, 0:256], gw_sb[:, 8 * c:8 * c + 8],
                              resT[:, 256 * c:256 * c + 256], c == 0, c == EC - 1)
                for c in range(EC):
                    op_matmul(pb[1][0:8, 0:256], consts_sb[:, 129:137],
                              resT[:, 256 * c:256 * c + 256], c == 0, c == EC - 1)
                t_g = inc("tensor", "tE")
                wait("vector", "tE", t_g)
                wait("vector", "dA", gstage_free[gslot])
                op_copy("vector", gstage[0:8, 512 * gslot:512 * gslot + 256], pb[0][0:8, 0:256])
                v_g = inc("vector", "vE")
                wait("scalar", "tE", t_g)
                wait("scalar", "dA", gstage_free[gslot])
                op_act(gstage[0:8, 512 * gslot + 256:512 * gslot + 512], pb[1][0:8, 0:256],
                       AF.Copy, scale=1.0 / E)
                s_g = inc("scalar", "sE")
                bank_wait[0] = ("vE", v_g)
                bank_wait[1] = ("sE", s_g)
                wait("scalar", "vE", v_g)
                d1 = dma("scalar", raw_d[b, :, 256 * n:256 * n + 256],
                         gstage[0:8, 512 * gslot:512 * gslot + 256], "dA")
                d2 = dma("scalar", mean_d[b, :, 256 * n:256 * n + 256],
                         gstage[0:8, 512 * gslot + 256:512 * gslot + 512], "dA")
                gstage_free[gslot] = d2

                if sub_phase < 6:
                    continue
                for c2 in range(EC):
                    for a in range(2):
                        bk_i = 2 + (c2 * 2 + a) % 2
                        use_banks([bk_i])
                        op_transpose(pb[bk_i][:, 0:128],
                                     resT[:, 256 * c2 + 128 * a:256 * c2 + 128 * a + 128],
                                     ident)
                        t_tr = inc("tensor", "tE")
                        wait("vector", "tE", t_tr)
                        if c2 == 0 and a == 0:
                            wait("vector", "sE", last_sq_sE)
                        op_copy("vector", res_tok[:, 1024 * a + 128 * c2:1024 * a + 128 * c2 + 128],
                                pb[bk_i][:, 0:128])
                        v_e = inc("vector", "vE")
                        bank_wait[bk_i] = ("vE", v_e)
                last_resTrans_tE = ctr["tE"]
                v_rtok = ctr["vE"]

                sslot = n % 2
                wait("vector", "vE", v_rtok)
                for a in range(2):
                    emit_ln(res_tok[:, 1024 * a:1024 * a + 1024],
                            xt_tok[:, 1024 * a:1024 * a + 1024],
                            store_wait=(last_xt_dA if a == 0 else 0))
                    v_xt = ctr["vE"]
                    row0 = b * S + 256 * n + 128 * a
                    wait("scalar", "vE", v_xt)
                    dma("scalar", xt_flat[row0:row0 + 128, :],
                        xt_tok[:, 1024 * a:1024 * a + 1024], "dA")
                    last_xt_dA = dma("scalar", acc_flat[row0:row0 + 128, :],
                                     xt_tok[:, 1024 * a:1024 * a + 1024], "dA")
                    drain("vector")
                    op_copy("vector", rstd_buf[:, 2 * n + a:2 * n + a + 1], stat[:, 13:14])
                    inc("vector", "vE")
            use_banks([2])
            wait("tensor", "vE", ctr["vE"])
            op_transpose(pb[2][0:16, 0:128], rstd_buf[:], ident)
            t_sd = inc("tensor", "tE")
            wait("vector", "tE", t_sd)
            op_copy("vector", rstdT_sb[:], pb[2][0:16, 0:128])
            v_sd = inc("vector", "vE")
            bank_wait[2] = ("vE", v_sd)
            wait("scalar", "vE", v_sd)
            dma("scalar", sdev_d[b].rearrange("(j t) -> j t", j=16), rstdT_sb[:], "dA")
            last_scores_tE = ctr["tE"]

        if max_phase <= 1:
            wait("gpsimd", "dA", ctr["dA"])
            wait("gpsimd", "dS", ctr["dS"])
            return _finish(nc, prog, block_ctx=None)

        a_done_dA = ctr["dA"]
        a_done_vE = ctr["vE"]
        a_end_tE = ctr["tE"]
        sel = actbuf[0:32, 0:2048]
        work = [actbuf[0:32, 2048:4096], actbuf[0:32, 4096:6144]]
        braw = actbuf[0:8, 6144:6656]
        bmean = actbuf[0:8, 6656:7168]
        bsdev = actbuf[0:8, 7168:7680]
        btmp = actbuf[0:8, 7680:8192]
        blnz = actbuf[0:8, 8192:8704]
        bprob = actbuf[0:8, 8704:9216]
        val32 = actbuf[0:32, 9216:9728]
        idxf = actbuf[0:32, 9728:10240]
        idx16 = idx16_t[0:32, :]

        wait("sync", "dA", a_done_dA)
        wait("sync", "tE", a_end_tE)
        wait("tensor", "vE", a_done_vE)
        wait("vector", "tE", a_end_tE)
        wait("vector", "sE", s_gsum)
        emit("vector", lambda q: q.memset(bsdev[:], 0.0))
        bsdev_z = inc("vector", "vE")
        wait("sync", "vE", bsdev_z)
        for b in range(b_pc):
            for blk in range(4):
                d1 = dma("sync", braw[:], raw_d[b, :, 512 * blk:512 * blk + 512], "dS")
                d2 = dma("sync", bmean[:], mean_d[b, :, 512 * blk:512 * blk + 512], "dS")
                d3 = dma("sync", bsdev[0:1, :], sdev_d[b, 512 * blk:512 * blk + 512], "dS")
                use_banks([4, 5])
                wait("tensor", "dS", d3)
                op_matmul(pb[4][0:8, 0:512], bcrow8, bsdev[:], True, True)
                t_bc = inc("tensor", "tE")
                wait("vector", "tE", t_bc)
                wait("vector", "dS", d3)
                op_stt(btmp[:], bmean[:], gsumneg[:], braw[:], ALU.mult, ALU.add)
                drain("vector")
                op_tt(btmp[:], btmp[:], pb[4][0:8, 0:512], ALU.mult)
                drain("vector")
                v_sc = inc("vector", "vE")
                bank_wait[4] = ("vE", v_sc)
                wait("scalar", "vE", v_sc)
                op_act(blnz[:], btmp[:], AF.Exp)
                s_ex = inc("scalar", "sE")
                wait("tensor", "sE", s_ex)
                op_matmul(pb[5][0:8, 0:512], ones_k8, blnz[:], True, True)
                t_z = inc("tensor", "tE")
                wait("scalar", "tE", t_z)
                op_act(blnz[:], pb[5][0:8, 0:512], AF.Ln)
                drain("scalar")
                s_ln = inc("scalar", "sE")
                bank_wait[5] = ("sE", s_ln)
                wait("vector", "sE", s_ln)
                op_tt(btmp[:], btmp[:], blnz[:], ALU.subtract)
                drain("vector")
                v_sub = inc("vector", "vE")
                wait("scalar", "vE", v_sub)
                op_act(bprob[:], btmp[:], AF.Exp)
                s_pr = inc("scalar", "sE")
                wait("sync", "sE", s_pr)
                d_sel = dma("sync", sel[8 * b:8 * b + 8, 512 * blk:512 * blk + 512], bprob[:], "dS")
        sel_ready = ctr["dS"]
        if debug:
            wait("gpsimd", "dS", sel_ready)
            dma("gpsimd", probs_d, sel, "dG")

        wait("vector", "dS", sel_ready)
        cur = 0
        emit("vector", lambda q: q.tensor_copy(work[0][:], sel[:]))
        for r in range(64):
            emit("vector", lambda q, r=r, c=cur: q.max(
                val32[:, 8 * r:8 * r + 8], work[c][:]))
            drain("vector")
            emit("vector", lambda q, r=r, c=cur: q.max_index(
                idx16[:, 8 * r:8 * r + 8], val32[:, 8 * r:8 * r + 8], work[c][:]))
            drain("vector")
            emit("vector", lambda q, r=r, c=cur: q.match_replace(
                work[1 - c][:], val32[:, 8 * r:8 * r + 8], work[c][:], 0.0))
            drain("vector")
            cur = 1 - cur
        emit("vector", lambda q: q.tensor_copy(idxf[:], idx16[:]))
        drain("vector")
        op_ts(idxf[:], idxf[:], boff32, ALU.add)
        drain("vector")
        v_self = inc("vector", "vE")
        wait("tensor", "vE", v_self)
        for qb in range(4):
            use_banks([6])
            op_transpose(pb[6][:, 0:32], idxf[:, 128 * qb:128 * qb + 128], ident[0:32, 0:32])
            t_ti = inc("tensor", "tE")
            wait("vector", "tE", t_ti)
            op_copy("vector", idxT[:, 32 * qb:32 * qb + 32], pb[6][:, 0:32])
            v_ti = inc("vector", "vE")
            bank_wait[6] = ("vE", v_ti)
            use_banks([7])
            op_transpose(pb[7][:, 0:32], val32[:, 128 * qb:128 * qb + 128], ident[0:32, 0:32])
            t_tw = inc("tensor", "tE")
            wait("vector", "tE", t_tw)
            op_copy("vector", wpT[:, 32 * qb:32 * qb + 32], pb[7][:, 0:32])
            v_tw = inc("vector", "vE")
            bank_wait[7] = ("vE", v_tw)
        sel_done_vE = ctr["vE"]
        if debug:
            wait("gpsimd", "vE", sel_done_vE)
            dma("gpsimd", idxt_d, idxT[:], "dG")
            dma("gpsimd", wpt_d, wpT[:], "dG")

        if max_phase <= 2:
            wait("gpsimd", "dA", ctr["dA"])
            wait("gpsimd", "dS", ctr["dS"])
            wait("gpsimd", "dG", ctr["dG"])
            wait("gpsimd", "vE", ctr["vE"])
            return _finish(nc, prog, block_ctx=None)

        XgR = actbuf[:, 0:4096]
        XgT_hi = actbuf[:, 4096:6144].bitcast(BF16)
        XgT_lo = actbuf[:, 6144:8192].bitcast(BF16)
        yT = actbuf[:, 8192:12288]
        y_tok = actbuf[:, 12288:16384]
        hT_hi = kt_ht[:, 0:8192].bitcast(BF16)
        hT_lo = kt_ht[:, 8192:16384].bitcast(BF16)
        gel = [gel_t[:, 512 * i:512 * i + 512] for i in range(2)]

        phaseC_t0 = ctr["tE"]
        xg_gather_done = [0, 0]
        xg_slot_free = [0, 0]
        scatter_done = [0] * b_pc
        last_pair_scatter = 0
        last_h_tE = 0
        last_y_tE = 0
        wslotC_free = [a_end_tE] * 4
        gel_free = [0, 0]

        pairs = [(e, b) for e in range(NE) for b in range(b_pc)]
        PASSES = ((0, 0), (0, 1), (1, 0))

        def issue_gathers(p):
            e, b = pairs[p]
            r = 8 * b + e
            slot = p % 2
            wait("gpsimd", "tE", xg_slot_free[slot])
            if p == 0:
                wait("gpsimd", "dA", a_done_dA)
                wait("gpsimd", "vE", sel_done_vE)
            for tt in range(4):
                g = idma_gather(tokbuf[:, 4096 * slot + 1024 * tt:4096 * slot + 1024 * tt + 1024],
                                xt_flat, idxT[:, 32 * tt + r:32 * tt + r + 1])
            xg_gather_done[slot] = g
            return g

        def wslot_views(slotw):
            base = 2048 * slotw
            raw = wbuf[:, base:base + 1024]
            hi = wbuf[:, base + 1024:base + 1536].bitcast(BF16)
            lo = wbuf[:, base + 1536:base + 2048].bitcast(BF16)
            return raw, hi, lo

        def split_wblock(slotw, d_w):
            raw, hi, lo = wslot_views(slotw)
            wait("scalar", "dS", d_w)
            op_act(hi, raw, AF.Copy)
            drain("scalar")
            s_w = inc("scalar", "sE")
            wait("vector", "sE", s_w)
            op_tt(lo, raw, hi, ALU.subtract)
            drain("vector")
            return inc("vector", "vE")

        issue_gathers(0)
        for p, (e, b) in enumerate(pairs):
            r = 8 * b + e
            slot = p % 2
            wait("tensor", "dG", xg_gather_done[slot])
            for tt in range(4):
                for c in range(EC):
                    bk_i = (tt * EC + c) % 4
                    use_banks([bk_i])
                    op_transpose(pb[bk_i][:, 0:128],
                                 tokbuf[:, 4096 * slot + 1024 * tt + 128 * c:
                                        4096 * slot + 1024 * tt + 128 * c + 128],
                                 ident)
                    t_tr = inc("tensor", "tE")
                    wait("vector", "tE", t_tr)
                    if tt == 0 and c == 0:
                        wait("vector", "tE", last_h_tE)
                    op_copy("vector", XgR[:, 512 * c + 128 * tt:512 * c + 128 * tt + 128],
                            pb[bk_i][:, 0:128])
                    v_e2 = inc("vector", "vE")
                    bank_wait[bk_i] = ("vE", v_e2)
            xg_slot_free[slot] = ctr["tE"]
            wait("scalar", "vE", ctr["vE"])
            wait("scalar", "tE", last_h_tE)
            op_act(XgT_hi, XgR, AF.Copy)
            drain("scalar")
            s_x = inc("scalar", "sE")
            wait("vector", "sE", s_x)
            op_tt(XgT_lo, XgR, XgT_hi, ALU.subtract)
            drain("vector")
            xgT_v = inc("vector", "vE")
            if p + 1 < len(pairs):
                issue_gathers(p + 1)

            wait("tensor", "vE", xgT_v)
            v_w_h = {}
            for qq in range(32 + 2):
                if qq < 32:
                    slotw = qq % 4
                    wait("sync", "tE", wslotC_free[slotw])
                    d_w = dma("sync", wslot_views(slotw)[0]
                              .rearrange("p (c j) -> p c j", c=8),
                              w1r[e, qq], "dS")
                    v_w_h[qq] = split_wblock(slotw, d_w)
                if qq < 2:
                    continue
                q = qq - 2
                slotw = q % 4
                _, w_hi, w_lo = wslot_views(slotw)
                wait("tensor", "vE", v_w_h[q])
                bk_i = 4 + (q % 4)
                use_banks([bk_i])
                nmm = 0
                for c in range(EC):
                    for wp, xp in PASSES:
                        wt = (w_lo if wp else w_hi)[:, 128 * c:128 * c + 128]
                        xt_ = (XgT_lo if xp else XgT_hi)[:, 512 * c:512 * c + 512]
                        op_matmul(pb[bk_i][:, 0:512], wt, xt_, nmm == 0, nmm == 23)
                        nmm += 1
                wslotC_free[slotw] = inc("tensor", "tE")
                t_h = ctr["tE"]
                gsl = q % 2
                wait("scalar", "tE", t_h)
                wait("scalar", "vE", gel_free[gsl])
                if q < 2:
                    wait("scalar", "tE", last_y_tE)
                op_act(gel[gsl], pb[bk_i][:, 0:512], AF.Gelu)
                drain("scalar")
                op_act(hT_hi[:, 512 * q:512 * q + 512], gel[gsl], AF.Copy)
                drain("scalar")
                s_h = inc("scalar", "sE")
                bank_wait[bk_i] = ("sE", s_h)
                wait("vector", "sE", s_h)
                op_tt(hT_lo[:, 512 * q:512 * q + 512], gel[gsl],
                      hT_hi[:, 512 * q:512 * q + 512], ALU.subtract)
                drain("vector")
                gel_free[gsl] = inc("vector", "vE")
            last_h_tE = ctr["tE"]
            h_done_vE = ctr["vE"]

            use_banks(range(8))
            wait("tensor", "vE", h_done_vE)
            v_w_y = {}
            for hq in range(32 + 2):
                if hq < 32:
                    slotw = hq % 4
                    wait("sync", "tE", wslotC_free[slotw])
                    d_w = dma("sync", wslot_views(slotw)[0],
                              w2f[e, 128 * hq:128 * hq + 128, :], "dS")
                    v_w_y[hq] = split_wblock(slotw, d_w)
                if hq < 2:
                    continue
                hh = hq - 2
                slotw = hh % 4
                _, w_hi, w_lo = wslot_views(slotw)
                wait("tensor", "vE", v_w_y[hh])
                for cc in range(8):
                    for ip, (wp, xp) in enumerate(PASSES):
                        wt = (w_lo if wp else w_hi)[:, 128 * cc:128 * cc + 128]
                        ht_ = (hT_lo if xp else hT_hi)[:, 512 * hh:512 * hh + 512]
                        op_matmul(pb[cc][:, 0:512], wt, ht_,
                                  hh == 0 and ip == 0, hh == 31 and ip == 2)
                wslotC_free[slotw] = inc("tensor", "tE")
            last_y_tE = ctr["tE"]
            wait("vector", "tE", last_y_tE)
            for cc in range(8):
                op_copy("vector", yT[:, 512 * cc:512 * cc + 512], pb[cc][:, 0:512])
            v_y = inc("vector", "vE")
            for cc in range(8):
                bank_wait[cc] = ("vE", v_y)

            wait("tensor", "vE", v_y)
            for tt in range(4):
                for cc in range(EC):
                    bk_i = (tt * EC + cc) % 4
                    use_banks([bk_i])
                    op_transpose(pb[bk_i][:, 0:128],
                                 yT[:, 512 * cc + 128 * tt:512 * cc + 128 * tt + 128],
                                 ident)
                    t_tr = inc("tensor", "tE")
                    wait("vector", "tE", t_tr)
                    if tt == 0 and cc == 0:
                        wait("vector", "dG", last_pair_scatter)
                    op_ts(y_tok[:, 1024 * tt + 128 * cc:1024 * tt + 128 * cc + 128],
                          pb[bk_i][:, 0:128], wpT[:, 32 * tt + r:32 * tt + r + 1], ALU.mult)
                    v_e3 = inc("vector", "vE")
                    bank_wait[bk_i] = ("vE", v_e3)
            last_ytok_vE = ctr["vE"]
            wait("gpsimd", "vE", last_ytok_vE)
            wait("gpsimd", "dG", scatter_done[b])
            for tt in range(4):
                g = idma_scatter_add(acc_flat, idxT[:, 32 * tt + r:32 * tt + r + 1],
                                     y_tok[:, 1024 * tt:1024 * tt + 1024])
            scatter_done[b] = g
            last_pair_scatter = g

        all_scatter_dG = ctr["dG"]

        if max_phase <= 3:
            wait("gpsimd", "dG", ctr["dG"])
            wait("gpsimd", "dA", ctr["dA"])
            return _finish(nc, prog, block_ctx=None)

        dslot_free = [0, 0]
        store_slot = [0, 0]
        wait("sync", "dG", all_scatter_dG)
        for b in range(b_pc):
            for tl in range(16):
                slot = tl % 2
                row0 = b * S + 128 * tl
                wait("sync", "vE", dslot_free[slot])
                d_l = dma("sync", tokbuf[:, 4096 * slot:4096 * slot + 1024],
                          acc_flat[row0:row0 + 128, :], "dS")
                ld = tokbuf[:, 4096 * slot:4096 * slot + 1024]
                ot = tokbuf[:, 4096 * slot + 1024:4096 * slot + 2048]
                wait("vector", "dS", d_l)
                emit_ln(ld, ot, store_wait=store_slot[slot])
                v_o = ctr["vE"]
                dslot_free[slot] = v_o
                wait("scalar", "vE", v_o)
                store_slot[slot] = dma("scalar", out_d[b, 128 * tl:128 * tl + 128, :], ot, "dA")
        final_dA = ctr["dA"]
        wait("gpsimd", "dA", final_dA)
        if debug:
            wait("gpsimd", "dG", ctr["dG"])

        return _finish(nc, prog, block_ctx=None)


def _finish(nc, prog, block_ctx):
    with nc.Block() as block:
        @block.sync
        def _(q):
            for f in prog["sync"]:
                f(q)

        @block.scalar
        def _(q):
            for f in prog["scalar"]:
                f(q)

        @block.vector
        def _(q):
            for f in prog["vector"]:
                f(q)

        @block.tensor
        def _(q):
            for f in prog["tensor"]:
                f(q)

        @block.gpsimd
        def _(q):
            for f in prog["gpsimd"]:
                f(q)
    return nc


def make_consts(bq, bk, bv, bo, gw):
    c = np.zeros((128, 243), dtype=np.float32)
    c[:, 0:128] = np.eye(128, dtype=np.float32)
    c[:, 128] = 1.0
    c[:, 129:137] = 1.0
    c[::32, 137:145] = 1.0
    r = np.arange(32)
    c[0:32, 145] = (r // 8) * float(S)
    for i, b in enumerate((bq, bk, bv, bo)):
        c[:, 147 + 8 * i:155 + 8 * i] = np.asarray(b, np.float32).reshape(8, 128).T
    c[:, 179:243] = np.asarray(gw, np.float32).reshape(8, 128, 8).transpose(1, 0, 2).reshape(128, 64)
    return c


def prep_inputs(L0_attn_x, L0_attn_Wq, L0_attn_bq, L0_attn_Wk, L0_attn_bk,
                L0_attn_Wv, L0_attn_bv, L0_attn_Wo, L0_attn_bo,
                L0_ffn_gate_w, L0_ffn_w1, L0_ffn_w2, b_pc=4, n_cores=8):
    x = np.ascontiguousarray(np.asarray(L0_attn_x, dtype=np.float32))
    w1 = np.asarray(L0_ffn_w1, dtype=np.float32)
    w2 = np.asarray(L0_ffn_w2, dtype=np.float32)
    w1r = np.ascontiguousarray(
        w1.reshape(NE, 8, 128, 32, 128).transpose(0, 3, 2, 1, 4))
    w2f = np.ascontiguousarray(w2)
    consts = make_consts(L0_attn_bq, L0_attn_bk, L0_attn_bv, L0_attn_bo, L0_ffn_gate_w)
    common = dict(
        Wq=np.ascontiguousarray(L0_attn_Wq, dtype=np.float32),
        Wk=np.ascontiguousarray(L0_attn_Wk, dtype=np.float32),
        Wv=np.ascontiguousarray(L0_attn_Wv, dtype=np.float32),
        Wo=np.ascontiguousarray(L0_attn_Wo, dtype=np.float32),
        bq=np.ascontiguousarray(L0_attn_bq, dtype=np.float32),
        bk=np.ascontiguousarray(L0_attn_bk, dtype=np.float32),
        bv=np.ascontiguousarray(L0_attn_bv, dtype=np.float32),
        bo=np.ascontiguousarray(L0_attn_bo, dtype=np.float32),
        gw=np.ascontiguousarray(L0_ffn_gate_w, dtype=np.float32),
        w1r=w1r, w2f=w2f, consts=consts,
    )
    in_maps = []
    for c in range(n_cores):
        m = dict(common)
        m["x"] = np.ascontiguousarray(x[c * b_pc:(c + 1) * b_pc])
        in_maps.append(m)
    return in_maps


def kernel(**inputs):
    b_pc = 4
    n_cores = 8
    in_maps = prep_inputs(**inputs, b_pc=b_pc, n_cores=n_cores)
    nc = build_program(b_pc=b_pc, debug=False)
    res = run_bass_kernel_spmd(nc, in_maps, core_ids=list(range(n_cores)))
    out = np.concatenate([r["out"] for r in res.results], axis=0)
    return out


def check_deadlock(nc):
    trace = nc._trace
    pos = {e: 0 for e in trace}
    sems = {n: 0 for n in ("dS", "dA", "dG", "tE", "vE", "sE")}
    progress = True
    while progress:
        progress = False
        for e in trace:
            while pos[e] < len(trace[e]):
                kind, sem, val = trace[e][pos[e]]
                if kind == "inc":
                    sems[sem] += val
                    pos[e] += 1
                    progress = True
                else:
                    if sems[sem] >= val:
                        pos[e] += 1
                        progress = True
                    else:
                        break
    stuck = {e: (pos[e], len(trace[e]), trace[e][pos[e]] if pos[e] < len(trace[e]) else None)
             for e in trace if pos[e] < len(trace[e])}
    if stuck:
        print("DEADLOCK:", stuck, "sems:", sems)
        return False
    print("no deadlock;", {e: len(trace[e]) for e in trace})
    return True


# version 14
# speedup vs baseline: 1.0068x; 1.0068x over previous
import numpy as np
from contextlib import ExitStack

import concourse.bass as bass
import concourse.mybir as mybir
from concourse.bass import IndirectOffsetOnAxis
from concourse.bass_utils import run_bass_kernel_spmd

F32 = mybir.dt.float32
F32R = mybir.dt.float32r
BF16 = mybir.dt.bfloat16
U16 = mybir.dt.uint16
U32 = mybir.dt.uint32
AF = mybir.ActivationFunctionType
AX = mybir.AxisListType
ALU = mybir.AluOpType

E = 1024
EC = E // 128
S = 2048
CH = 256
NCH = S // CH
NE = 8
CAP = 512
H = 4096
NCORE = 8

ENGINES = ("sync", "scalar", "vector", "tensor", "gpsimd")
A0_STEPS = {"load", "trans", "mm", "evac"}


EPS = 1e-5


def build_program(b_pc=4, debug=False, max_phase=4, sub_phase=9):
    nc = bass.Bass("TRN2", target_bir_lowering=False)

    x_in = nc.dram_tensor("x", [b_pc, S, E], F32, kind="ExternalInput").ap()
    Wq = nc.dram_tensor("Wq", [E, E], F32, kind="ExternalInput").ap()
    Wk = nc.dram_tensor("Wk", [E, E], F32, kind="ExternalInput").ap()
    Wv = nc.dram_tensor("Wv", [E, E], F32, kind="ExternalInput").ap()
    Wo = nc.dram_tensor("Wo", [E, E], F32, kind="ExternalInput").ap()
    bq = nc.dram_tensor("bq", [E], F32, kind="ExternalInput").ap()
    bk = nc.dram_tensor("bk", [E], F32, kind="ExternalInput").ap()
    bv = nc.dram_tensor("bv", [E], F32, kind="ExternalInput").ap()
    bo = nc.dram_tensor("bo", [E], F32, kind="ExternalInput").ap()
    gw = nc.dram_tensor("gw", [E, NE], F32, kind="ExternalInput").ap()
    w1r = nc.dram_tensor("w1r", [NE, 32, 128, 8, 128], F32, kind="ExternalInput").ap()
    w2f = nc.dram_tensor("w2f", [NE, H, E], F32, kind="ExternalInput").ap()
    consts = nc.dram_tensor("consts", [128, 243], F32, kind="ExternalInput").ap()

    out_d = nc.dram_tensor("out", [b_pc, S, E], F32, kind="ExternalOutput").ap()
    dbg_kind = "ExternalOutput" if debug else "Internal"
    xt_d = nc.dram_tensor("xt_d", [b_pc * S, E], F32, kind=dbg_kind).ap()
    acc_d = nc.dram_tensor("acc_d", [b_pc * S, E], F32, kind=dbg_kind).ap()
    raw_d = nc.dram_tensor("raw_d", [b_pc, NE, S], F32, kind="Internal").ap()
    mean_d = nc.dram_tensor("mean_d", [b_pc, NE, S], F32, kind="Internal").ap()
    sdev_d = nc.dram_tensor("sdev_d", [b_pc, S], F32, kind="Internal").ap()
    if debug:
        probs_d = nc.dram_tensor("probs_d", [32, S], F32, kind="ExternalOutput").ap()
        idxt_d = nc.dram_tensor("idxt_d", [128, 128], U32, kind="ExternalOutput").ap()
        wpt_d = nc.dram_tensor("wpt_d", [128, 128], F32, kind="ExternalOutput").ap()

    ctx = ExitStack()
    sb = lambda name, shape, dt=F32: ctx.enter_context(nc.sbuf_tensor(name, shape, dt))
    psum = lambda name, shape: ctx.enter_context(nc.psum_tensor(name, shape, F32))
    mksem = lambda name: ctx.enter_context(nc.semaphore(name))

    with ctx:
        consts_sb = sb("consts_sb", [128, 243])
        bv9_sb = sb("bv9_sb", [128, 8])
        gsumneg = sb("gsumneg", [8, 1])
        eps_t = sb("eps_t", [128, 1])
        half3_t = sb("half3_t", [128, 1])
        kt_ht = sb("kt_ht", [128, 16384])
        wbuf = sb("wbuf", [128, 8192])
        tokbuf = sb("tokbuf", [128, 8192])
        actbuf = sb("actbuf", [128, 16896])
        gstage = sb("gstage", [8, 1024])
        sstage = sb("sstage", [1, 512])
        rstd_buf = sb("rstd_buf", [128, 16])
        rstdT_sb = sb("rstdT_sb", [16, 128])
        idxT = sb("idxT", [128, 128], U32)
        wpT = sb("wpT", [128, 128])
        idx16_t = sb("idx16_t", [32, 512], U16)
        stat = sb("stat", [128, 16])
        gel_t = sb("gel_t", [128, 1024])

        pb = [psum(f"pb{i}", [128, 512]) for i in range(8)]

        sems = {n: mksem(n) for n in ("dS", "dA", "dG", "tE", "vE", "sE")}
        ctr = {n: 0 for n in sems}
        prog = {e: [] for e in ENGINES}
        trace = {e: [] for e in ENGINES}
        nc._trace = trace

        wrappable = {e: False for e in ENGINES}

        def emit(e, fn, op=True):
            prog[e].append(fn)
            wrappable[e] = op

        def wait(e, sem, val):
            if val <= 0:
                return
            trace[e].append(("wait", sem, val))
            emit(e, lambda q, s=sems[sem], v=val: q.wait_ge(s, v), op=False)

        def drain(e):
            emit(e, lambda q: q.drain())

        def inc(e, sem):
            ctr[sem] += 1
            trace[e].append(("inc", sem, 1))
            assert wrappable[e], f"inc({e},{sem}) with no preceding instruction"
            prev = prog[e][-1]
            prog[e][-1] = (lambda q, p=prev, s=sems[sem]: p(q).then_inc(s, 1))
            return ctr[sem]

        def dma(e, out, in_, sem, slow=False):
            ctr[sem] += 16
            trace[e].append(("inc", sem, 16))
            emit(e, lambda q, o=out, i=in_, s=sems[sem], sl=slow:
                 q.dma_start(out=o, in_=i, allow_slow_non_contiguous=sl).then_inc(s, 16))
            return ctr[sem]

        def idma_gather(out, in_, idx_ap):
            ctr["dG"] += 16
            trace["gpsimd"].append(("inc", "dG", 16))
            emit("gpsimd", lambda q, o=out, i=in_, x=idx_ap, s=sems["dG"]:
                 q.indirect_dma_start(out=o, out_offset=None, in_=i,
                                      in_offset=IndirectOffsetOnAxis(x, 0)).then_inc(s, 16))
            return ctr["dG"]

        def idma_scatter_add(out, idx_ap, in_):
            ctr["dG"] += 16
            trace["gpsimd"].append(("inc", "dG", 16))
            emit("gpsimd", lambda q, o=out, i=in_, x=idx_ap, s=sems["dG"]:
                 q.indirect_dma_start(out=o, out_offset=IndirectOffsetOnAxis(x, 0),
                                      in_=i, in_offset=None,
                                      compute_op=ALU.add).then_inc(s, 16))
            return ctr["dG"]

        def op_matmul(o, l, r, st, sp):
            emit("tensor", lambda q, o=o, l=l, r=r, st=st, sp=sp:
                 q.matmul(o, l, r, start=st, stop=sp))

        def op_transpose(o, i, ident):
            emit("tensor", lambda q, o=o, i=i, d=ident: q.transpose(o, i, d))

        def op_copy(e, o, i):
            emit(e, lambda q, o=o, i=i: (q.tensor_copy(o, i) if e == "vector" else q.copy(o, i)))

        def op_act(o, i, func, bias=None, scale=None, accum=None):
            kw = {}
            if bias is not None:
                kw["bias"] = bias
            if scale is not None:
                kw["scale"] = scale
            if accum is not None:
                kw["accum_out"] = accum
            emit("scalar", lambda q, o=o, i=i, f=func, kw=kw: q.activation(o, i, f, **kw))

        def op_ts(o, i, s1, op0, s2=None, op1=None, accum=None):
            kw = {}
            if op1 is not None:
                kw["op1"] = op1
            if accum is not None:
                kw["accum_out"] = accum
            emit("vector", lambda q, o=o, i=i, s1=s1, s2=s2, op0=op0, kw=kw:
                 q.tensor_scalar(o, i, s1, s2, op0=op0, **kw))

        def op_stt(o, i0, s, i1, op0, op1):
            emit("vector", lambda q, o=o, i0=i0, s=s, i1=i1, a=op0, b=op1:
                 q.scalar_tensor_tensor(o, i0, s, i1, a, b))

        def op_tt(o, a, b, op):
            emit("vector", lambda q, o=o, a=a, b=b, op=op: q.tensor_tensor(o, a, b, op=op))

        def op_red(o, i, negate=False, rsum=False):
            emit("vector", lambda q, o=o, i=i, n=negate, rs=rsum:
                 (q.reduce_sum(o, i, axis=AX.X) if rs else q.reduce_max(o, i, axis=AX.X, negate=n)))

        bank_wait = [None] * 8

        def use_banks(banks):
            for i in banks:
                if bank_wait[i]:
                    wait("tensor", *bank_wait[i])
                    bank_wait[i] = None

        def emit_ln(rt, xt_a, store_wait=0):
            op_red(stat[:, 10:11], rt, rsum=True)
            drain("vector")
            op_ts(stat[:, 11:12], stat[:, 10:11], 1.0 / E, ALU.mult)
            drain("vector")
            if store_wait:
                wait("vector", "dA", store_wait)
            op_ts(xt_a, rt, stat[:, 11:12], ALU.subtract)
            v_c = inc("vector", "vE")
            wait("scalar", "vE", v_c)
            op_act(sq_scr[:], xt_a, AF.Square, accum=stat[:, 12:13])
            drain("scalar")
            op_act(stat[:, 14:15], stat[:, 12:13], AF.Ln, bias=eps_t[:], scale=1.0 / E)
            drain("scalar")
            op_act(stat[:, 13:14], stat[:, 14:15], AF.Exp, scale=-0.5)
            drain("scalar")
            s_r0 = inc("scalar", "sE")
            wait("vector", "sE", s_r0)
            op_ts(stat[:, 12:13], stat[:, 12:13], 1.0 / E, ALU.mult, s2=EPS, op1=ALU.add)
            drain("vector")
            op_tt(stat[:, 14:15], stat[:, 12:13], stat[:, 13:14], ALU.mult)
            drain("vector")
            op_tt(stat[:, 14:15], stat[:, 14:15], stat[:, 13:14], ALU.mult)
            drain("vector")
            v_nt = inc("vector", "vE")
            wait("scalar", "vE", v_nt)
            op_act(stat[:, 14:15], stat[:, 14:15], AF.Identity, bias=half3_t[:], scale=-0.5)
            drain("scalar")
            s_nt = inc("scalar", "sE")
            wait("vector", "sE", s_nt)
            op_tt(stat[:, 13:14], stat[:, 13:14], stat[:, 14:15], ALU.mult)
            drain("vector")
            op_ts(xt_a, xt_a, stat[:, 13:14], ALU.mult)
            inc("vector", "vE")

        ident = consts_sb[:, 0:128]
        ones_col = consts_sb[:, 128:129]
        ones_k8 = consts_sb[0:8, 129:137]
        bcrow8 = consts_sb[0:8, 137:145]
        boff32 = consts_sb[0:32, 145:146]

        xt_flat = xt_d
        acc_flat = acc_d

        v_setup = dma("sync", consts_sb[:], consts, "dS")
        bq_sb = consts_sb[:, 147:155]
        bk_sb = consts_sb[:, 155:163]
        bv_sb = consts_sb[:, 163:171]
        bo_sb = consts_sb[:, 171:179]
        gw_sb = consts_sb[:, 179:243]
        wait("vector", "dS", v_setup)
        op_ts(bv9_sb[:], bv_sb[:], 9.0, ALU.mult)
        emit("vector", lambda q: q.memset(eps_t[:], 1e-5))
        emit("vector", lambda q: q.memset(half3_t[:], 1.5))
        setup_v = inc("vector", "vE")
        wait("tensor", "dS", v_setup)
        for c in range(EC):
            op_matmul(pb[0][0:8, 0:8], gw_sb[:, 8 * c:8 * c + 8],
                      consts_sb[:, 129:137], c == 0, c == EC - 1)
        t_gsum = inc("tensor", "tE")
        wait("scalar", "tE", t_gsum)
        op_act(gsumneg[:], pb[0][0:8, 0:1], AF.Copy, scale=-1.0)
        s_gsum = inc("scalar", "sE")
        bank_wait[0] = ("sE", s_gsum)

        xTn = actbuf[:, 0:2048]
        QTn = actbuf[:, 2048:4096]
        Vn = actbuf[:, 4096:6144]
        combT = actbuf[:, 6144:8192]
        resT = actbuf[:, 8192:10240]
        res_tok = actbuf[:, 10240:12288]
        xt_tok = actbuf[:, 12288:14336]
        sq_scr = actbuf[:, 14336:15360]
        exp_sb = actbuf[:, 15360:15872]
        Ptil = actbuf[:, 15872:16384]
        PtilT = actbuf[:, 16384:16896]
        xT_tb = actbuf[:, 0:4096]

        tok_slot_free = [0, 0]
        wslot_free = [0, 0]
        last_scores_tE = 0
        last_AV_tE = 0
        last_O_tE = 0
        last_resTrans_tE = 0
        last_sq_sE = 0
        last_stt_vE = 0
        gstage_free = [0, 0]
        sstage_free = [0, 0]
        last_xt_dA = 0

        def load_weight_block_A(w_ap, k, slot):
            wait("sync", "tE", wslot_free[slot])
            return dma("sync", wbuf[:, 1024 * slot:1024 * slot + 1024],
                       w_ap[128 * k:128 * k + 128, :], "dS")

        for b in range(b_pc if sub_phase >= 0 else 0):
            for tb in range(4 if sub_phase >= 0 else 0):
                slot = tb % 2
                wait("sync", "tE", tok_slot_free[slot])
                d_x = dma("sync", tokbuf[:, 4096 * slot:4096 * slot + 4096]
                          .rearrange("p (a c) -> p a c", a=4),
                          x_in[b, 512 * tb:512 * tb + 512, :]
                          .rearrange("(a p) c -> p a c", p=128), "dS")
                if "trans" in A0_STEPS:
                    wait("tensor", "dS", d_x)
                    if tb == 0 and b == 0:
                        wait("tensor", "vE", setup_v)
                    for a in range(4):
                        for c in range(EC):
                            bk_i = (a * EC + c) % 4
                            use_banks([bk_i])
                            op_transpose(pb[bk_i][:, 0:128],
                                         tokbuf[:, 4096 * slot + 1024 * a + 128 * c:
                                                4096 * slot + 1024 * a + 128 * c + 128],
                                         ident)
                            t_tr = inc("tensor", "tE")
                            wait("vector", "tE", t_tr)
                            op_copy("vector", xT_tb[:, 512 * c + 128 * a:512 * c + 128 * a + 128],
                                    pb[bk_i][:, 0:128])
                            v_e = inc("vector", "vE")
                            bank_wait[bk_i] = ("vE", v_e)
                    tok_slot_free[slot] = ctr["tE"]
                if "mm" in A0_STEPS:
                    use_banks(range(8))
                    wait("tensor", "vE", ctr["vE"])
                    for k in range(EC):
                        slotw = k % 2
                        d_w = load_weight_block_A(Wk, k, slotw)
                        wait("tensor", "dS", d_w)
                        for ep in range(8):
                            op_matmul(pb[ep][:, 0:512],
                                      wbuf[:, 1024 * slotw + 128 * ep:1024 * slotw + 128 * ep + 128],
                                      xT_tb[:, 512 * k:512 * k + 512],
                                      k == 0, k == EC - 1)
                        wslot_free[slotw] = inc("tensor", "tE")
                if "evac" in A0_STEPS:
                    t_kt = ctr["tE"]
                    wait("scalar", "tE", t_kt)
                    if b > 0:
                        wait("scalar", "tE", last_scores_tE)
                    for ep in range(8):
                        op_act(kt_ht[:, 2048 * ep + 512 * tb:2048 * ep + 512 * tb + 512],
                               pb[ep][:, 0:512], AF.Identity, bias=bk_sb[:, ep:ep + 1])
                    s_kt = inc("scalar", "sE")
                    for ep in range(8):
                        bank_wait[ep] = ("sE", s_kt)
            kt_ready_sE = ctr["sE"]

            for n in range(NCH if sub_phase >= 1 else 0):
                slot = n % 2
                wait("sync", "tE", tok_slot_free[slot])
                d_x = dma("sync", tokbuf[:, 4096 * slot:4096 * slot + 2048]
                          .rearrange("p (a c) -> p a c", a=2),
                          x_in[b, 256 * n:256 * n + 256, :]
                          .rearrange("(a p) c -> p a c", p=128), "dS")
                wait("tensor", "dS", d_x)
                for a in range(2):
                    for c in range(EC):
                        bk_i = (a * EC + c) % 4
                        use_banks([bk_i])
                        op_transpose(pb[bk_i][:, 0:128],
                                     tokbuf[:, 4096 * slot + 1024 * a + 128 * c:
                                            4096 * slot + 1024 * a + 128 * c + 128],
                                     ident)
                        t_tr = inc("tensor", "tE")
                        wait("vector", "tE", t_tr)
                        op_copy("vector", xTn[:, 256 * c + 128 * a:256 * c + 128 * a + 128],
                                pb[bk_i][:, 0:128])
                        v_e = inc("vector", "vE")
                        bank_wait[bk_i] = ("vE", v_e)
                tok_slot_free[slot] = ctr["tE"]
                xTn_v = ctr["vE"]

                use_banks(range(8))
                wait("tensor", "vE", xTn_v)
                for k in range(EC):
                    slotw = k % 2
                    d_w = load_weight_block_A(Wq, k, slotw)
                    wait("tensor", "dS", d_w)
                    for ep in range(8):
                        op_matmul(pb[ep][:, 0:256],
                                  wbuf[:, 1024 * slotw + 128 * ep:1024 * slotw + 128 * ep + 128],
                                  xTn[:, 256 * k:256 * k + 256],
                                  k == 0, k == EC - 1)
                    wslot_free[slotw] = inc("tensor", "tE")
                wait("scalar", "tE", ctr["tE"])
                wait("scalar", "tE", last_scores_tE)
                for ep in range(8):
                    op_act(QTn[:, 256 * ep:256 * ep + 256], pb[ep][:, 0:256],
                           AF.Identity, bias=bq_sb[:, ep:ep + 1])
                s_q = inc("scalar", "sE")
                for ep in range(8):
                    bank_wait[ep] = ("sE", s_q)

                use_banks(range(4))
                for k in range(EC):
                    slotw = k % 2
                    d_w = load_weight_block_A(Wv, k, slotw)
                    wait("tensor", "dS", d_w)
                    for ts in range(2):
                        for eb in range(2):
                            op_matmul(pb[2 * ts + eb][:, 0:512],
                                      xTn[:, 256 * k + 128 * ts:256 * k + 128 * ts + 128],
                                      wbuf[:, 1024 * slotw + 512 * eb:1024 * slotw + 512 * eb + 512],
                                      k == 0, k == EC - 1)
                    wslot_free[slotw] = inc("tensor", "tE")
                wait("vector", "tE", ctr["tE"])
                for ts in range(2):
                    for eb in range(2):
                        op_copy("vector", Vn[:, 1024 * ts + 512 * eb:1024 * ts + 512 * eb + 512],
                                pb[2 * ts + eb][:, 0:512])
                v_v = inc("vector", "vE")
                for i in range(4):
                    bank_wait[i] = ("vE", v_v)

                if b == 0 and n == 0:
                    wait("tensor", "sE", kt_ready_sE)
                first_in_qs = [True, True]
                sc_grp_tE = {}
                s2_done = {}
                s4_done = {}
                s5_done = {}

                def sc_cols(m):
                    return 5 * (m % 2)

                def blk_ap(m):
                    qs, kb, mb = m // 8, (m // 2) % 4, m % 2
                    bk_i = 4 + 2 * qs + (kb % 2)
                    return pb[bk_i][:, 256 * mb:256 * mb + 256], bk_i

                for m in range(16 + 2):
                    if 0 <= m - 1 < 16:
                        mm_ = m - 1
                        cb = sc_cols(mm_)
                        blk, bk_i = blk_ap(mm_)
                        wait("vector", "sE", s2_done[mm_])
                        op_tt(stat[:, cb + 4:cb + 5], stat[:, cb + 1:cb + 2],
                              stat[:, cb + 3:cb + 4], ALU.subtract)
                        drain("vector")
                        if 2 * ((mm_ // 2) % 4) + (mm_ % 2) == n:
                            op_ts(stat[:, cb + 4:cb + 5], stat[:, cb + 4:cb + 5],
                                  float(np.log(2.0)), ALU.add)
                            drain("vector")
                        v_b2 = inc("vector", "vE")
                        wait("scalar", "vE", v_b2)
                        op_act(exp_sb[:, 256 * (mm_ % 2):256 * (mm_ % 2) + 256], blk, AF.Exp,
                               bias=stat[:, cb + 4:cb + 5], scale=1.0 / 32.0)
                        drain("scalar")
                        s4_done[mm_] = inc("scalar", "sE")
                        bank_wait[bk_i] = ("sE", s4_done[mm_])
                    if 0 <= m - 2:
                        mm_ = m - 2
                        qs = mm_ // 8
                        wait("vector", "sE", s4_done[mm_])
                        if first_in_qs[qs]:
                            op_copy("vector", Ptil[:, 256 * qs:256 * qs + 256],
                                    exp_sb[:, 256 * (mm_ % 2):256 * (mm_ % 2) + 256])
                            first_in_qs[qs] = False
                        else:
                            op_tt(Ptil[:, 256 * qs:256 * qs + 256],
                                  Ptil[:, 256 * qs:256 * qs + 256],
                                  exp_sb[:, 256 * (mm_ % 2):256 * (mm_ % 2) + 256], ALU.add)
                        drain("vector")
                        s5_done[mm_] = inc("vector", "vE")
                    if m < 16 and m % 2 == 0:
                        qs, kb = m // 8, (m // 2) % 4
                        bk_i = 4 + 2 * qs + (kb % 2)
                        use_banks([bk_i])
                        if kb == 0:
                            wait("tensor", "sE", s_q)
                        for c in range(EC):
                            op_matmul(pb[bk_i][:, 0:512],
                                      QTn[:, 256 * c + 128 * qs:256 * c + 128 * qs + 128],
                                      kt_ht[:, 2048 * c + 512 * kb:2048 * c + 512 * kb + 512],
                                      c == 0, c == EC - 1)
                        sc_grp_tE[m] = sc_grp_tE[m + 1] = inc("tensor", "tE")
                    if m < 16:
                        cb = sc_cols(m)
                        blk, bk_i = blk_ap(m)
                        wait("vector", "tE", sc_grp_tE[m])
                        if m % 2 == 1:
                            wait("vector", "sE", s4_done[m - 1])
                        op_red(stat[:, cb:cb + 1], blk, negate=True)
                        drain("vector")
                        op_ts(stat[:, cb + 1:cb + 2], stat[:, cb:cb + 1], 1.0 / 32.0, ALU.mult)
                        drain("vector")
                        v_m = inc("vector", "vE")
                        wait("scalar", "vE", v_m)
                        if m >= 2:
                            wait("scalar", "vE", s5_done[m - 2])
                        op_act(exp_sb[:, 256 * (m % 2):256 * (m % 2) + 256], blk, AF.Exp,
                               bias=stat[:, cb + 1:cb + 2], scale=1.0 / 32.0,
                               accum=stat[:, cb + 2:cb + 3])
                        drain("scalar")
                        op_act(stat[:, cb + 3:cb + 4], stat[:, cb + 2:cb + 3], AF.Ln)
                        drain("scalar")
                        s2_done[m] = inc("scalar", "sE")
                ptil_v = ctr["vE"]

                if sub_phase < 3:
                    continue
                wait("tensor", "vE", ptil_v)
                for qs in range(2):
                    for ks in range(2):
                        bk_i = (2 * qs + ks) % 4
                        use_banks([bk_i])
                        op_transpose(pb[bk_i][:, 0:128],
                                     Ptil[:, 256 * qs + 128 * ks:256 * qs + 128 * ks + 128],
                                     ident)
                        t_tr = inc("tensor", "tE")
                        wait("vector", "tE", t_tr)
                        op_copy("vector", PtilT[:, 256 * ks + 128 * qs:256 * ks + 128 * qs + 128],
                                pb[bk_i][:, 0:128])
                        v_e = inc("vector", "vE")
                        bank_wait[bk_i] = ("vE", v_e)
                ptilT_v = ctr["vE"]

                wait("tensor", "vE", ptilT_v)
                for ep in range(8):
                    bk_i = ep % 4
                    use_banks([bk_i])
                    for ks in range(2):
                        op_matmul(pb[bk_i][:, 0:256],
                                  Vn[:, 1024 * ks + 128 * ep:1024 * ks + 128 * ep + 128],
                                  PtilT[:, 256 * ks:256 * ks + 256],
                                  ks == 0, ks == 1)
                    last_AV_tE = inc("tensor", "tE")
                    wait("scalar", "tE", last_AV_tE)
                    if ep == 0:
                        wait("scalar", "tE", last_O_tE)
                    op_act(combT[:, 256 * ep:256 * ep + 256], pb[bk_i][:, 0:256],
                           AF.Identity, bias=bv9_sb[:, ep:ep + 1])
                    s_av = inc("scalar", "sE")
                    bank_wait[bk_i] = ("sE", s_av)
                comb_s = ctr["sE"]

                if sub_phase < 4:
                    continue
                use_banks(range(8))
                wait("tensor", "sE", comb_s)
                for c2 in range(EC):
                    slotw = c2 % 2
                    d_w = load_weight_block_A(Wo, c2, slotw)
                    wait("tensor", "dS", d_w)
                    for ep in range(8):
                        op_matmul(pb[ep][:, 0:256],
                                  wbuf[:, 1024 * slotw + 128 * ep:1024 * slotw + 128 * ep + 128],
                                  combT[:, 256 * c2:256 * c2 + 256],
                                  c2 == 0, c2 == EC - 1)
                    wslot_free[slotw] = inc("tensor", "tE")
                last_O_tE = ctr["tE"]
                wait("vector", "tE", last_O_tE)
                wait("vector", "tE", last_resTrans_tE)
                for ep in range(8):
                    op_stt(resT[:, 256 * ep:256 * ep + 256], pb[ep][:, 0:256],
                           bo_sb[:, ep:ep + 1], xTn[:, 256 * ep:256 * ep + 256],
                           ALU.add, ALU.add)
                v_res = inc("vector", "vE")
                for ep in range(8):
                    bank_wait[ep] = ("vE", v_res)

                if sub_phase < 5:
                    continue
                gslot = n % 2
                use_banks([0, 1])
                wait("tensor", "vE", v_res)
                for c in range(EC):
                    op_matmul(pb[0][0:8, 0:256], gw_sb[:, 8 * c:8 * c + 8],
                              resT[:, 256 * c:256 * c + 256], c == 0, c == EC - 1)
                for c in range(EC):
                    op_matmul(pb[1][0:8, 0:256], consts_sb[:, 129:137],
                              resT[:, 256 * c:256 * c + 256], c == 0, c == EC - 1)
                t_g = inc("tensor", "tE")
                wait("vector", "tE", t_g)
                wait("vector", "dA", gstage_free[gslot])
                op_copy("vector", gstage[0:8, 512 * gslot:512 * gslot + 256], pb[0][0:8, 0:256])
                v_g = inc("vector", "vE")
                wait("scalar", "tE", t_g)
                wait("scalar", "dA", gstage_free[gslot])
                op_act(gstage[0:8, 512 * gslot + 256:512 * gslot + 512], pb[1][0:8, 0:256],
                       AF.Copy, scale=1.0 / E)
                s_g = inc("scalar", "sE")
                bank_wait[0] = ("vE", v_g)
                bank_wait[1] = ("sE", s_g)
                wait("scalar", "vE", v_g)
                d1 = dma("scalar", raw_d[b, :, 256 * n:256 * n + 256],
                         gstage[0:8, 512 * gslot:512 * gslot + 256], "dA")
                d2 = dma("scalar", mean_d[b, :, 256 * n:256 * n + 256],
                         gstage[0:8, 512 * gslot + 256:512 * gslot + 512], "dA")
                gstage_free[gslot] = d2

                if sub_phase < 6:
                    continue
                for c2 in range(EC):
                    for a in range(2):
                        bk_i = 2 + (c2 * 2 + a) % 2
                        use_banks([bk_i])
                        op_transpose(pb[bk_i][:, 0:128],
                                     resT[:, 256 * c2 + 128 * a:256 * c2 + 128 * a + 128],
                                     ident)
                        t_tr = inc("tensor", "tE")
                        wait("vector", "tE", t_tr)
                        if c2 == 0 and a == 0:
                            wait("vector", "sE", last_sq_sE)
                        op_copy("vector", res_tok[:, 1024 * a + 128 * c2:1024 * a + 128 * c2 + 128],
                                pb[bk_i][:, 0:128])
                        v_e = inc("vector", "vE")
                        bank_wait[bk_i] = ("vE", v_e)
                last_resTrans_tE = ctr["tE"]
                v_rtok = ctr["vE"]

                sslot = n % 2
                wait("vector", "vE", v_rtok)
                for a in range(2):
                    emit_ln(res_tok[:, 1024 * a:1024 * a + 1024],
                            xt_tok[:, 1024 * a:1024 * a + 1024],
                            store_wait=(last_xt_dA if a == 0 else 0))
                    v_xt = ctr["vE"]
                    row0 = b * S + 256 * n + 128 * a
                    wait("scalar", "vE", v_xt)
                    dma("scalar", xt_flat[row0:row0 + 128, :],
                        xt_tok[:, 1024 * a:1024 * a + 1024], "dA")
                    last_xt_dA = dma("scalar", acc_flat[row0:row0 + 128, :],
                                     xt_tok[:, 1024 * a:1024 * a + 1024], "dA")
                    drain("vector")
                    op_copy("vector", rstd_buf[:, 2 * n + a:2 * n + a + 1], stat[:, 13:14])
                    inc("vector", "vE")
            use_banks([2])
            wait("tensor", "vE", ctr["vE"])
            op_transpose(pb[2][0:16, 0:128], rstd_buf[:], ident)
            t_sd = inc("tensor", "tE")
            wait("vector", "tE", t_sd)
            op_copy("vector", rstdT_sb[:], pb[2][0:16, 0:128])
            v_sd = inc("vector", "vE")
            bank_wait[2] = ("vE", v_sd)
            wait("scalar", "vE", v_sd)
            dma("scalar", sdev_d[b].rearrange("(j t) -> j t", j=16), rstdT_sb[:], "dA")
            last_scores_tE = ctr["tE"]

        if max_phase <= 1:
            wait("gpsimd", "dA", ctr["dA"])
            wait("gpsimd", "dS", ctr["dS"])
            return _finish(nc, prog, block_ctx=None)

        a_done_dA = ctr["dA"]
        a_done_vE = ctr["vE"]
        a_end_tE = ctr["tE"]
        sel = actbuf[0:32, 0:2048]
        work = [actbuf[0:32, 2048:4096], actbuf[0:32, 4096:6144]]
        braw = actbuf[0:8, 6144:6656]
        bmean = actbuf[0:8, 6656:7168]
        bsdev = actbuf[0:8, 7168:7680]
        btmp = actbuf[0:8, 7680:8192]
        blnz = actbuf[0:8, 8192:8704]
        bprob = actbuf[0:8, 8704:9216]
        val32 = actbuf[0:32, 9216:9728]
        idxf = actbuf[0:32, 9728:10240]
        idx16 = idx16_t[0:32, :]

        wait("sync", "dA", a_done_dA)
        wait("sync", "tE", a_end_tE)
        wait("tensor", "vE", a_done_vE)
        wait("vector", "tE", a_end_tE)
        wait("vector", "sE", s_gsum)
        emit("vector", lambda q: q.memset(bsdev[:], 0.0))
        bsdev_z = inc("vector", "vE")
        wait("sync", "vE", bsdev_z)
        for b in range(b_pc):
            for blk in range(4):
                d1 = dma("sync", braw[:], raw_d[b, :, 512 * blk:512 * blk + 512], "dS")
                d2 = dma("sync", bmean[:], mean_d[b, :, 512 * blk:512 * blk + 512], "dS")
                d3 = dma("sync", bsdev[0:1, :], sdev_d[b, 512 * blk:512 * blk + 512], "dS")
                use_banks([4, 5])
                wait("tensor", "dS", d3)
                op_matmul(pb[4][0:8, 0:512], bcrow8, bsdev[:], True, True)
                t_bc = inc("tensor", "tE")
                wait("vector", "tE", t_bc)
                wait("vector", "dS", d3)
                op_stt(btmp[:], bmean[:], gsumneg[:], braw[:], ALU.mult, ALU.add)
                drain("vector")
                op_tt(btmp[:], btmp[:], pb[4][0:8, 0:512], ALU.mult)
                drain("vector")
                v_sc = inc("vector", "vE")
                bank_wait[4] = ("vE", v_sc)
                wait("scalar", "vE", v_sc)
                op_act(blnz[:], btmp[:], AF.Exp)
                s_ex = inc("scalar", "sE")
                wait("tensor", "sE", s_ex)
                op_matmul(pb[5][0:8, 0:512], ones_k8, blnz[:], True, True)
                t_z = inc("tensor", "tE")
                wait("scalar", "tE", t_z)
                op_act(blnz[:], pb[5][0:8, 0:512], AF.Ln)
                drain("scalar")
                s_ln = inc("scalar", "sE")
                bank_wait[5] = ("sE", s_ln)
                wait("vector", "sE", s_ln)
                op_tt(btmp[:], btmp[:], blnz[:], ALU.subtract)
                drain("vector")
                v_sub = inc("vector", "vE")
                wait("scalar", "vE", v_sub)
                op_act(bprob[:], btmp[:], AF.Exp)
                s_pr = inc("scalar", "sE")
                wait("sync", "sE", s_pr)
                d_sel = dma("sync", sel[8 * b:8 * b + 8, 512 * blk:512 * blk + 512], bprob[:], "dS")
        sel_ready = ctr["dS"]
        if debug:
            wait("gpsimd", "dS", sel_ready)
            dma("gpsimd", probs_d, sel, "dG")

        wait("vector", "dS", sel_ready)
        cur = 0
        emit("vector", lambda q: q.tensor_copy(work[0][:], sel[:]))
        for r in range(64):
            emit("vector", lambda q, r=r, c=cur: q.max(
                val32[:, 8 * r:8 * r + 8], work[c][:]))
            drain("vector")
            emit("vector", lambda q, r=r, c=cur: q.max_index(
                idx16[:, 8 * r:8 * r + 8], val32[:, 8 * r:8 * r + 8], work[c][:]))
            drain("vector")
            emit("vector", lambda q, r=r, c=cur: q.match_replace(
                work[1 - c][:], val32[:, 8 * r:8 * r + 8], work[c][:], 0.0))
            drain("vector")
            cur = 1 - cur
        emit("vector", lambda q: q.tensor_copy(idxf[:], idx16[:]))
        drain("vector")
        op_ts(idxf[:], idxf[:], boff32, ALU.add)
        drain("vector")
        v_self = inc("vector", "vE")
        wait("tensor", "vE", v_self)
        for qb in range(4):
            use_banks([6])
            op_transpose(pb[6][:, 0:32], idxf[:, 128 * qb:128 * qb + 128], ident[0:32, 0:32])
            t_ti = inc("tensor", "tE")
            wait("vector", "tE", t_ti)
            op_copy("vector", idxT[:, 32 * qb:32 * qb + 32], pb[6][:, 0:32])
            v_ti = inc("vector", "vE")
            bank_wait[6] = ("vE", v_ti)
            use_banks([7])
            op_transpose(pb[7][:, 0:32], val32[:, 128 * qb:128 * qb + 128], ident[0:32, 0:32])
            t_tw = inc("tensor", "tE")
            wait("vector", "tE", t_tw)
            op_copy("vector", wpT[:, 32 * qb:32 * qb + 32], pb[7][:, 0:32])
            v_tw = inc("vector", "vE")
            bank_wait[7] = ("vE", v_tw)
        sel_done_vE = ctr["vE"]
        if debug:
            wait("gpsimd", "vE", sel_done_vE)
            dma("gpsimd", idxt_d, idxT[:], "dG")
            dma("gpsimd", wpt_d, wpT[:], "dG")

        if max_phase <= 2:
            wait("gpsimd", "dA", ctr["dA"])
            wait("gpsimd", "dS", ctr["dS"])
            wait("gpsimd", "dG", ctr["dG"])
            wait("gpsimd", "vE", ctr["vE"])
            return _finish(nc, prog, block_ctx=None)

        XgR = actbuf[:, 0:4096]
        XgT_hi = actbuf[:, 4096:6144].bitcast(BF16)
        XgT_lo = actbuf[:, 6144:8192].bitcast(BF16)
        yT = actbuf[:, 8192:12288]
        y_tok = actbuf[:, 12288:16384]
        hT_hi = kt_ht[:, 0:8192].bitcast(BF16)
        hT_lo = kt_ht[:, 8192:16384].bitcast(BF16)
        gel = [gel_t[:, 512 * i:512 * i + 512] for i in range(2)]

        phaseC_t0 = ctr["tE"]
        xg_gather_done = [0, 0]
        xg_slot_free = [0, 0]
        scatter_done = [0] * b_pc
        last_pair_scatter = 0
        last_h_tE = 0
        last_y_tE = 0
        wslotC_free = [a_end_tE] * 4
        gel_free = [0, 0]

        pairs = [(e, b) for e in range(NE) for b in range(b_pc)]
        PASSES = ((0, 0), (0, 1), (1, 0))

        def issue_gathers(p):
            e, b = pairs[p]
            r = 8 * b + e
            slot = p % 2
            wait("gpsimd", "tE", xg_slot_free[slot])
            if p == 0:
                wait("gpsimd", "dA", a_done_dA)
                wait("gpsimd", "vE", sel_done_vE)
            for tt in range(4):
                g = idma_gather(tokbuf[:, 4096 * slot + 1024 * tt:4096 * slot + 1024 * tt + 1024],
                                xt_flat, idxT[:, 32 * tt + r:32 * tt + r + 1])
            xg_gather_done[slot] = g
            return g

        def wslot_views(slotw):
            base = 2048 * slotw
            raw = wbuf[:, base:base + 1024]
            hi = wbuf[:, base + 1024:base + 1536].bitcast(BF16)
            lo = wbuf[:, base + 1536:base + 2048].bitcast(BF16)
            return raw, hi, lo

        def split_wblock(slotw, d_w):
            raw, hi, lo = wslot_views(slotw)
            wait("scalar", "dS", d_w)
            op_act(hi, raw, AF.Copy)
            drain("scalar")
            s_w = inc("scalar", "sE")
            wait("vector", "sE", s_w)
            op_tt(lo, raw, hi, ALU.subtract)
            drain("vector")
            return inc("vector", "vE")

        def emit_xg_trans(p):
            slot = p % 2
            wait("tensor", "dG", xg_gather_done[slot])
            for tt in range(4):
                for c in range(EC):
                    bk_i = (tt * EC + c) % 4
                    use_banks([bk_i])
                    op_transpose(pb[bk_i][:, 0:128],
                                 tokbuf[:, 4096 * slot + 1024 * tt + 128 * c:
                                        4096 * slot + 1024 * tt + 128 * c + 128],
                                 ident)
                    t_tr = inc("tensor", "tE")
                    wait("vector", "tE", t_tr)
                    if tt == 0 and c == 0:
                        wait("vector", "tE", last_h_tE)
                    op_copy("vector", XgR[:, 512 * c + 128 * tt:512 * c + 128 * tt + 128],
                            pb[bk_i][:, 0:128])
                    v_e2 = inc("vector", "vE")
                    bank_wait[bk_i] = ("vE", v_e2)
            xg_slot_free[slot] = ctr["tE"]
            wait("scalar", "vE", ctr["vE"])
            wait("scalar", "tE", last_h_tE)
            op_act(XgT_hi, XgR, AF.Copy)
            drain("scalar")
            s_x = inc("scalar", "sE")
            wait("vector", "sE", s_x)
            op_tt(XgT_lo, XgR, XgT_hi, ALU.subtract)
            drain("vector")
            return inc("vector", "vE")

        issue_gathers(0)
        xgT_v = None
        for p, (e, b) in enumerate(pairs):
            r = 8 * b + e
            slot = p % 2
            if p == 0:
                issue_gathers(1)
                xgT_v = emit_xg_trans(0)
            elif p + 1 < len(pairs):
                issue_gathers(p + 1)

            wait("tensor", "vE", xgT_v)
            v_w_h = {}
            for qq in range(32 + 2):
                if qq < 32:
                    slotw = qq % 4
                    wait("sync", "tE", wslotC_free[slotw])
                    d_w = dma("sync", wslot_views(slotw)[0]
                              .rearrange("p (c j) -> p c j", c=8),
                              w1r[e, qq], "dS")
                    v_w_h[qq] = split_wblock(slotw, d_w)
                if qq < 2:
                    continue
                q = qq - 2
                slotw = q % 4
                _, w_hi, w_lo = wslot_views(slotw)
                wait("tensor", "vE", v_w_h[q])
                bk_i = 4 + (q % 4)
                use_banks([bk_i])
                nmm = 0
                for c in range(EC):
                    for wp, xp in PASSES:
                        wt = (w_lo if wp else w_hi)[:, 128 * c:128 * c + 128]
                        xt_ = (XgT_lo if xp else XgT_hi)[:, 512 * c:512 * c + 512]
                        op_matmul(pb[bk_i][:, 0:512], wt, xt_, nmm == 0, nmm == 23)
                        nmm += 1
                wslotC_free[slotw] = inc("tensor", "tE")
                t_h = ctr["tE"]
                gsl = q % 2
                wait("scalar", "tE", t_h)
                wait("scalar", "vE", gel_free[gsl])
                if q < 2:
                    wait("scalar", "tE", last_y_tE)
                op_act(gel[gsl], pb[bk_i][:, 0:512], AF.Gelu)
                drain("scalar")
                op_act(hT_hi[:, 512 * q:512 * q + 512], gel[gsl], AF.Copy)
                drain("scalar")
                s_h = inc("scalar", "sE")
                bank_wait[bk_i] = ("sE", s_h)
                wait("vector", "sE", s_h)
                op_tt(hT_lo[:, 512 * q:512 * q + 512], gel[gsl],
                      hT_hi[:, 512 * q:512 * q + 512], ALU.subtract)
                drain("vector")
                gel_free[gsl] = inc("vector", "vE")
            last_h_tE = ctr["tE"]
            h_done_vE = ctr["vE"]

            use_banks(range(8))
            wait("tensor", "vE", h_done_vE)
            v_y_half = [0, 0]
            for half in range(2):
                v_w_y = {}
                for hq in range(32 + 2):
                    if hq < 32:
                        slotw = hq % 4
                        wait("sync", "tE", wslotC_free[slotw])
                        d_w = dma("sync", wslot_views(slotw)[0],
                                  w2f[e, 128 * hq:128 * hq + 128, :], "dS")
                        v_w_y[hq] = split_wblock(slotw, d_w)
                    if hq < 2:
                        continue
                    hh = hq - 2
                    slotw = hh % 4
                    _, w_hi, w_lo = wslot_views(slotw)
                    wait("tensor", "vE", v_w_y[hh])
                    for cc4 in range(4):
                        cc = 4 * half + cc4
                        for ip, (wp, xp) in enumerate(PASSES):
                            wt = (w_lo if wp else w_hi)[:, 128 * cc:128 * cc + 128]
                            ht_ = (hT_lo if xp else hT_hi)[:, 512 * hh:512 * hh + 512]
                            op_matmul(pb[cc][:, 0:512], wt, ht_,
                                      hh == 0 and ip == 0, hh == 31 and ip == 2)
                    wslotC_free[slotw] = inc("tensor", "tE")
                if half == 0:
                    t_half = ctr["tE"]
                    wait("vector", "tE", t_half)
                    for cc4 in range(4):
                        op_copy("vector", yT[:, 512 * cc4:512 * cc4 + 512], pb[cc4][:, 0:512])
                    v_y_half[0] = inc("vector", "vE")
                    for cc4 in range(4):
                        bank_wait[cc4] = ("vE", v_y_half[0])
            last_y_tE = ctr["tE"]
            next_xgT_v = emit_xg_trans(p + 1) if p + 1 < len(pairs) else None
            wait("vector", "tE", last_y_tE)
            for cc4 in range(4):
                cc = 4 + cc4
                op_copy("vector", yT[:, 512 * cc:512 * cc + 512], pb[cc][:, 0:512])
            v_y_half[1] = inc("vector", "vE")
            for cc4 in range(4):
                bank_wait[4 + cc4] = ("vE", v_y_half[1])
            v_y = v_y_half[1]

            wait("tensor", "vE", v_y)
            for tt in range(4):
                for cc in range(EC):
                    bk_i = (tt * EC + cc) % 4
                    use_banks([bk_i])
                    op_transpose(pb[bk_i][:, 0:128],
                                 yT[:, 512 * cc + 128 * tt:512 * cc + 128 * tt + 128],
                                 ident)
                    t_tr = inc("tensor", "tE")
                    wait("vector", "tE", t_tr)
                    if tt == 0 and cc == 0:
                        wait("vector", "dG", last_pair_scatter)
                    op_ts(y_tok[:, 1024 * tt + 128 * cc:1024 * tt + 128 * cc + 128],
                          pb[bk_i][:, 0:128], wpT[:, 32 * tt + r:32 * tt + r + 1], ALU.mult)
                    v_e3 = inc("vector", "vE")
                    bank_wait[bk_i] = ("vE", v_e3)
            last_ytok_vE = ctr["vE"]
            wait("gpsimd", "vE", last_ytok_vE)
            wait("gpsimd", "dG", scatter_done[b])
            for tt in range(4):
                g = idma_scatter_add(acc_flat, idxT[:, 32 * tt + r:32 * tt + r + 1],
                                     y_tok[:, 1024 * tt:1024 * tt + 1024])
            scatter_done[b] = g
            last_pair_scatter = g
            xgT_v = next_xgT_v

        all_scatter_dG = ctr["dG"]

        if max_phase <= 3:
            wait("gpsimd", "dG", ctr["dG"])
            wait("gpsimd", "dA", ctr["dA"])
            return _finish(nc, prog, block_ctx=None)

        dslot_free = [0, 0]
        store_slot = [0, 0]
        wait("sync", "dG", all_scatter_dG)
        for b in range(b_pc):
            for tl in range(16):
                slot = tl % 2
                row0 = b * S + 128 * tl
                wait("sync", "vE", dslot_free[slot])
                d_l = dma("sync", tokbuf[:, 4096 * slot:4096 * slot + 1024],
                          acc_flat[row0:row0 + 128, :], "dS")
                ld = tokbuf[:, 4096 * slot:4096 * slot + 1024]
                ot = tokbuf[:, 4096 * slot + 1024:4096 * slot + 2048]
                wait("vector", "dS", d_l)
                emit_ln(ld, ot, store_wait=store_slot[slot])
                v_o = ctr["vE"]
                dslot_free[slot] = v_o
                wait("scalar", "vE", v_o)
                store_slot[slot] = dma("scalar", out_d[b, 128 * tl:128 * tl + 128, :], ot, "dA")
        final_dA = ctr["dA"]
        wait("gpsimd", "dA", final_dA)
        if debug:
            wait("gpsimd", "dG", ctr["dG"])

        return _finish(nc, prog, block_ctx=None)


def _finish(nc, prog, block_ctx):
    with nc.Block() as block:
        @block.sync
        def _(q):
            for f in prog["sync"]:
                f(q)

        @block.scalar
        def _(q):
            for f in prog["scalar"]:
                f(q)

        @block.vector
        def _(q):
            for f in prog["vector"]:
                f(q)

        @block.tensor
        def _(q):
            for f in prog["tensor"]:
                f(q)

        @block.gpsimd
        def _(q):
            for f in prog["gpsimd"]:
                f(q)
    return nc


def make_consts(bq, bk, bv, bo, gw):
    c = np.zeros((128, 243), dtype=np.float32)
    c[:, 0:128] = np.eye(128, dtype=np.float32)
    c[:, 128] = 1.0
    c[:, 129:137] = 1.0
    c[::32, 137:145] = 1.0
    r = np.arange(32)
    c[0:32, 145] = (r // 8) * float(S)
    for i, b in enumerate((bq, bk, bv, bo)):
        c[:, 147 + 8 * i:155 + 8 * i] = np.asarray(b, np.float32).reshape(8, 128).T
    c[:, 179:243] = np.asarray(gw, np.float32).reshape(8, 128, 8).transpose(1, 0, 2).reshape(128, 64)
    return c


def prep_inputs(L0_attn_x, L0_attn_Wq, L0_attn_bq, L0_attn_Wk, L0_attn_bk,
                L0_attn_Wv, L0_attn_bv, L0_attn_Wo, L0_attn_bo,
                L0_ffn_gate_w, L0_ffn_w1, L0_ffn_w2, b_pc=4, n_cores=8):
    x = np.ascontiguousarray(np.asarray(L0_attn_x, dtype=np.float32))
    w1 = np.asarray(L0_ffn_w1, dtype=np.float32)
    w2 = np.asarray(L0_ffn_w2, dtype=np.float32)
    w1r = np.ascontiguousarray(
        w1.reshape(NE, 8, 128, 32, 128).transpose(0, 3, 2, 1, 4))
    w2f = np.ascontiguousarray(w2)
    consts = make_consts(L0_attn_bq, L0_attn_bk, L0_attn_bv, L0_attn_bo, L0_ffn_gate_w)
    common = dict(
        Wq=np.ascontiguousarray(L0_attn_Wq, dtype=np.float32),
        Wk=np.ascontiguousarray(L0_attn_Wk, dtype=np.float32),
        Wv=np.ascontiguousarray(L0_attn_Wv, dtype=np.float32),
        Wo=np.ascontiguousarray(L0_attn_Wo, dtype=np.float32),
        bq=np.ascontiguousarray(L0_attn_bq, dtype=np.float32),
        bk=np.ascontiguousarray(L0_attn_bk, dtype=np.float32),
        bv=np.ascontiguousarray(L0_attn_bv, dtype=np.float32),
        bo=np.ascontiguousarray(L0_attn_bo, dtype=np.float32),
        gw=np.ascontiguousarray(L0_ffn_gate_w, dtype=np.float32),
        w1r=w1r, w2f=w2f, consts=consts,
    )
    in_maps = []
    for c in range(n_cores):
        m = dict(common)
        m["x"] = np.ascontiguousarray(x[c * b_pc:(c + 1) * b_pc])
        in_maps.append(m)
    return in_maps


def kernel(**inputs):
    b_pc = 4
    n_cores = 8
    in_maps = prep_inputs(**inputs, b_pc=b_pc, n_cores=n_cores)
    nc = build_program(b_pc=b_pc, debug=False)
    res = run_bass_kernel_spmd(nc, in_maps, core_ids=list(range(n_cores)))
    out = np.concatenate([r["out"] for r in res.results], axis=0)
    return out


def check_deadlock(nc):
    trace = nc._trace
    pos = {e: 0 for e in trace}
    sems = {n: 0 for n in ("dS", "dA", "dG", "tE", "vE", "sE")}
    progress = True
    while progress:
        progress = False
        for e in trace:
            while pos[e] < len(trace[e]):
                kind, sem, val = trace[e][pos[e]]
                if kind == "inc":
                    sems[sem] += val
                    pos[e] += 1
                    progress = True
                else:
                    if sems[sem] >= val:
                        pos[e] += 1
                        progress = True
                    else:
                        break
    stuck = {e: (pos[e], len(trace[e]), trace[e][pos[e]] if pos[e] < len(trace[e]) else None)
             for e in trace if pos[e] < len(trace[e])}
    if stuck:
        print("DEADLOCK:", stuck, "sems:", sems)
        return False
    print("no deadlock;", {e: len(trace[e]) for e in trace})
    return True


# version 15
# speedup vs baseline: 1.0232x; 1.0163x over previous
import numpy as np
from contextlib import ExitStack

import concourse.bass as bass
import concourse.mybir as mybir
from concourse.bass import IndirectOffsetOnAxis
from concourse.bass_utils import run_bass_kernel_spmd

F32 = mybir.dt.float32
F32R = mybir.dt.float32r
BF16 = mybir.dt.bfloat16
U16 = mybir.dt.uint16
U32 = mybir.dt.uint32
AF = mybir.ActivationFunctionType
AX = mybir.AxisListType
ALU = mybir.AluOpType

E = 1024
EC = E // 128
S = 2048
CH = 256
NCH = S // CH
NE = 8
CAP = 512
H = 4096
NCORE = 8

ENGINES = ("sync", "scalar", "vector", "tensor", "gpsimd")
A0_STEPS = {"load", "trans", "mm", "evac"}


EPS = 1e-5


def build_program(b_pc=4, debug=False, max_phase=4, sub_phase=9):
    nc = bass.Bass("TRN2", target_bir_lowering=False)

    x_in = nc.dram_tensor("x", [b_pc, S, E], F32, kind="ExternalInput").ap()
    Wq = nc.dram_tensor("Wq", [E, E], F32, kind="ExternalInput").ap()
    Wk = nc.dram_tensor("Wk", [E, E], F32, kind="ExternalInput").ap()
    Wv = nc.dram_tensor("Wv", [E, E], F32, kind="ExternalInput").ap()
    Wo = nc.dram_tensor("Wo", [E, E], F32, kind="ExternalInput").ap()
    bq = nc.dram_tensor("bq", [E], F32, kind="ExternalInput").ap()
    bk = nc.dram_tensor("bk", [E], F32, kind="ExternalInput").ap()
    bv = nc.dram_tensor("bv", [E], F32, kind="ExternalInput").ap()
    bo = nc.dram_tensor("bo", [E], F32, kind="ExternalInput").ap()
    gw = nc.dram_tensor("gw", [E, NE], F32, kind="ExternalInput").ap()
    w1r = nc.dram_tensor("w1r", [NE, 32, 128, 8, 128], F32, kind="ExternalInput").ap()
    w2f = nc.dram_tensor("w2f", [NE, H, E], F32, kind="ExternalInput").ap()
    consts = nc.dram_tensor("consts", [128, 243], F32, kind="ExternalInput").ap()

    out_d = nc.dram_tensor("out", [b_pc, S, E], F32, kind="ExternalOutput").ap()
    dbg_kind = "ExternalOutput" if debug else "Internal"
    xt_d = nc.dram_tensor("xt_d", [b_pc * S, E], F32, kind=dbg_kind).ap()
    acc_d = nc.dram_tensor("acc_d", [b_pc * S, E], F32, kind=dbg_kind).ap()
    raw_d = nc.dram_tensor("raw_d", [b_pc, NE, S], F32, kind="Internal").ap()
    mean_d = nc.dram_tensor("mean_d", [b_pc, NE, S], F32, kind="Internal").ap()
    sdev_d = nc.dram_tensor("sdev_d", [b_pc, S], F32, kind="Internal").ap()
    if debug:
        probs_d = nc.dram_tensor("probs_d", [32, S], F32, kind="ExternalOutput").ap()
        idxt_d = nc.dram_tensor("idxt_d", [128, 128], U32, kind="ExternalOutput").ap()
        wpt_d = nc.dram_tensor("wpt_d", [128, 128], F32, kind="ExternalOutput").ap()

    ctx = ExitStack()
    sb = lambda name, shape, dt=F32: ctx.enter_context(nc.sbuf_tensor(name, shape, dt))
    psum = lambda name, shape: ctx.enter_context(nc.psum_tensor(name, shape, F32))
    mksem = lambda name: ctx.enter_context(nc.semaphore(name))

    with ctx:
        consts_sb = sb("consts_sb", [128, 243])
        bv9_sb = sb("bv9_sb", [128, 8])
        gsumneg = sb("gsumneg", [8, 1])
        eps_t = sb("eps_t", [128, 1])
        half3_t = sb("half3_t", [128, 1])
        kt_ht = sb("kt_ht", [128, 16384])
        wbuf = sb("wbuf", [128, 8192])
        tokbuf = sb("tokbuf", [128, 8192])
        actbuf = sb("actbuf", [128, 16896])
        gstage = sb("gstage", [8, 1024])
        sstage = sb("sstage", [1, 512])
        rstd_buf = sb("rstd_buf", [128, 16])
        rstdT_sb = sb("rstdT_sb", [16, 128])
        idxT = sb("idxT", [128, 128], U32)
        wpT = sb("wpT", [128, 128])
        idx16_t = sb("idx16_t", [32, 512], U16)
        stat = sb("stat", [128, 16])
        gel_t = sb("gel_t", [128, 1024])

        pb = [psum(f"pb{i}", [128, 512]) for i in range(8)]

        sems = {n: mksem(n) for n in ("dS", "dA", "dG", "tE", "vE", "sE")}
        ctr = {n: 0 for n in sems}
        prog = {e: [] for e in ENGINES}
        trace = {e: [] for e in ENGINES}
        nc._trace = trace

        wrappable = {e: False for e in ENGINES}

        def emit(e, fn, op=True):
            prog[e].append(fn)
            wrappable[e] = op

        def wait(e, sem, val):
            if val <= 0:
                return
            trace[e].append(("wait", sem, val))
            emit(e, lambda q, s=sems[sem], v=val: q.wait_ge(s, v), op=False)

        def drain(e):
            emit(e, lambda q: q.drain())

        def inc(e, sem):
            ctr[sem] += 1
            trace[e].append(("inc", sem, 1))
            assert wrappable[e], f"inc({e},{sem}) with no preceding instruction"
            prev = prog[e][-1]
            prog[e][-1] = (lambda q, p=prev, s=sems[sem]: p(q).then_inc(s, 1))
            return ctr[sem]

        def dma(e, out, in_, sem, slow=False):
            ctr[sem] += 16
            trace[e].append(("inc", sem, 16))
            emit(e, lambda q, o=out, i=in_, s=sems[sem], sl=slow:
                 q.dma_start(out=o, in_=i, allow_slow_non_contiguous=sl).then_inc(s, 16))
            return ctr[sem]

        def idma_gather(out, in_, idx_ap):
            ctr["dG"] += 16
            trace["gpsimd"].append(("inc", "dG", 16))
            emit("gpsimd", lambda q, o=out, i=in_, x=idx_ap, s=sems["dG"]:
                 q.indirect_dma_start(out=o, out_offset=None, in_=i,
                                      in_offset=IndirectOffsetOnAxis(x, 0)).then_inc(s, 16))
            return ctr["dG"]

        def idma_scatter_add(out, idx_ap, in_):
            ctr["dG"] += 16
            trace["gpsimd"].append(("inc", "dG", 16))
            emit("gpsimd", lambda q, o=out, i=in_, x=idx_ap, s=sems["dG"]:
                 q.indirect_dma_start(out=o, out_offset=IndirectOffsetOnAxis(x, 0),
                                      in_=i, in_offset=None,
                                      compute_op=ALU.add).then_inc(s, 16))
            return ctr["dG"]

        def op_matmul(o, l, r, st, sp):
            emit("tensor", lambda q, o=o, l=l, r=r, st=st, sp=sp:
                 q.matmul(o, l, r, start=st, stop=sp))

        def op_transpose(o, i, ident):
            emit("tensor", lambda q, o=o, i=i, d=ident: q.transpose(o, i, d))

        def op_copy(e, o, i):
            emit(e, lambda q, o=o, i=i: (q.tensor_copy(o, i) if e == "vector" else q.copy(o, i)))

        def op_act(o, i, func, bias=None, scale=None, accum=None):
            kw = {}
            if bias is not None:
                kw["bias"] = bias
            if scale is not None:
                kw["scale"] = scale
            if accum is not None:
                kw["accum_out"] = accum
            emit("scalar", lambda q, o=o, i=i, f=func, kw=kw: q.activation(o, i, f, **kw))

        def op_ts(o, i, s1, op0, s2=None, op1=None, accum=None):
            kw = {}
            if op1 is not None:
                kw["op1"] = op1
            if accum is not None:
                kw["accum_out"] = accum
            emit("vector", lambda q, o=o, i=i, s1=s1, s2=s2, op0=op0, kw=kw:
                 q.tensor_scalar(o, i, s1, s2, op0=op0, **kw))

        def op_stt(o, i0, s, i1, op0, op1):
            emit("vector", lambda q, o=o, i0=i0, s=s, i1=i1, a=op0, b=op1:
                 q.scalar_tensor_tensor(o, i0, s, i1, a, b))

        def op_tt(o, a, b, op):
            emit("vector", lambda q, o=o, a=a, b=b, op=op: q.tensor_tensor(o, a, b, op=op))

        def op_red(o, i, negate=False, rsum=False):
            emit("vector", lambda q, o=o, i=i, n=negate, rs=rsum:
                 (q.reduce_sum(o, i, axis=AX.X) if rs else q.reduce_max(o, i, axis=AX.X, negate=n)))

        bank_wait = [None] * 8

        def use_banks(banks):
            for i in banks:
                if bank_wait[i]:
                    wait("tensor", *bank_wait[i])
                    bank_wait[i] = None

        def emit_ln(rt, xt_a, store_wait=0):
            op_red(stat[:, 10:11], rt, rsum=True)
            drain("vector")
            op_ts(stat[:, 11:12], stat[:, 10:11], 1.0 / E, ALU.mult)
            drain("vector")
            if store_wait:
                wait("vector", "dA", store_wait)
            op_ts(xt_a, rt, stat[:, 11:12], ALU.subtract)
            v_c = inc("vector", "vE")
            wait("scalar", "vE", v_c)
            op_act(sq_scr[:], xt_a, AF.Square, accum=stat[:, 12:13])
            drain("scalar")
            op_act(stat[:, 14:15], stat[:, 12:13], AF.Ln, bias=eps_t[:], scale=1.0 / E)
            drain("scalar")
            op_act(stat[:, 13:14], stat[:, 14:15], AF.Exp, scale=-0.5)
            drain("scalar")
            s_r0 = inc("scalar", "sE")
            wait("vector", "sE", s_r0)
            op_ts(stat[:, 12:13], stat[:, 12:13], 1.0 / E, ALU.mult, s2=EPS, op1=ALU.add)
            drain("vector")
            op_tt(stat[:, 14:15], stat[:, 12:13], stat[:, 13:14], ALU.mult)
            drain("vector")
            op_tt(stat[:, 14:15], stat[:, 14:15], stat[:, 13:14], ALU.mult)
            drain("vector")
            v_nt = inc("vector", "vE")
            wait("scalar", "vE", v_nt)
            op_act(stat[:, 14:15], stat[:, 14:15], AF.Identity, bias=half3_t[:], scale=-0.5)
            drain("scalar")
            s_nt = inc("scalar", "sE")
            wait("vector", "sE", s_nt)
            op_tt(stat[:, 13:14], stat[:, 13:14], stat[:, 14:15], ALU.mult)
            drain("vector")
            op_ts(xt_a, xt_a, stat[:, 13:14], ALU.mult)
            inc("vector", "vE")

        ident = consts_sb[:, 0:128]
        ones_col = consts_sb[:, 128:129]
        ones_k8 = consts_sb[0:8, 129:137]
        bcrow8 = consts_sb[0:8, 137:145]
        boff32 = consts_sb[0:32, 145:146]

        xt_flat = xt_d
        acc_flat = acc_d

        v_setup = dma("sync", consts_sb[:], consts, "dS")
        bq_sb = consts_sb[:, 147:155]
        bk_sb = consts_sb[:, 155:163]
        bv_sb = consts_sb[:, 163:171]
        bo_sb = consts_sb[:, 171:179]
        gw_sb = consts_sb[:, 179:243]
        wait("vector", "dS", v_setup)
        op_ts(bv9_sb[:], bv_sb[:], 9.0, ALU.mult)
        emit("vector", lambda q: q.memset(eps_t[:], 1e-5))
        emit("vector", lambda q: q.memset(half3_t[:], 1.5))
        setup_v = inc("vector", "vE")
        wait("tensor", "dS", v_setup)
        for c in range(EC):
            op_matmul(pb[0][0:8, 0:8], gw_sb[:, 8 * c:8 * c + 8],
                      consts_sb[:, 129:137], c == 0, c == EC - 1)
        t_gsum = inc("tensor", "tE")
        wait("scalar", "tE", t_gsum)
        op_act(gsumneg[:], pb[0][0:8, 0:1], AF.Copy, scale=-1.0)
        s_gsum = inc("scalar", "sE")
        bank_wait[0] = ("sE", s_gsum)

        xTn = actbuf[:, 0:2048]
        QTn = actbuf[:, 2048:4096]
        Vn = actbuf[:, 4096:6144]
        combT = actbuf[:, 6144:8192]
        resT = actbuf[:, 8192:10240]
        res_tok = actbuf[:, 10240:12288]
        xt_tok = actbuf[:, 12288:14336]
        sq_scr = actbuf[:, 14336:15360]
        exp_sb = actbuf[:, 15360:15872]
        Ptil = actbuf[:, 15872:16384]
        PtilT = actbuf[:, 16384:16896]
        xT_tb = actbuf[:, 0:4096]

        tok_slot_free = [0, 0]
        wslot_free = [0, 0]
        last_scores_tE = 0
        last_AV_tE = 0
        last_O_tE = 0
        last_resTrans_tE = 0
        last_sq_sE = 0
        last_stt_vE = 0
        gstage_free = [0, 0]
        sstage_free = [0, 0]
        last_xt_dA = 0

        def load_weight_block_A(w_ap, k, slot):
            wait("sync", "tE", wslot_free[slot])
            return dma("sync", wbuf[:, 1024 * slot:1024 * slot + 1024],
                       w_ap[128 * k:128 * k + 128, :], "dS")

        for b in range(b_pc if sub_phase >= 0 else 0):
            for tb in range(4 if sub_phase >= 0 else 0):
                slot = tb % 2
                wait("sync", "tE", tok_slot_free[slot])
                d_x = dma("sync", tokbuf[:, 4096 * slot:4096 * slot + 4096]
                          .rearrange("p (a c) -> p a c", a=4),
                          x_in[b, 512 * tb:512 * tb + 512, :]
                          .rearrange("(a p) c -> p a c", p=128), "dS")
                if "trans" in A0_STEPS:
                    wait("tensor", "dS", d_x)
                    if tb == 0 and b == 0:
                        wait("tensor", "vE", setup_v)
                    for a in range(4):
                        for c in range(EC):
                            bk_i = (a * EC + c) % 4
                            use_banks([bk_i])
                            op_transpose(pb[bk_i][:, 0:128],
                                         tokbuf[:, 4096 * slot + 1024 * a + 128 * c:
                                                4096 * slot + 1024 * a + 128 * c + 128],
                                         ident)
                            t_tr = inc("tensor", "tE")
                            wait("vector", "tE", t_tr)
                            op_copy("vector", xT_tb[:, 512 * c + 128 * a:512 * c + 128 * a + 128],
                                    pb[bk_i][:, 0:128])
                            v_e = inc("vector", "vE")
                            bank_wait[bk_i] = ("vE", v_e)
                    tok_slot_free[slot] = ctr["tE"]
                if "mm" in A0_STEPS:
                    use_banks(range(8))
                    wait("tensor", "vE", ctr["vE"])
                    for k in range(EC):
                        slotw = k % 2
                        d_w = load_weight_block_A(Wk, k, slotw)
                        wait("tensor", "dS", d_w)
                        for ep in range(8):
                            op_matmul(pb[ep][:, 0:512],
                                      wbuf[:, 1024 * slotw + 128 * ep:1024 * slotw + 128 * ep + 128],
                                      xT_tb[:, 512 * k:512 * k + 512],
                                      k == 0, k == EC - 1)
                        wslot_free[slotw] = inc("tensor", "tE")
                if "evac" in A0_STEPS:
                    t_kt = ctr["tE"]
                    wait("scalar", "tE", t_kt)
                    if b > 0:
                        wait("scalar", "tE", last_scores_tE)
                    for ep in range(8):
                        op_act(kt_ht[:, 2048 * ep + 512 * tb:2048 * ep + 512 * tb + 512],
                               pb[ep][:, 0:512], AF.Identity, bias=bk_sb[:, ep:ep + 1])
                    s_kt = inc("scalar", "sE")
                    for ep in range(8):
                        bank_wait[ep] = ("sE", s_kt)
            kt_ready_sE = ctr["sE"]

            def emit_x_trans(n):
                slot = n % 2
                wait("sync", "tE", tok_slot_free[slot])
                d_x = dma("sync", tokbuf[:, 4096 * slot:4096 * slot + 2048]
                          .rearrange("p (a c) -> p a c", a=2),
                          x_in[b, 256 * n:256 * n + 256, :]
                          .rearrange("(a p) c -> p a c", p=128), "dS")
                wait("tensor", "dS", d_x)
                for a in range(2):
                    for c in range(EC):
                        bk_i = (a * EC + c) % 4
                        use_banks([bk_i])
                        op_transpose(pb[bk_i][:, 0:128],
                                     tokbuf[:, 4096 * slot + 1024 * a + 128 * c:
                                            4096 * slot + 1024 * a + 128 * c + 128],
                                     ident)
                        t_tr = inc("tensor", "tE")
                        wait("vector", "tE", t_tr)
                        op_copy("vector", xTn[:, 256 * c + 128 * a:256 * c + 128 * a + 128],
                                pb[bk_i][:, 0:128])
                        v_e = inc("vector", "vE")
                        bank_wait[bk_i] = ("vE", v_e)
                tok_slot_free[slot] = ctr["tE"]
                return ctr["vE"]

            next_xTn_v = None
            for n in range(NCH if sub_phase >= 1 else 0):
                slot = n % 2
                xTn_v = next_xTn_v if next_xTn_v is not None else emit_x_trans(n)
                next_xTn_v = None

                use_banks(range(8))
                wait("tensor", "vE", xTn_v)
                for k in range(EC):
                    slotw = k % 2
                    d_w = load_weight_block_A(Wq, k, slotw)
                    wait("tensor", "dS", d_w)
                    for ep in range(8):
                        op_matmul(pb[ep][:, 0:256],
                                  wbuf[:, 1024 * slotw + 128 * ep:1024 * slotw + 128 * ep + 128],
                                  xTn[:, 256 * k:256 * k + 256],
                                  k == 0, k == EC - 1)
                    wslot_free[slotw] = inc("tensor", "tE")
                wait("scalar", "tE", ctr["tE"])
                wait("scalar", "tE", last_scores_tE)
                for ep in range(8):
                    op_act(QTn[:, 256 * ep:256 * ep + 256], pb[ep][:, 0:256],
                           AF.Identity, bias=bq_sb[:, ep:ep + 1])
                s_q = inc("scalar", "sE")
                for ep in range(8):
                    bank_wait[ep] = ("sE", s_q)

                use_banks(range(4))
                for k in range(EC):
                    slotw = k % 2
                    d_w = load_weight_block_A(Wv, k, slotw)
                    wait("tensor", "dS", d_w)
                    for ts in range(2):
                        for eb in range(2):
                            op_matmul(pb[2 * ts + eb][:, 0:512],
                                      xTn[:, 256 * k + 128 * ts:256 * k + 128 * ts + 128],
                                      wbuf[:, 1024 * slotw + 512 * eb:1024 * slotw + 512 * eb + 512],
                                      k == 0, k == EC - 1)
                    wslot_free[slotw] = inc("tensor", "tE")
                wait("vector", "tE", ctr["tE"])
                for ts in range(2):
                    for eb in range(2):
                        op_copy("vector", Vn[:, 1024 * ts + 512 * eb:1024 * ts + 512 * eb + 512],
                                pb[2 * ts + eb][:, 0:512])
                v_v = inc("vector", "vE")
                for i in range(4):
                    bank_wait[i] = ("vE", v_v)

                if b == 0 and n == 0:
                    wait("tensor", "sE", kt_ready_sE)
                first_in_qs = [True, True]
                sc_grp_tE = {}
                s2_done = {}
                s4_done = {}
                s5_done = {}

                def sc_cols(m):
                    return 5 * (m % 2)

                def blk_ap(m):
                    qs, kb, mb = m // 8, (m // 2) % 4, m % 2
                    bk_i = 4 + 2 * qs + (kb % 2)
                    return pb[bk_i][:, 256 * mb:256 * mb + 256], bk_i

                for m in range(16 + 2):
                    if 0 <= m - 1 < 16:
                        mm_ = m - 1
                        cb = sc_cols(mm_)
                        blk, bk_i = blk_ap(mm_)
                        wait("vector", "sE", s2_done[mm_])
                        op_tt(stat[:, cb + 4:cb + 5], stat[:, cb + 1:cb + 2],
                              stat[:, cb + 3:cb + 4], ALU.subtract)
                        drain("vector")
                        if 2 * ((mm_ // 2) % 4) + (mm_ % 2) == n:
                            op_ts(stat[:, cb + 4:cb + 5], stat[:, cb + 4:cb + 5],
                                  float(np.log(2.0)), ALU.add)
                            drain("vector")
                        v_b2 = inc("vector", "vE")
                        wait("scalar", "vE", v_b2)
                        op_act(exp_sb[:, 256 * (mm_ % 2):256 * (mm_ % 2) + 256], blk, AF.Exp,
                               bias=stat[:, cb + 4:cb + 5], scale=1.0 / 32.0)
                        drain("scalar")
                        s4_done[mm_] = inc("scalar", "sE")
                        bank_wait[bk_i] = ("sE", s4_done[mm_])
                    if 0 <= m - 2:
                        mm_ = m - 2
                        qs = mm_ // 8
                        wait("vector", "sE", s4_done[mm_])
                        if first_in_qs[qs]:
                            op_copy("vector", Ptil[:, 256 * qs:256 * qs + 256],
                                    exp_sb[:, 256 * (mm_ % 2):256 * (mm_ % 2) + 256])
                            first_in_qs[qs] = False
                        else:
                            op_tt(Ptil[:, 256 * qs:256 * qs + 256],
                                  Ptil[:, 256 * qs:256 * qs + 256],
                                  exp_sb[:, 256 * (mm_ % 2):256 * (mm_ % 2) + 256], ALU.add)
                        drain("vector")
                        s5_done[mm_] = inc("vector", "vE")
                    if m < 16 and m % 2 == 0:
                        qs, kb = m // 8, (m // 2) % 4
                        bk_i = 4 + 2 * qs + (kb % 2)
                        use_banks([bk_i])
                        if kb == 0:
                            wait("tensor", "sE", s_q)
                        for c in range(EC):
                            op_matmul(pb[bk_i][:, 0:512],
                                      QTn[:, 256 * c + 128 * qs:256 * c + 128 * qs + 128],
                                      kt_ht[:, 2048 * c + 512 * kb:2048 * c + 512 * kb + 512],
                                      c == 0, c == EC - 1)
                        sc_grp_tE[m] = sc_grp_tE[m + 1] = inc("tensor", "tE")
                    if m < 16:
                        cb = sc_cols(m)
                        blk, bk_i = blk_ap(m)
                        wait("vector", "tE", sc_grp_tE[m])
                        if m % 2 == 1:
                            wait("vector", "sE", s4_done[m - 1])
                        op_red(stat[:, cb:cb + 1], blk, negate=True)
                        drain("vector")
                        op_ts(stat[:, cb + 1:cb + 2], stat[:, cb:cb + 1], 1.0 / 32.0, ALU.mult)
                        drain("vector")
                        v_m = inc("vector", "vE")
                        wait("scalar", "vE", v_m)
                        if m >= 2:
                            wait("scalar", "vE", s5_done[m - 2])
                        op_act(exp_sb[:, 256 * (m % 2):256 * (m % 2) + 256], blk, AF.Exp,
                               bias=stat[:, cb + 1:cb + 2], scale=1.0 / 32.0,
                               accum=stat[:, cb + 2:cb + 3])
                        drain("scalar")
                        op_act(stat[:, cb + 3:cb + 4], stat[:, cb + 2:cb + 3], AF.Ln)
                        drain("scalar")
                        s2_done[m] = inc("scalar", "sE")
                ptil_v = ctr["vE"]

                if sub_phase < 3:
                    continue
                wait("tensor", "vE", ptil_v)
                for qs in range(2):
                    for ks in range(2):
                        bk_i = (2 * qs + ks) % 4
                        use_banks([bk_i])
                        op_transpose(pb[bk_i][:, 0:128],
                                     Ptil[:, 256 * qs + 128 * ks:256 * qs + 128 * ks + 128],
                                     ident)
                        t_tr = inc("tensor", "tE")
                        wait("vector", "tE", t_tr)
                        op_copy("vector", PtilT[:, 256 * ks + 128 * qs:256 * ks + 128 * qs + 128],
                                pb[bk_i][:, 0:128])
                        v_e = inc("vector", "vE")
                        bank_wait[bk_i] = ("vE", v_e)
                ptilT_v = ctr["vE"]

                wait("tensor", "vE", ptilT_v)
                for ep in range(8):
                    bk_i = ep % 4
                    use_banks([bk_i])
                    for ks in range(2):
                        op_matmul(pb[bk_i][:, 0:256],
                                  Vn[:, 1024 * ks + 128 * ep:1024 * ks + 128 * ep + 128],
                                  PtilT[:, 256 * ks:256 * ks + 256],
                                  ks == 0, ks == 1)
                    last_AV_tE = inc("tensor", "tE")
                    wait("scalar", "tE", last_AV_tE)
                    if ep == 0:
                        wait("scalar", "tE", last_O_tE)
                    op_act(combT[:, 256 * ep:256 * ep + 256], pb[bk_i][:, 0:256],
                           AF.Identity, bias=bv9_sb[:, ep:ep + 1])
                    s_av = inc("scalar", "sE")
                    bank_wait[bk_i] = ("sE", s_av)
                comb_s = ctr["sE"]

                if sub_phase < 4:
                    continue
                use_banks(range(8))
                wait("tensor", "sE", comb_s)
                for c2 in range(EC):
                    slotw = c2 % 2
                    d_w = load_weight_block_A(Wo, c2, slotw)
                    wait("tensor", "dS", d_w)
                    for ep in range(8):
                        op_matmul(pb[ep][:, 0:256],
                                  wbuf[:, 1024 * slotw + 128 * ep:1024 * slotw + 128 * ep + 128],
                                  combT[:, 256 * c2:256 * c2 + 256],
                                  c2 == 0, c2 == EC - 1)
                    wslot_free[slotw] = inc("tensor", "tE")
                last_O_tE = ctr["tE"]
                wait("vector", "tE", last_O_tE)
                wait("vector", "tE", last_resTrans_tE)
                for ep in range(8):
                    op_stt(resT[:, 256 * ep:256 * ep + 256], pb[ep][:, 0:256],
                           bo_sb[:, ep:ep + 1], xTn[:, 256 * ep:256 * ep + 256],
                           ALU.add, ALU.add)
                v_res = inc("vector", "vE")
                for ep in range(8):
                    bank_wait[ep] = ("vE", v_res)

                if sub_phase < 5:
                    continue
                gslot = n % 2
                use_banks([0, 1])
                wait("tensor", "vE", v_res)
                for c in range(EC):
                    op_matmul(pb[0][0:8, 0:256], gw_sb[:, 8 * c:8 * c + 8],
                              resT[:, 256 * c:256 * c + 256], c == 0, c == EC - 1)
                for c in range(EC):
                    op_matmul(pb[1][0:8, 0:256], consts_sb[:, 129:137],
                              resT[:, 256 * c:256 * c + 256], c == 0, c == EC - 1)
                t_g = inc("tensor", "tE")
                wait("vector", "tE", t_g)
                wait("vector", "dA", gstage_free[gslot])
                op_copy("vector", gstage[0:8, 512 * gslot:512 * gslot + 256], pb[0][0:8, 0:256])
                v_g = inc("vector", "vE")
                wait("scalar", "tE", t_g)
                wait("scalar", "dA", gstage_free[gslot])
                op_act(gstage[0:8, 512 * gslot + 256:512 * gslot + 512], pb[1][0:8, 0:256],
                       AF.Copy, scale=1.0 / E)
                s_g = inc("scalar", "sE")
                bank_wait[0] = ("vE", v_g)
                bank_wait[1] = ("sE", s_g)
                wait("scalar", "vE", v_g)
                d1 = dma("scalar", raw_d[b, :, 256 * n:256 * n + 256],
                         gstage[0:8, 512 * gslot:512 * gslot + 256], "dA")
                d2 = dma("scalar", mean_d[b, :, 256 * n:256 * n + 256],
                         gstage[0:8, 512 * gslot + 256:512 * gslot + 512], "dA")
                gstage_free[gslot] = d2

                if sub_phase < 6:
                    continue
                for c2 in range(EC):
                    for a in range(2):
                        bk_i = 2 + (c2 * 2 + a) % 2
                        use_banks([bk_i])
                        op_transpose(pb[bk_i][:, 0:128],
                                     resT[:, 256 * c2 + 128 * a:256 * c2 + 128 * a + 128],
                                     ident)
                        t_tr = inc("tensor", "tE")
                        wait("vector", "tE", t_tr)
                        if c2 == 0 and a == 0:
                            wait("vector", "sE", last_sq_sE)
                        op_copy("vector", res_tok[:, 1024 * a + 128 * c2:1024 * a + 128 * c2 + 128],
                                pb[bk_i][:, 0:128])
                        v_e = inc("vector", "vE")
                        bank_wait[bk_i] = ("vE", v_e)
                last_resTrans_tE = ctr["tE"]
                v_rtok = ctr["vE"]

                sslot = n % 2
                wait("vector", "vE", v_rtok)
                for a in range(2):
                    emit_ln(res_tok[:, 1024 * a:1024 * a + 1024],
                            xt_tok[:, 1024 * a:1024 * a + 1024],
                            store_wait=(last_xt_dA if a == 0 else 0))
                    v_xt = ctr["vE"]
                    row0 = b * S + 256 * n + 128 * a
                    wait("scalar", "vE", v_xt)
                    dma("scalar", xt_flat[row0:row0 + 128, :],
                        xt_tok[:, 1024 * a:1024 * a + 1024], "dA")
                    last_xt_dA = dma("scalar", acc_flat[row0:row0 + 128, :],
                                     xt_tok[:, 1024 * a:1024 * a + 1024], "dA")
                    drain("vector")
                    op_copy("vector", rstd_buf[:, 2 * n + a:2 * n + a + 1], stat[:, 13:14])
                    inc("vector", "vE")
            use_banks([2])
            wait("tensor", "vE", ctr["vE"])
            op_transpose(pb[2][0:16, 0:128], rstd_buf[:], ident)
            t_sd = inc("tensor", "tE")
            wait("vector", "tE", t_sd)
            op_copy("vector", rstdT_sb[:], pb[2][0:16, 0:128])
            v_sd = inc("vector", "vE")
            bank_wait[2] = ("vE", v_sd)
            wait("scalar", "vE", v_sd)
            dma("scalar", sdev_d[b].rearrange("(j t) -> j t", j=16), rstdT_sb[:], "dA")
            last_scores_tE = ctr["tE"]

        if max_phase <= 1:
            wait("gpsimd", "dA", ctr["dA"])
            wait("gpsimd", "dS", ctr["dS"])
            return _finish(nc, prog, block_ctx=None)

        a_done_dA = ctr["dA"]
        a_done_vE = ctr["vE"]
        a_end_tE = ctr["tE"]
        sel = actbuf[0:32, 0:2048]
        work = [actbuf[0:32, 2048:4096], actbuf[0:32, 4096:6144]]
        braw = actbuf[0:8, 6144:6656]
        bmean = actbuf[0:8, 6656:7168]
        bsdev = actbuf[0:8, 7168:7680]
        btmp = actbuf[0:8, 7680:8192]
        blnz = actbuf[0:8, 8192:8704]
        bprob = actbuf[0:8, 8704:9216]
        val32 = actbuf[0:32, 9216:9728]
        idxf = actbuf[0:32, 9728:10240]
        idx16 = idx16_t[0:32, :]

        wait("sync", "dA", a_done_dA)
        wait("sync", "tE", a_end_tE)
        wait("tensor", "vE", a_done_vE)
        wait("vector", "tE", a_end_tE)
        wait("vector", "sE", s_gsum)
        emit("vector", lambda q: q.memset(bsdev[:], 0.0))
        bsdev_z = inc("vector", "vE")
        wait("sync", "vE", bsdev_z)
        for b in range(b_pc):
            for blk in range(4):
                d1 = dma("sync", braw[:], raw_d[b, :, 512 * blk:512 * blk + 512], "dS")
                d2 = dma("sync", bmean[:], mean_d[b, :, 512 * blk:512 * blk + 512], "dS")
                d3 = dma("sync", bsdev[0:1, :], sdev_d[b, 512 * blk:512 * blk + 512], "dS")
                use_banks([4, 5])
                wait("tensor", "dS", d3)
                op_matmul(pb[4][0:8, 0:512], bcrow8, bsdev[:], True, True)
                t_bc = inc("tensor", "tE")
                wait("vector", "tE", t_bc)
                wait("vector", "dS", d3)
                op_stt(btmp[:], bmean[:], gsumneg[:], braw[:], ALU.mult, ALU.add)
                drain("vector")
                op_tt(btmp[:], btmp[:], pb[4][0:8, 0:512], ALU.mult)
                drain("vector")
                v_sc = inc("vector", "vE")
                bank_wait[4] = ("vE", v_sc)
                wait("scalar", "vE", v_sc)
                op_act(blnz[:], btmp[:], AF.Exp)
                s_ex = inc("scalar", "sE")
                wait("tensor", "sE", s_ex)
                op_matmul(pb[5][0:8, 0:512], ones_k8, blnz[:], True, True)
                t_z = inc("tensor", "tE")
                wait("scalar", "tE", t_z)
                op_act(blnz[:], pb[5][0:8, 0:512], AF.Ln)
                drain("scalar")
                s_ln = inc("scalar", "sE")
                bank_wait[5] = ("sE", s_ln)
                wait("vector", "sE", s_ln)
                op_tt(btmp[:], btmp[:], blnz[:], ALU.subtract)
                drain("vector")
                v_sub = inc("vector", "vE")
                wait("scalar", "vE", v_sub)
                op_act(bprob[:], btmp[:], AF.Exp)
                s_pr = inc("scalar", "sE")
                wait("sync", "sE", s_pr)
                d_sel = dma("sync", sel[8 * b:8 * b + 8, 512 * blk:512 * blk + 512], bprob[:], "dS")
        sel_ready = ctr["dS"]
        if debug:
            wait("gpsimd", "dS", sel_ready)
            dma("gpsimd", probs_d, sel, "dG")

        wait("vector", "dS", sel_ready)
        cur = 0
        emit("vector", lambda q: q.tensor_copy(work[0][:], sel[:]))
        for r in range(64):
            emit("vector", lambda q, r=r, c=cur: q.max(
                val32[:, 8 * r:8 * r + 8], work[c][:]))
            drain("vector")
            emit("vector", lambda q, r=r, c=cur: q.max_index(
                idx16[:, 8 * r:8 * r + 8], val32[:, 8 * r:8 * r + 8], work[c][:]))
            drain("vector")
            emit("vector", lambda q, r=r, c=cur: q.match_replace(
                work[1 - c][:], val32[:, 8 * r:8 * r + 8], work[c][:], 0.0))
            drain("vector")
            cur = 1 - cur
        emit("vector", lambda q: q.tensor_copy(idxf[:], idx16[:]))
        drain("vector")
        op_ts(idxf[:], idxf[:], boff32, ALU.add)
        drain("vector")
        v_self = inc("vector", "vE")
        wait("tensor", "vE", v_self)
        for qb in range(4):
            use_banks([6])
            op_transpose(pb[6][:, 0:32], idxf[:, 128 * qb:128 * qb + 128], ident[0:32, 0:32])
            t_ti = inc("tensor", "tE")
            wait("vector", "tE", t_ti)
            op_copy("vector", idxT[:, 32 * qb:32 * qb + 32], pb[6][:, 0:32])
            v_ti = inc("vector", "vE")
            bank_wait[6] = ("vE", v_ti)
            use_banks([7])
            op_transpose(pb[7][:, 0:32], val32[:, 128 * qb:128 * qb + 128], ident[0:32, 0:32])
            t_tw = inc("tensor", "tE")
            wait("vector", "tE", t_tw)
            op_copy("vector", wpT[:, 32 * qb:32 * qb + 32], pb[7][:, 0:32])
            v_tw = inc("vector", "vE")
            bank_wait[7] = ("vE", v_tw)
        sel_done_vE = ctr["vE"]
        if debug:
            wait("gpsimd", "vE", sel_done_vE)
            dma("gpsimd", idxt_d, idxT[:], "dG")
            dma("gpsimd", wpt_d, wpT[:], "dG")

        if max_phase <= 2:
            wait("gpsimd", "dA", ctr["dA"])
            wait("gpsimd", "dS", ctr["dS"])
            wait("gpsimd", "dG", ctr["dG"])
            wait("gpsimd", "vE", ctr["vE"])
            return _finish(nc, prog, block_ctx=None)

        XgR = actbuf[:, 0:4096]
        XgT_hi = actbuf[:, 4096:6144].bitcast(BF16)
        XgT_lo = actbuf[:, 6144:8192].bitcast(BF16)
        yT = actbuf[:, 8192:12288]
        y_tok = actbuf[:, 12288:16384]
        hT_hi = kt_ht[:, 0:8192].bitcast(BF16)
        hT_lo = kt_ht[:, 8192:16384].bitcast(BF16)
        gel = [gel_t[:, 512 * i:512 * i + 512] for i in range(2)]

        phaseC_t0 = ctr["tE"]
        xg_gather_done = [0, 0]
        xg_slot_free = [0, 0]
        scatter_done = [0] * b_pc
        last_pair_scatter = 0
        last_h_tE = 0
        last_y_tE = 0
        wslotC_free = [a_end_tE] * 4
        gel_free = [0, 0]

        pairs = [(e, b) for e in range(NE) for b in range(b_pc)]
        PASSES = ((0, 0), (0, 1), (1, 0))

        def issue_gathers(p):
            e, b = pairs[p]
            r = 8 * b + e
            slot = p % 2
            wait("gpsimd", "tE", xg_slot_free[slot])
            if p == 0:
                wait("gpsimd", "dA", a_done_dA)
                wait("gpsimd", "vE", sel_done_vE)
            for tt in range(4):
                g = idma_gather(tokbuf[:, 4096 * slot + 1024 * tt:4096 * slot + 1024 * tt + 1024],
                                xt_flat, idxT[:, 32 * tt + r:32 * tt + r + 1])
            xg_gather_done[slot] = g
            return g

        def wslot_views(slotw):
            base = 2048 * slotw
            raw = wbuf[:, base:base + 1024]
            hi = wbuf[:, base + 1024:base + 1536].bitcast(BF16)
            lo = wbuf[:, base + 1536:base + 2048].bitcast(BF16)
            return raw, hi, lo

        def split_wblock(slotw, d_w):
            raw, hi, lo = wslot_views(slotw)
            wait("scalar", "dS", d_w)
            op_act(hi, raw, AF.Copy)
            drain("scalar")
            s_w = inc("scalar", "sE")
            wait("vector", "sE", s_w)
            op_tt(lo, raw, hi, ALU.subtract)
            drain("vector")
            return inc("vector", "vE")

        def emit_xg_trans(p):
            slot = p % 2
            wait("tensor", "dG", xg_gather_done[slot])
            for tt in range(4):
                for c in range(EC):
                    bk_i = (tt * EC + c) % 4
                    use_banks([bk_i])
                    op_transpose(pb[bk_i][:, 0:128],
                                 tokbuf[:, 4096 * slot + 1024 * tt + 128 * c:
                                        4096 * slot + 1024 * tt + 128 * c + 128],
                                 ident)
                    t_tr = inc("tensor", "tE")
                    wait("vector", "tE", t_tr)
                    if tt == 0 and c == 0:
                        wait("vector", "tE", last_h_tE)
                    op_copy("vector", XgR[:, 512 * c + 128 * tt:512 * c + 128 * tt + 128],
                            pb[bk_i][:, 0:128])
                    v_e2 = inc("vector", "vE")
                    bank_wait[bk_i] = ("vE", v_e2)
            xg_slot_free[slot] = ctr["tE"]
            wait("scalar", "vE", ctr["vE"])
            wait("scalar", "tE", last_h_tE)
            op_act(XgT_hi, XgR, AF.Copy)
            drain("scalar")
            s_x = inc("scalar", "sE")
            wait("vector", "sE", s_x)
            op_tt(XgT_lo, XgR, XgT_hi, ALU.subtract)
            drain("vector")
            return inc("vector", "vE")

        issue_gathers(0)
        xgT_v = None
        for p, (e, b) in enumerate(pairs):
            r = 8 * b + e
            slot = p % 2
            if p == 0:
                issue_gathers(1)
                xgT_v = emit_xg_trans(0)
            elif p + 1 < len(pairs):
                issue_gathers(p + 1)

            wait("tensor", "vE", xgT_v)
            v_w_h = {}
            for qq in range(32 + 2):
                if qq < 32:
                    slotw = qq % 4
                    wait("sync", "tE", wslotC_free[slotw])
                    d_w = dma("sync", wslot_views(slotw)[0]
                              .rearrange("p (c j) -> p c j", c=8),
                              w1r[e, qq], "dS")
                    v_w_h[qq] = split_wblock(slotw, d_w)
                if qq < 2:
                    continue
                q = qq - 2
                slotw = q % 4
                _, w_hi, w_lo = wslot_views(slotw)
                wait("tensor", "vE", v_w_h[q])
                bk_i = 4 + (q % 4)
                use_banks([bk_i])
                nmm = 0
                for c in range(EC):
                    for wp, xp in PASSES:
                        wt = (w_lo if wp else w_hi)[:, 128 * c:128 * c + 128]
                        xt_ = (XgT_lo if xp else XgT_hi)[:, 512 * c:512 * c + 512]
                        op_matmul(pb[bk_i][:, 0:512], wt, xt_, nmm == 0, nmm == 23)
                        nmm += 1
                wslotC_free[slotw] = inc("tensor", "tE")
                t_h = ctr["tE"]
                gsl = q % 2
                wait("scalar", "tE", t_h)
                wait("scalar", "vE", gel_free[gsl])
                if q < 2:
                    wait("scalar", "tE", last_y_tE)
                op_act(gel[gsl], pb[bk_i][:, 0:512], AF.Gelu)
                drain("scalar")
                op_act(hT_hi[:, 512 * q:512 * q + 512], gel[gsl], AF.Copy)
                drain("scalar")
                s_h = inc("scalar", "sE")
                bank_wait[bk_i] = ("sE", s_h)
                wait("vector", "sE", s_h)
                op_tt(hT_lo[:, 512 * q:512 * q + 512], gel[gsl],
                      hT_hi[:, 512 * q:512 * q + 512], ALU.subtract)
                drain("vector")
                gel_free[gsl] = inc("vector", "vE")
            last_h_tE = ctr["tE"]
            h_done_vE = ctr["vE"]

            use_banks(range(8))
            wait("tensor", "vE", h_done_vE)
            v_y_half = [0, 0]
            for half in range(2):
                v_w_y = {}
                for hq in range(32 + 2):
                    if hq < 32:
                        slotw = hq % 4
                        wait("sync", "tE", wslotC_free[slotw])
                        d_w = dma("sync", wslot_views(slotw)[0],
                                  w2f[e, 128 * hq:128 * hq + 128, :], "dS")
                        v_w_y[hq] = split_wblock(slotw, d_w)
                    if hq < 2:
                        continue
                    hh = hq - 2
                    slotw = hh % 4
                    _, w_hi, w_lo = wslot_views(slotw)
                    wait("tensor", "vE", v_w_y[hh])
                    for cc4 in range(4):
                        cc = 4 * half + cc4
                        for ip, (wp, xp) in enumerate(PASSES):
                            wt = (w_lo if wp else w_hi)[:, 128 * cc:128 * cc + 128]
                            ht_ = (hT_lo if xp else hT_hi)[:, 512 * hh:512 * hh + 512]
                            op_matmul(pb[cc][:, 0:512], wt, ht_,
                                      hh == 0 and ip == 0, hh == 31 and ip == 2)
                    wslotC_free[slotw] = inc("tensor", "tE")
                if half == 0:
                    t_half = ctr["tE"]
                    wait("vector", "tE", t_half)
                    for cc4 in range(4):
                        op_copy("vector", yT[:, 512 * cc4:512 * cc4 + 512], pb[cc4][:, 0:512])
                    v_y_half[0] = inc("vector", "vE")
                    for cc4 in range(4):
                        bank_wait[cc4] = ("vE", v_y_half[0])
            last_y_tE = ctr["tE"]
            next_xgT_v = emit_xg_trans(p + 1) if p + 1 < len(pairs) else None
            wait("vector", "tE", last_y_tE)
            for cc4 in range(4):
                cc = 4 + cc4
                op_copy("vector", yT[:, 512 * cc:512 * cc + 512], pb[cc][:, 0:512])
            v_y_half[1] = inc("vector", "vE")
            for cc4 in range(4):
                bank_wait[4 + cc4] = ("vE", v_y_half[1])
            v_y = v_y_half[1]

            wait("tensor", "vE", v_y)
            for tt in range(4):
                for cc in range(EC):
                    bk_i = (tt * EC + cc) % 4
                    use_banks([bk_i])
                    op_transpose(pb[bk_i][:, 0:128],
                                 yT[:, 512 * cc + 128 * tt:512 * cc + 128 * tt + 128],
                                 ident)
                    t_tr = inc("tensor", "tE")
                    wait("vector", "tE", t_tr)
                    if tt == 0 and cc == 0:
                        wait("vector", "dG", last_pair_scatter)
                    op_ts(y_tok[:, 1024 * tt + 128 * cc:1024 * tt + 128 * cc + 128],
                          pb[bk_i][:, 0:128], wpT[:, 32 * tt + r:32 * tt + r + 1], ALU.mult)
                    v_e3 = inc("vector", "vE")
                    bank_wait[bk_i] = ("vE", v_e3)
            last_ytok_vE = ctr["vE"]
            wait("gpsimd", "vE", last_ytok_vE)
            wait("gpsimd", "dG", scatter_done[b])
            for tt in range(4):
                g = idma_scatter_add(acc_flat, idxT[:, 32 * tt + r:32 * tt + r + 1],
                                     y_tok[:, 1024 * tt:1024 * tt + 1024])
            scatter_done[b] = g
            last_pair_scatter = g
            xgT_v = next_xgT_v

        all_scatter_dG = ctr["dG"]

        if max_phase <= 3:
            wait("gpsimd", "dG", ctr["dG"])
            wait("gpsimd", "dA", ctr["dA"])
            return _finish(nc, prog, block_ctx=None)

        dslot_free = [0, 0]
        store_slot = [0, 0]
        wait("sync", "dG", all_scatter_dG)
        for b in range(b_pc):
            for tl in range(16):
                slot = tl % 2
                row0 = b * S + 128 * tl
                wait("sync", "vE", dslot_free[slot])
                d_l = dma("sync", tokbuf[:, 4096 * slot:4096 * slot + 1024],
                          acc_flat[row0:row0 + 128, :], "dS")
                ld = tokbuf[:, 4096 * slot:4096 * slot + 1024]
                ot = tokbuf[:, 4096 * slot + 1024:4096 * slot + 2048]
                wait("vector", "dS", d_l)
                emit_ln(ld, ot, store_wait=store_slot[slot])
                v_o = ctr["vE"]
                dslot_free[slot] = v_o
                wait("scalar", "vE", v_o)
                store_slot[slot] = dma("scalar", out_d[b, 128 * tl:128 * tl + 128, :], ot, "dA")
        final_dA = ctr["dA"]
        wait("gpsimd", "dA", final_dA)
        if debug:
            wait("gpsimd", "dG", ctr["dG"])

        return _finish(nc, prog, block_ctx=None)


def _finish(nc, prog, block_ctx):
    with nc.Block() as block:
        @block.sync
        def _(q):
            for f in prog["sync"]:
                f(q)

        @block.scalar
        def _(q):
            for f in prog["scalar"]:
                f(q)

        @block.vector
        def _(q):
            for f in prog["vector"]:
                f(q)

        @block.tensor
        def _(q):
            for f in prog["tensor"]:
                f(q)

        @block.gpsimd
        def _(q):
            for f in prog["gpsimd"]:
                f(q)
    return nc


def make_consts(bq, bk, bv, bo, gw):
    c = np.zeros((128, 243), dtype=np.float32)
    c[:, 0:128] = np.eye(128, dtype=np.float32)
    c[:, 128] = 1.0
    c[:, 129:137] = 1.0
    c[::32, 137:145] = 1.0
    r = np.arange(32)
    c[0:32, 145] = (r // 8) * float(S)
    for i, b in enumerate((bq, bk, bv, bo)):
        c[:, 147 + 8 * i:155 + 8 * i] = np.asarray(b, np.float32).reshape(8, 128).T
    c[:, 179:243] = np.asarray(gw, np.float32).reshape(8, 128, 8).transpose(1, 0, 2).reshape(128, 64)
    return c


def prep_inputs(L0_attn_x, L0_attn_Wq, L0_attn_bq, L0_attn_Wk, L0_attn_bk,
                L0_attn_Wv, L0_attn_bv, L0_attn_Wo, L0_attn_bo,
                L0_ffn_gate_w, L0_ffn_w1, L0_ffn_w2, b_pc=4, n_cores=8):
    x = np.ascontiguousarray(np.asarray(L0_attn_x, dtype=np.float32))
    w1 = np.asarray(L0_ffn_w1, dtype=np.float32)
    w2 = np.asarray(L0_ffn_w2, dtype=np.float32)
    w1r = np.ascontiguousarray(
        w1.reshape(NE, 8, 128, 32, 128).transpose(0, 3, 2, 1, 4))
    w2f = np.ascontiguousarray(w2)
    consts = make_consts(L0_attn_bq, L0_attn_bk, L0_attn_bv, L0_attn_bo, L0_ffn_gate_w)
    common = dict(
        Wq=np.ascontiguousarray(L0_attn_Wq, dtype=np.float32),
        Wk=np.ascontiguousarray(L0_attn_Wk, dtype=np.float32),
        Wv=np.ascontiguousarray(L0_attn_Wv, dtype=np.float32),
        Wo=np.ascontiguousarray(L0_attn_Wo, dtype=np.float32),
        bq=np.ascontiguousarray(L0_attn_bq, dtype=np.float32),
        bk=np.ascontiguousarray(L0_attn_bk, dtype=np.float32),
        bv=np.ascontiguousarray(L0_attn_bv, dtype=np.float32),
        bo=np.ascontiguousarray(L0_attn_bo, dtype=np.float32),
        gw=np.ascontiguousarray(L0_ffn_gate_w, dtype=np.float32),
        w1r=w1r, w2f=w2f, consts=consts,
    )
    in_maps = []
    for c in range(n_cores):
        m = dict(common)
        m["x"] = np.ascontiguousarray(x[c * b_pc:(c + 1) * b_pc])
        in_maps.append(m)
    return in_maps


def kernel(**inputs):
    b_pc = 4
    n_cores = 8
    in_maps = prep_inputs(**inputs, b_pc=b_pc, n_cores=n_cores)
    nc = build_program(b_pc=b_pc, debug=False)
    res = run_bass_kernel_spmd(nc, in_maps, core_ids=list(range(n_cores)))
    out = np.concatenate([r["out"] for r in res.results], axis=0)
    return out


def check_deadlock(nc):
    trace = nc._trace
    pos = {e: 0 for e in trace}
    sems = {n: 0 for n in ("dS", "dA", "dG", "tE", "vE", "sE")}
    progress = True
    while progress:
        progress = False
        for e in trace:
            while pos[e] < len(trace[e]):
                kind, sem, val = trace[e][pos[e]]
                if kind == "inc":
                    sems[sem] += val
                    pos[e] += 1
                    progress = True
                else:
                    if sems[sem] >= val:
                        pos[e] += 1
                        progress = True
                    else:
                        break
    stuck = {e: (pos[e], len(trace[e]), trace[e][pos[e]] if pos[e] < len(trace[e]) else None)
             for e in trace if pos[e] < len(trace[e])}
    if stuck:
        print("DEADLOCK:", stuck, "sems:", sems)
        return False
    print("no deadlock;", {e: len(trace[e]) for e in trace})
    return True


# version 16
# speedup vs baseline: 1.0469x; 1.0231x over previous
import numpy as np
from contextlib import ExitStack

import concourse.bass as bass
import concourse.mybir as mybir
from concourse.bass import IndirectOffsetOnAxis
from concourse.bass_utils import run_bass_kernel_spmd

F32 = mybir.dt.float32
F32R = mybir.dt.float32r
BF16 = mybir.dt.bfloat16
U16 = mybir.dt.uint16
U32 = mybir.dt.uint32
AF = mybir.ActivationFunctionType
AX = mybir.AxisListType
ALU = mybir.AluOpType

E = 1024
EC = E // 128
S = 2048
CH = 256
NCH = S // CH
NE = 8
CAP = 512
H = 4096
NCORE = 8

ENGINES = ("sync", "scalar", "vector", "tensor", "gpsimd")
A0_STEPS = {"load", "trans", "mm", "evac"}


EPS = 1e-5


def build_program(b_pc=4, debug=False, max_phase=4, sub_phase=9):
    nc = bass.Bass("TRN2", target_bir_lowering=False)

    x_in = nc.dram_tensor("x", [b_pc, S, E], F32, kind="ExternalInput").ap()
    Wq = nc.dram_tensor("Wq", [E, E], F32, kind="ExternalInput").ap()
    Wk = nc.dram_tensor("Wk", [E, E], F32, kind="ExternalInput").ap()
    Wv = nc.dram_tensor("Wv", [E, E], F32, kind="ExternalInput").ap()
    Wo = nc.dram_tensor("Wo", [E, E], F32, kind="ExternalInput").ap()
    bq = nc.dram_tensor("bq", [E], F32, kind="ExternalInput").ap()
    bk = nc.dram_tensor("bk", [E], F32, kind="ExternalInput").ap()
    bv = nc.dram_tensor("bv", [E], F32, kind="ExternalInput").ap()
    bo = nc.dram_tensor("bo", [E], F32, kind="ExternalInput").ap()
    gw = nc.dram_tensor("gw", [E, NE], F32, kind="ExternalInput").ap()
    w1r = nc.dram_tensor("w1r", [NE, 32, 128, 8, 128], F32, kind="ExternalInput").ap()
    w2f = nc.dram_tensor("w2f", [NE, H, E], F32, kind="ExternalInput").ap()
    consts = nc.dram_tensor("consts", [128, 243], F32, kind="ExternalInput").ap()

    out_d = nc.dram_tensor("out", [b_pc, S, E], F32, kind="ExternalOutput").ap()
    dbg_kind = "ExternalOutput" if debug else "Internal"
    xt_d = nc.dram_tensor("xt_d", [b_pc * S, E], F32, kind=dbg_kind).ap()
    acc_d = nc.dram_tensor("acc_d", [b_pc * S, E], F32, kind=dbg_kind).ap()
    raw_d = nc.dram_tensor("raw_d", [b_pc, NE, S], F32, kind="Internal").ap()
    mean_d = nc.dram_tensor("mean_d", [b_pc, NE, S], F32, kind="Internal").ap()
    sdev_d = nc.dram_tensor("sdev_d", [b_pc, S], F32, kind="Internal").ap()
    if debug:
        probs_d = nc.dram_tensor("probs_d", [32, S], F32, kind="ExternalOutput").ap()
        idxt_d = nc.dram_tensor("idxt_d", [128, 128], U32, kind="ExternalOutput").ap()
        wpt_d = nc.dram_tensor("wpt_d", [128, 128], F32, kind="ExternalOutput").ap()

    ctx = ExitStack()
    sb = lambda name, shape, dt=F32: ctx.enter_context(nc.sbuf_tensor(name, shape, dt))
    psum = lambda name, shape: ctx.enter_context(nc.psum_tensor(name, shape, F32))
    mksem = lambda name: ctx.enter_context(nc.semaphore(name))

    with ctx:
        consts_sb = sb("consts_sb", [128, 243])
        bv9_sb = sb("bv9_sb", [128, 8])
        gsumneg = sb("gsumneg", [8, 1])
        eps_t = sb("eps_t", [128, 1])
        half3_t = sb("half3_t", [128, 1])
        kt_ht = sb("kt_ht", [128, 16384])
        wbuf = sb("wbuf", [128, 8192])
        tokbuf = sb("tokbuf", [128, 8192])
        actbuf = sb("actbuf", [128, 16896])
        gstage = sb("gstage", [8, 1024])
        sstage = sb("sstage", [1, 512])
        rstd_buf = sb("rstd_buf", [128, 16])
        rstdT_sb = sb("rstdT_sb", [16, 128])
        idxT = sb("idxT", [128, 128], U32)
        wpT = sb("wpT", [128, 128])
        idx16_t = sb("idx16_t", [32, 512], U16)
        stat = sb("stat", [128, 16])
        gel_t = sb("gel_t", [128, 1024])

        pb = [psum(f"pb{i}", [128, 512]) for i in range(8)]

        sems = {n: mksem(n) for n in ("dS", "dA", "dG", "tE", "vE", "sE")}
        ctr = {n: 0 for n in sems}
        prog = {e: [] for e in ENGINES}
        trace = {e: [] for e in ENGINES}
        nc._trace = trace

        wrappable = {e: False for e in ENGINES}

        def emit(e, fn, op=True):
            prog[e].append(fn)
            wrappable[e] = op

        def wait(e, sem, val):
            if val <= 0:
                return
            trace[e].append(("wait", sem, val))
            emit(e, lambda q, s=sems[sem], v=val: q.wait_ge(s, v), op=False)

        def drain(e):
            emit(e, lambda q: q.drain())

        def inc(e, sem):
            ctr[sem] += 1
            trace[e].append(("inc", sem, 1))
            assert wrappable[e], f"inc({e},{sem}) with no preceding instruction"
            prev = prog[e][-1]
            prog[e][-1] = (lambda q, p=prev, s=sems[sem]: p(q).then_inc(s, 1))
            return ctr[sem]

        def dma(e, out, in_, sem, slow=False):
            ctr[sem] += 16
            trace[e].append(("inc", sem, 16))
            emit(e, lambda q, o=out, i=in_, s=sems[sem], sl=slow:
                 q.dma_start(out=o, in_=i, allow_slow_non_contiguous=sl).then_inc(s, 16))
            return ctr[sem]

        def idma_gather(out, in_, idx_ap):
            ctr["dG"] += 16
            trace["gpsimd"].append(("inc", "dG", 16))
            emit("gpsimd", lambda q, o=out, i=in_, x=idx_ap, s=sems["dG"]:
                 q.indirect_dma_start(out=o, out_offset=None, in_=i,
                                      in_offset=IndirectOffsetOnAxis(x, 0)).then_inc(s, 16))
            return ctr["dG"]

        def idma_scatter_add(out, idx_ap, in_):
            ctr["dG"] += 16
            trace["gpsimd"].append(("inc", "dG", 16))
            emit("gpsimd", lambda q, o=out, i=in_, x=idx_ap, s=sems["dG"]:
                 q.indirect_dma_start(out=o, out_offset=IndirectOffsetOnAxis(x, 0),
                                      in_=i, in_offset=None,
                                      compute_op=ALU.add).then_inc(s, 16))
            return ctr["dG"]

        def op_matmul(o, l, r, st, sp):
            emit("tensor", lambda q, o=o, l=l, r=r, st=st, sp=sp:
                 q.matmul(o, l, r, start=st, stop=sp))

        def op_transpose(o, i, ident):
            emit("tensor", lambda q, o=o, i=i, d=ident: q.transpose(o, i, d))

        def op_copy(e, o, i):
            emit(e, lambda q, o=o, i=i: (q.tensor_copy(o, i) if e == "vector" else q.copy(o, i)))

        def op_act(o, i, func, bias=None, scale=None, accum=None):
            kw = {}
            if bias is not None:
                kw["bias"] = bias
            if scale is not None:
                kw["scale"] = scale
            if accum is not None:
                kw["accum_out"] = accum
            emit("scalar", lambda q, o=o, i=i, f=func, kw=kw: q.activation(o, i, f, **kw))

        def op_ts(o, i, s1, op0, s2=None, op1=None, accum=None):
            kw = {}
            if op1 is not None:
                kw["op1"] = op1
            if accum is not None:
                kw["accum_out"] = accum
            emit("vector", lambda q, o=o, i=i, s1=s1, s2=s2, op0=op0, kw=kw:
                 q.tensor_scalar(o, i, s1, s2, op0=op0, **kw))

        def op_stt(o, i0, s, i1, op0, op1):
            emit("vector", lambda q, o=o, i0=i0, s=s, i1=i1, a=op0, b=op1:
                 q.scalar_tensor_tensor(o, i0, s, i1, a, b))

        def op_tt(o, a, b, op):
            emit("vector", lambda q, o=o, a=a, b=b, op=op: q.tensor_tensor(o, a, b, op=op))

        def op_red(o, i, negate=False, rsum=False):
            emit("vector", lambda q, o=o, i=i, n=negate, rs=rsum:
                 (q.reduce_sum(o, i, axis=AX.X) if rs else q.reduce_max(o, i, axis=AX.X, negate=n)))

        bank_wait = [None] * 8

        def use_banks(banks):
            for i in banks:
                if bank_wait[i]:
                    wait("tensor", *bank_wait[i])
                    bank_wait[i] = None

        def emit_ln(rt, xt_a, store_wait=0):
            op_red(stat[:, 10:11], rt, rsum=True)
            drain("vector")
            op_ts(stat[:, 11:12], stat[:, 10:11], 1.0 / E, ALU.mult)
            drain("vector")
            if store_wait:
                wait("vector", "dA", store_wait)
            op_ts(xt_a, rt, stat[:, 11:12], ALU.subtract)
            v_c = inc("vector", "vE")
            wait("scalar", "vE", v_c)
            op_act(sq_scr[:], xt_a, AF.Square, accum=stat[:, 12:13])
            drain("scalar")
            op_act(stat[:, 14:15], stat[:, 12:13], AF.Ln, bias=eps_t[:], scale=1.0 / E)
            drain("scalar")
            op_act(stat[:, 13:14], stat[:, 14:15], AF.Exp, scale=-0.5)
            drain("scalar")
            s_r0 = inc("scalar", "sE")
            wait("vector", "sE", s_r0)
            op_ts(stat[:, 12:13], stat[:, 12:13], 1.0 / E, ALU.mult, s2=EPS, op1=ALU.add)
            drain("vector")
            op_tt(stat[:, 14:15], stat[:, 12:13], stat[:, 13:14], ALU.mult)
            drain("vector")
            op_tt(stat[:, 14:15], stat[:, 14:15], stat[:, 13:14], ALU.mult)
            drain("vector")
            v_nt = inc("vector", "vE")
            wait("scalar", "vE", v_nt)
            op_act(stat[:, 14:15], stat[:, 14:15], AF.Identity, bias=half3_t[:], scale=-0.5)
            drain("scalar")
            s_nt = inc("scalar", "sE")
            wait("vector", "sE", s_nt)
            op_tt(stat[:, 13:14], stat[:, 13:14], stat[:, 14:15], ALU.mult)
            drain("vector")
            op_ts(xt_a, xt_a, stat[:, 13:14], ALU.mult)
            inc("vector", "vE")

        ident = consts_sb[:, 0:128]
        ones_col = consts_sb[:, 128:129]
        ones_k8 = consts_sb[0:8, 129:137]
        bcrow8 = consts_sb[0:8, 137:145]
        boff32 = consts_sb[0:32, 145:146]

        xt_flat = xt_d
        acc_flat = acc_d

        v_setup = dma("sync", consts_sb[:], consts, "dS")
        bq_sb = consts_sb[:, 147:155]
        bk_sb = consts_sb[:, 155:163]
        bv_sb = consts_sb[:, 163:171]
        bo_sb = consts_sb[:, 171:179]
        gw_sb = consts_sb[:, 179:243]
        wait("vector", "dS", v_setup)
        op_ts(bv9_sb[:], bv_sb[:], 9.0, ALU.mult)
        emit("vector", lambda q: q.memset(eps_t[:], 1e-5))
        emit("vector", lambda q: q.memset(half3_t[:], 1.5))
        setup_v = inc("vector", "vE")
        wait("tensor", "dS", v_setup)
        for c in range(EC):
            op_matmul(pb[0][0:8, 0:8], gw_sb[:, 8 * c:8 * c + 8],
                      consts_sb[:, 129:137], c == 0, c == EC - 1)
        t_gsum = inc("tensor", "tE")
        wait("scalar", "tE", t_gsum)
        op_act(gsumneg[:], pb[0][0:8, 0:1], AF.Copy, scale=-1.0)
        s_gsum = inc("scalar", "sE")
        bank_wait[0] = ("sE", s_gsum)

        xTn = actbuf[:, 0:2048]
        QTn = actbuf[:, 2048:4096]
        Vn = actbuf[:, 4096:6144]
        combT = actbuf[:, 6144:8192]
        resT = actbuf[:, 8192:10240]
        res_tok = actbuf[:, 10240:12288]
        xt_tok = actbuf[:, 12288:14336]
        sq_scr = actbuf[:, 14336:15360]
        exp_sb = actbuf[:, 15360:15872]
        Ptil = actbuf[:, 15872:16384]
        PtilT = actbuf[:, 16384:16896]
        xT_tb = actbuf[:, 0:4096]

        tok_slot_free = [0, 0]
        wslot_free = [0, 0]
        last_scores_tE = 0
        last_AV_tE = 0
        last_O_tE = 0
        last_resTrans_tE = 0
        last_sq_sE = 0
        last_stt_vE = 0
        gstage_free = [0, 0]
        sstage_free = [0, 0]
        last_xt_dA = 0

        def load_weight_block_A(w_ap, k, slot):
            wait("sync", "tE", wslot_free[slot])
            return dma("sync", wbuf[:, 1024 * slot:1024 * slot + 1024],
                       w_ap[128 * k:128 * k + 128, :], "dS")

        for b in range(b_pc if sub_phase >= 0 else 0):
            for tb in range(4 if sub_phase >= 0 else 0):
                slot = tb % 2
                wait("sync", "tE", tok_slot_free[slot])
                d_x = dma("sync", tokbuf[:, 4096 * slot:4096 * slot + 4096]
                          .rearrange("p (a c) -> p a c", a=4),
                          x_in[b, 512 * tb:512 * tb + 512, :]
                          .rearrange("(a p) c -> p a c", p=128), "dS")
                if "trans" in A0_STEPS:
                    wait("tensor", "dS", d_x)
                    if tb == 0 and b == 0:
                        wait("tensor", "vE", setup_v)
                    for a in range(4):
                        for c in range(EC):
                            bk_i = (a * EC + c) % 4
                            use_banks([bk_i])
                            op_transpose(pb[bk_i][:, 0:128],
                                         tokbuf[:, 4096 * slot + 1024 * a + 128 * c:
                                                4096 * slot + 1024 * a + 128 * c + 128],
                                         ident)
                            t_tr = inc("tensor", "tE")
                            wait("vector", "tE", t_tr)
                            op_copy("vector", xT_tb[:, 512 * c + 128 * a:512 * c + 128 * a + 128],
                                    pb[bk_i][:, 0:128])
                            v_e = inc("vector", "vE")
                            bank_wait[bk_i] = ("vE", v_e)
                    tok_slot_free[slot] = ctr["tE"]
                if "mm" in A0_STEPS:
                    use_banks(range(8))
                    wait("tensor", "vE", ctr["vE"])
                    for k in range(EC):
                        slotw = k % 2
                        d_w = load_weight_block_A(Wk, k, slotw)
                        wait("tensor", "dS", d_w)
                        for ep in range(8):
                            op_matmul(pb[ep][:, 0:512],
                                      wbuf[:, 1024 * slotw + 128 * ep:1024 * slotw + 128 * ep + 128],
                                      xT_tb[:, 512 * k:512 * k + 512],
                                      k == 0, k == EC - 1)
                        wslot_free[slotw] = inc("tensor", "tE")
                if "evac" in A0_STEPS:
                    t_kt = ctr["tE"]
                    wait("scalar", "tE", t_kt)
                    if b > 0:
                        wait("scalar", "tE", last_scores_tE)
                    for ep in range(8):
                        op_act(kt_ht[:, 2048 * ep + 512 * tb:2048 * ep + 512 * tb + 512],
                               pb[ep][:, 0:512], AF.Identity, bias=bk_sb[:, ep:ep + 1])
                    s_kt = inc("scalar", "sE")
                    for ep in range(8):
                        bank_wait[ep] = ("sE", s_kt)
            kt_ready_sE = ctr["sE"]

            def emit_x_trans(n):
                slot = n % 2
                wait("sync", "tE", tok_slot_free[slot])
                d_x = dma("sync", tokbuf[:, 4096 * slot:4096 * slot + 2048]
                          .rearrange("p (a c) -> p a c", a=2),
                          x_in[b, 256 * n:256 * n + 256, :]
                          .rearrange("(a p) c -> p a c", p=128), "dS")
                wait("tensor", "dS", d_x)
                for a in range(2):
                    for c in range(EC):
                        bk_i = (a * EC + c) % 4
                        use_banks([bk_i])
                        op_transpose(pb[bk_i][:, 0:128],
                                     tokbuf[:, 4096 * slot + 1024 * a + 128 * c:
                                            4096 * slot + 1024 * a + 128 * c + 128],
                                     ident)
                        t_tr = inc("tensor", "tE")
                        wait("vector", "tE", t_tr)
                        op_copy("vector", xTn[:, 256 * c + 128 * a:256 * c + 128 * a + 128],
                                pb[bk_i][:, 0:128])
                        v_e = inc("vector", "vE")
                        bank_wait[bk_i] = ("vE", v_e)
                tok_slot_free[slot] = ctr["tE"]
                return ctr["vE"]

            next_xTn_v = None
            for n in range(NCH if sub_phase >= 1 else 0):
                slot = n % 2
                xTn_v = next_xTn_v if next_xTn_v is not None else emit_x_trans(n)
                next_xTn_v = None

                use_banks(range(8))
                wait("tensor", "vE", xTn_v)
                for k in range(EC):
                    slotw = k % 2
                    d_w = load_weight_block_A(Wq, k, slotw)
                    wait("tensor", "dS", d_w)
                    for ep in range(8):
                        op_matmul(pb[ep][:, 0:256],
                                  wbuf[:, 1024 * slotw + 128 * ep:1024 * slotw + 128 * ep + 128],
                                  xTn[:, 256 * k:256 * k + 256],
                                  k == 0, k == EC - 1)
                    wslot_free[slotw] = inc("tensor", "tE")
                wait("scalar", "tE", ctr["tE"])
                wait("scalar", "tE", last_scores_tE)
                for ep in range(8):
                    op_act(QTn[:, 256 * ep:256 * ep + 256], pb[ep][:, 0:256],
                           AF.Identity, bias=bq_sb[:, ep:ep + 1])
                s_q = inc("scalar", "sE")
                for ep in range(8):
                    bank_wait[ep] = ("sE", s_q)

                use_banks(range(4))
                for k in range(EC):
                    slotw = k % 2
                    d_w = load_weight_block_A(Wv, k, slotw)
                    wait("tensor", "dS", d_w)
                    for ts in range(2):
                        for eb in range(2):
                            op_matmul(pb[2 * ts + eb][:, 0:512],
                                      xTn[:, 256 * k + 128 * ts:256 * k + 128 * ts + 128],
                                      wbuf[:, 1024 * slotw + 512 * eb:1024 * slotw + 512 * eb + 512],
                                      k == 0, k == EC - 1)
                    wslot_free[slotw] = inc("tensor", "tE")
                wait("vector", "tE", ctr["tE"])
                for ts in range(2):
                    for eb in range(2):
                        op_copy("vector", Vn[:, 1024 * ts + 512 * eb:1024 * ts + 512 * eb + 512],
                                pb[2 * ts + eb][:, 0:512])
                v_v = inc("vector", "vE")
                for i in range(4):
                    bank_wait[i] = ("vE", v_v)

                if b == 0 and n == 0:
                    wait("tensor", "sE", kt_ready_sE)
                first_in_qs = [True, True]
                sc_grp_tE = {}
                s2_done = {}
                s4_done = {}
                s5_done = {}

                def sc_cols(m):
                    return 5 * (m % 2)

                def blk_ap(m):
                    qs, kb, mb = m // 8, (m // 2) % 4, m % 2
                    bk_i = 4 + 2 * qs + (kb % 2)
                    return pb[bk_i][:, 256 * mb:256 * mb + 256], bk_i

                for m in range(16 + 2):
                    if 0 <= m - 1 < 16:
                        mm_ = m - 1
                        cb = sc_cols(mm_)
                        blk, bk_i = blk_ap(mm_)
                        wait("vector", "sE", s2_done[mm_])
                        op_tt(stat[:, cb + 4:cb + 5], stat[:, cb + 1:cb + 2],
                              stat[:, cb + 3:cb + 4], ALU.subtract)
                        drain("vector")
                        if 2 * ((mm_ // 2) % 4) + (mm_ % 2) == n:
                            op_ts(stat[:, cb + 4:cb + 5], stat[:, cb + 4:cb + 5],
                                  float(np.log(2.0)), ALU.add)
                            drain("vector")
                        v_b2 = inc("vector", "vE")
                        wait("scalar", "vE", v_b2)
                        op_act(exp_sb[:, 256 * (mm_ % 2):256 * (mm_ % 2) + 256], blk, AF.Exp,
                               bias=stat[:, cb + 4:cb + 5], scale=1.0 / 32.0)
                        drain("scalar")
                        s4_done[mm_] = inc("scalar", "sE")
                        bank_wait[bk_i] = ("sE", s4_done[mm_])
                    if 0 <= m - 2:
                        mm_ = m - 2
                        qs = mm_ // 8
                        wait("vector", "sE", s4_done[mm_])
                        if first_in_qs[qs]:
                            op_copy("vector", Ptil[:, 256 * qs:256 * qs + 256],
                                    exp_sb[:, 256 * (mm_ % 2):256 * (mm_ % 2) + 256])
                            first_in_qs[qs] = False
                        else:
                            op_tt(Ptil[:, 256 * qs:256 * qs + 256],
                                  Ptil[:, 256 * qs:256 * qs + 256],
                                  exp_sb[:, 256 * (mm_ % 2):256 * (mm_ % 2) + 256], ALU.add)
                        drain("vector")
                        s5_done[mm_] = inc("vector", "vE")
                    if m < 16 and m % 2 == 0:
                        qs, kb = m // 8, (m // 2) % 4
                        bk_i = 4 + 2 * qs + (kb % 2)
                        use_banks([bk_i])
                        if kb == 0:
                            wait("tensor", "sE", s_q)
                        for c in range(EC):
                            op_matmul(pb[bk_i][:, 0:512],
                                      QTn[:, 256 * c + 128 * qs:256 * c + 128 * qs + 128],
                                      kt_ht[:, 2048 * c + 512 * kb:2048 * c + 512 * kb + 512],
                                      c == 0, c == EC - 1)
                        sc_grp_tE[m] = sc_grp_tE[m + 1] = inc("tensor", "tE")
                    if m < 16:
                        cb = sc_cols(m)
                        blk, bk_i = blk_ap(m)
                        wait("vector", "tE", sc_grp_tE[m])
                        if m % 2 == 1:
                            wait("vector", "sE", s4_done[m - 1])
                        op_red(stat[:, cb:cb + 1], blk, negate=True)
                        drain("vector")
                        op_ts(stat[:, cb + 1:cb + 2], stat[:, cb:cb + 1], 1.0 / 32.0, ALU.mult)
                        drain("vector")
                        v_m = inc("vector", "vE")
                        wait("scalar", "vE", v_m)
                        if m >= 2:
                            wait("scalar", "vE", s5_done[m - 2])
                        op_act(exp_sb[:, 256 * (m % 2):256 * (m % 2) + 256], blk, AF.Exp,
                               bias=stat[:, cb + 1:cb + 2], scale=1.0 / 32.0,
                               accum=stat[:, cb + 2:cb + 3])
                        drain("scalar")
                        op_act(stat[:, cb + 3:cb + 4], stat[:, cb + 2:cb + 3], AF.Ln)
                        drain("scalar")
                        s2_done[m] = inc("scalar", "sE")
                ptil_v = ctr["vE"]

                if sub_phase < 3:
                    continue
                wait("tensor", "vE", ptil_v)
                for qs in range(2):
                    for ks in range(2):
                        bk_i = (2 * qs + ks) % 4
                        use_banks([bk_i])
                        op_transpose(pb[bk_i][:, 0:128],
                                     Ptil[:, 256 * qs + 128 * ks:256 * qs + 128 * ks + 128],
                                     ident)
                        t_tr = inc("tensor", "tE")
                        wait("vector", "tE", t_tr)
                        op_copy("vector", PtilT[:, 256 * ks + 128 * qs:256 * ks + 128 * qs + 128],
                                pb[bk_i][:, 0:128])
                        v_e = inc("vector", "vE")
                        bank_wait[bk_i] = ("vE", v_e)
                ptilT_v = ctr["vE"]

                wait("tensor", "vE", ptilT_v)
                for ep in range(8):
                    bk_i = ep % 4
                    use_banks([bk_i])
                    for ks in range(2):
                        op_matmul(pb[bk_i][:, 0:256],
                                  Vn[:, 1024 * ks + 128 * ep:1024 * ks + 128 * ep + 128],
                                  PtilT[:, 256 * ks:256 * ks + 256],
                                  ks == 0, ks == 1)
                    last_AV_tE = inc("tensor", "tE")
                    wait("scalar", "tE", last_AV_tE)
                    if ep == 0:
                        wait("scalar", "tE", last_O_tE)
                    op_act(combT[:, 256 * ep:256 * ep + 256], pb[bk_i][:, 0:256],
                           AF.Identity, bias=bv9_sb[:, ep:ep + 1])
                    s_av = inc("scalar", "sE")
                    bank_wait[bk_i] = ("sE", s_av)
                comb_s = ctr["sE"]

                if sub_phase < 4:
                    continue
                use_banks(range(8))
                wait("tensor", "sE", comb_s)
                for c2 in range(EC):
                    slotw = c2 % 2
                    d_w = load_weight_block_A(Wo, c2, slotw)
                    wait("tensor", "dS", d_w)
                    for ep in range(8):
                        op_matmul(pb[ep][:, 0:256],
                                  wbuf[:, 1024 * slotw + 128 * ep:1024 * slotw + 128 * ep + 128],
                                  combT[:, 256 * c2:256 * c2 + 256],
                                  c2 == 0, c2 == EC - 1)
                    wslot_free[slotw] = inc("tensor", "tE")
                last_O_tE = ctr["tE"]
                wait("vector", "tE", last_O_tE)
                wait("vector", "tE", last_resTrans_tE)
                for ep in range(8):
                    op_stt(resT[:, 256 * ep:256 * ep + 256], pb[ep][:, 0:256],
                           bo_sb[:, ep:ep + 1], xTn[:, 256 * ep:256 * ep + 256],
                           ALU.add, ALU.add)
                v_res = inc("vector", "vE")
                for ep in range(8):
                    bank_wait[ep] = ("vE", v_res)

                if sub_phase < 5:
                    continue
                gslot = n % 2
                use_banks([0, 1])
                wait("tensor", "vE", v_res)
                for c in range(EC):
                    op_matmul(pb[0][0:8, 0:256], gw_sb[:, 8 * c:8 * c + 8],
                              resT[:, 256 * c:256 * c + 256], c == 0, c == EC - 1)
                for c in range(EC):
                    op_matmul(pb[1][0:8, 0:256], consts_sb[:, 129:137],
                              resT[:, 256 * c:256 * c + 256], c == 0, c == EC - 1)
                t_g = inc("tensor", "tE")
                wait("vector", "tE", t_g)
                wait("vector", "dA", gstage_free[gslot])
                op_copy("vector", gstage[0:8, 512 * gslot:512 * gslot + 256], pb[0][0:8, 0:256])
                v_g = inc("vector", "vE")
                wait("scalar", "tE", t_g)
                wait("scalar", "dA", gstage_free[gslot])
                op_act(gstage[0:8, 512 * gslot + 256:512 * gslot + 512], pb[1][0:8, 0:256],
                       AF.Copy, scale=1.0 / E)
                s_g = inc("scalar", "sE")
                bank_wait[0] = ("vE", v_g)
                bank_wait[1] = ("sE", s_g)
                wait("scalar", "vE", v_g)
                d1 = dma("scalar", raw_d[b, :, 256 * n:256 * n + 256],
                         gstage[0:8, 512 * gslot:512 * gslot + 256], "dA")
                d2 = dma("scalar", mean_d[b, :, 256 * n:256 * n + 256],
                         gstage[0:8, 512 * gslot + 256:512 * gslot + 512], "dA")
                gstage_free[gslot] = d2

                if sub_phase < 6:
                    continue
                for c2 in range(EC):
                    for a in range(2):
                        bk_i = 2 + (c2 * 2 + a) % 2
                        use_banks([bk_i])
                        op_transpose(pb[bk_i][:, 0:128],
                                     resT[:, 256 * c2 + 128 * a:256 * c2 + 128 * a + 128],
                                     ident)
                        t_tr = inc("tensor", "tE")
                        wait("vector", "tE", t_tr)
                        if c2 == 0 and a == 0:
                            wait("vector", "sE", last_sq_sE)
                        op_copy("vector", res_tok[:, 1024 * a + 128 * c2:1024 * a + 128 * c2 + 128],
                                pb[bk_i][:, 0:128])
                        v_e = inc("vector", "vE")
                        bank_wait[bk_i] = ("vE", v_e)
                last_resTrans_tE = ctr["tE"]
                v_rtok = ctr["vE"]

                sslot = n % 2
                wait("vector", "vE", v_rtok)
                for a in range(2):
                    emit_ln(res_tok[:, 1024 * a:1024 * a + 1024],
                            xt_tok[:, 1024 * a:1024 * a + 1024],
                            store_wait=(last_xt_dA if a == 0 else 0))
                    v_xt = ctr["vE"]
                    row0 = b * S + 256 * n + 128 * a
                    wait("scalar", "vE", v_xt)
                    dma("scalar", xt_flat[row0:row0 + 128, :],
                        xt_tok[:, 1024 * a:1024 * a + 1024], "dA")
                    last_xt_dA = dma("scalar", acc_flat[row0:row0 + 128, :],
                                     xt_tok[:, 1024 * a:1024 * a + 1024], "dA")
                    drain("vector")
                    op_copy("vector", rstd_buf[:, 2 * n + a:2 * n + a + 1], stat[:, 13:14])
                    inc("vector", "vE")
            use_banks([2])
            wait("tensor", "vE", ctr["vE"])
            op_transpose(pb[2][0:16, 0:128], rstd_buf[:], ident)
            t_sd = inc("tensor", "tE")
            wait("vector", "tE", t_sd)
            op_copy("vector", rstdT_sb[:], pb[2][0:16, 0:128])
            v_sd = inc("vector", "vE")
            bank_wait[2] = ("vE", v_sd)
            wait("scalar", "vE", v_sd)
            dma("scalar", sdev_d[b].rearrange("(j t) -> j t", j=16), rstdT_sb[:], "dA")
            last_scores_tE = ctr["tE"]

        if max_phase <= 1:
            wait("gpsimd", "dA", ctr["dA"])
            wait("gpsimd", "dS", ctr["dS"])
            return _finish(nc, prog, block_ctx=None)

        a_done_dA = ctr["dA"]
        a_done_vE = ctr["vE"]
        a_end_tE = ctr["tE"]
        sel = actbuf[0:32, 0:2048]
        work = [actbuf[0:32, 2048:4096], actbuf[0:32, 4096:6144]]
        val32 = actbuf[0:32, 12288:12800]
        idxf = actbuf[0:32, 12800:13312]
        idx16 = idx16_t[0:32, :]

        wait("sync", "dA", a_done_dA)
        wait("sync", "tE", a_end_tE)
        wait("tensor", "vE", a_done_vE)
        wait("vector", "tE", a_end_tE)
        wait("vector", "sE", s_gsum)
        emit("vector", lambda q: q.memset(actbuf[0:8, 6144 + 1024:6144 + 1536], 0.0))
        emit("vector", lambda q: q.memset(actbuf[0:8, 6144 + 3072 + 1024:6144 + 3072 + 1536], 0.0))
        bsdev_z = inc("vector", "vE")
        wait("sync", "vE", bsdev_z)
        gb = [(b, blk) for b in range(b_pc) for blk in range(4)]
        NG = len(gb)
        st_bc = {}; st_sc = {}; st_ex = {}; st_z = {}; st_ln = {}; st_sub = {}; st_pr = {}
        d_ld = {}

        def gv(i):
            par = i % 2
            off = 6144 + 3072 * par
            return (actbuf[0:8, off:off + 512], actbuf[0:8, off + 512:off + 1024],
                    actbuf[0:8, off + 1024:off + 1536], actbuf[0:8, off + 1536:off + 2048],
                    actbuf[0:8, off + 2048:off + 2560])

        for i in range(NG + 2):
            if 0 <= i - 1 < NG:
                j = i - 1
                braw_j, bmean_j, bsdev_j, btmp_j, blnz_j = gv(j)
                bk_b = 4 + (j % 2)
                use_banks([bk_b])
                wait("tensor", "dS", d_ld[j])
                op_matmul(pb[bk_b][0:8, 0:512], bcrow8, bsdev_j[:], True, True)
                st_bc[j] = inc("tensor", "tE")
                wait("vector", "tE", st_bc[j])
                op_stt(btmp_j[:], bmean_j[:], gsumneg[:], braw_j[:], ALU.mult, ALU.add)
                drain("vector")
                op_tt(btmp_j[:], btmp_j[:], pb[bk_b][0:8, 0:512], ALU.mult)
                drain("vector")
                st_sc[j] = inc("vector", "vE")
                bank_wait[bk_b] = ("vE", st_sc[j])
                wait("scalar", "vE", st_sc[j])
                op_act(blnz_j[:], btmp_j[:], AF.Exp)
                drain("scalar")
                st_ex[j] = inc("scalar", "sE")
            if 0 <= i - 2 < NG:
                j = i - 2
                b_, blk = gb[j]
                braw_j, bmean_j, bsdev_j, btmp_j, blnz_j = gv(j)
                bk_z = 6 + (j % 2)
                use_banks([bk_z])
                wait("tensor", "sE", st_ex[j])
                op_matmul(pb[bk_z][0:8, 0:512], ones_k8, blnz_j[:], True, True)
                st_z[j] = inc("tensor", "tE")
                wait("scalar", "tE", st_z[j])
                op_act(blnz_j[:], pb[bk_z][0:8, 0:512], AF.Ln)
                drain("scalar")
                st_ln[j] = inc("scalar", "sE")
                bank_wait[bk_z] = ("sE", st_ln[j])
                wait("vector", "sE", st_ln[j])
                op_tt(btmp_j[:], btmp_j[:], blnz_j[:], ALU.subtract)
                drain("vector")
                st_sub[j] = inc("vector", "vE")
                wait("scalar", "vE", st_sub[j])
                op_act(btmp_j[:], btmp_j[:], AF.Exp)
                drain("scalar")
                st_pr[j] = inc("scalar", "sE")
                wait("sync", "sE", st_pr[j])
                dma("sync", sel[8 * b_:8 * b_ + 8, 512 * blk:512 * blk + 512], btmp_j[:], "dS")
            if i < NG:
                b_, blk = gb[i]
                braw_i, bmean_i, bsdev_i, btmp_i, blnz_i = gv(i)
                if i >= 2:
                    wait("sync", "sE", st_pr.get(i - 2, 0))
                dma("sync", braw_i, raw_d[b_, :, 512 * blk:512 * blk + 512], "dS")
                dma("sync", bmean_i, mean_d[b_, :, 512 * blk:512 * blk + 512], "dS")
                d_ld[i] = dma("sync", bsdev_i[0:1, :], sdev_d[b_, 512 * blk:512 * blk + 512], "dS")
        sel_ready = ctr["dS"]
        if debug:
            wait("gpsimd", "dS", sel_ready)
            dma("gpsimd", probs_d, sel, "dG")

        wait("vector", "dS", sel_ready)
        cur = 0
        emit("vector", lambda q: q.tensor_copy(work[0][:], sel[:]))
        for r in range(64):
            emit("vector", lambda q, r=r, c=cur: q.max(
                val32[:, 8 * r:8 * r + 8], work[c][:]))
            drain("vector")
            emit("vector", lambda q, r=r, c=cur: q.max_index(
                idx16[:, 8 * r:8 * r + 8], val32[:, 8 * r:8 * r + 8], work[c][:]))
            drain("vector")
            emit("vector", lambda q, r=r, c=cur: q.match_replace(
                work[1 - c][:], val32[:, 8 * r:8 * r + 8], work[c][:], 0.0))
            drain("vector")
            cur = 1 - cur
        emit("vector", lambda q: q.tensor_copy(idxf[:], idx16[:]))
        drain("vector")
        op_ts(idxf[:], idxf[:], boff32, ALU.add)
        drain("vector")
        v_self = inc("vector", "vE")
        wait("tensor", "vE", v_self)
        for qb in range(4):
            use_banks([6])
            op_transpose(pb[6][:, 0:32], idxf[:, 128 * qb:128 * qb + 128], ident[0:32, 0:32])
            t_ti = inc("tensor", "tE")
            wait("vector", "tE", t_ti)
            op_copy("vector", idxT[:, 32 * qb:32 * qb + 32], pb[6][:, 0:32])
            v_ti = inc("vector", "vE")
            bank_wait[6] = ("vE", v_ti)
            use_banks([7])
            op_transpose(pb[7][:, 0:32], val32[:, 128 * qb:128 * qb + 128], ident[0:32, 0:32])
            t_tw = inc("tensor", "tE")
            wait("vector", "tE", t_tw)
            op_copy("vector", wpT[:, 32 * qb:32 * qb + 32], pb[7][:, 0:32])
            v_tw = inc("vector", "vE")
            bank_wait[7] = ("vE", v_tw)
        sel_done_vE = ctr["vE"]
        if debug:
            wait("gpsimd", "vE", sel_done_vE)
            dma("gpsimd", idxt_d, idxT[:], "dG")
            dma("gpsimd", wpt_d, wpT[:], "dG")

        if max_phase <= 2:
            wait("gpsimd", "dA", ctr["dA"])
            wait("gpsimd", "dS", ctr["dS"])
            wait("gpsimd", "dG", ctr["dG"])
            wait("gpsimd", "vE", ctr["vE"])
            return _finish(nc, prog, block_ctx=None)

        XgR = actbuf[:, 0:4096]
        XgT_hi = actbuf[:, 4096:6144].bitcast(BF16)
        XgT_lo = actbuf[:, 6144:8192].bitcast(BF16)
        yT = actbuf[:, 8192:12288]
        y_tok = actbuf[:, 12288:16384]
        hT_hi = kt_ht[:, 0:8192].bitcast(BF16)
        hT_lo = kt_ht[:, 8192:16384].bitcast(BF16)
        gel = [gel_t[:, 512 * i:512 * i + 512] for i in range(2)]

        phaseC_t0 = ctr["tE"]
        xg_gather_done = [0, 0]
        xg_slot_free = [0, 0]
        scatter_done = [0] * b_pc
        last_pair_scatter = 0
        last_h_tE = 0
        last_y_tE = 0
        wslotC_free = [a_end_tE] * 4
        gel_free = [0, 0]

        pairs = [(e, b) for e in range(NE) for b in range(b_pc)]
        PASSES = ((0, 0), (0, 1), (1, 0))

        def issue_gathers(p):
            e, b = pairs[p]
            r = 8 * b + e
            slot = p % 2
            wait("gpsimd", "tE", xg_slot_free[slot])
            if p == 0:
                wait("gpsimd", "dA", a_done_dA)
                wait("gpsimd", "vE", sel_done_vE)
            for tt in range(4):
                g = idma_gather(tokbuf[:, 4096 * slot + 1024 * tt:4096 * slot + 1024 * tt + 1024],
                                xt_flat, idxT[:, 32 * tt + r:32 * tt + r + 1])
            xg_gather_done[slot] = g
            return g

        def wslot_views(slotw):
            base = 2048 * slotw
            raw = wbuf[:, base:base + 1024]
            hi = wbuf[:, base + 1024:base + 1536].bitcast(BF16)
            lo = wbuf[:, base + 1536:base + 2048].bitcast(BF16)
            return raw, hi, lo

        def split_wblock(slotw, d_w):
            raw, hi, lo = wslot_views(slotw)
            wait("scalar", "dS", d_w)
            op_act(hi, raw, AF.Copy)
            drain("scalar")
            s_w = inc("scalar", "sE")
            wait("vector", "sE", s_w)
            op_tt(lo, raw, hi, ALU.subtract)
            drain("vector")
            return inc("vector", "vE")

        def emit_xg_trans(p):
            slot = p % 2
            wait("tensor", "dG", xg_gather_done[slot])
            for tt in range(4):
                for c in range(EC):
                    bk_i = (tt * EC + c) % 4
                    use_banks([bk_i])
                    op_transpose(pb[bk_i][:, 0:128],
                                 tokbuf[:, 4096 * slot + 1024 * tt + 128 * c:
                                        4096 * slot + 1024 * tt + 128 * c + 128],
                                 ident)
                    t_tr = inc("tensor", "tE")
                    wait("vector", "tE", t_tr)
                    if tt == 0 and c == 0:
                        wait("vector", "tE", last_h_tE)
                    op_copy("vector", XgR[:, 512 * c + 128 * tt:512 * c + 128 * tt + 128],
                            pb[bk_i][:, 0:128])
                    v_e2 = inc("vector", "vE")
                    bank_wait[bk_i] = ("vE", v_e2)
            xg_slot_free[slot] = ctr["tE"]
            wait("scalar", "vE", ctr["vE"])
            wait("scalar", "tE", last_h_tE)
            op_act(XgT_hi, XgR, AF.Copy)
            drain("scalar")
            s_x = inc("scalar", "sE")
            wait("vector", "sE", s_x)
            op_tt(XgT_lo, XgR, XgT_hi, ALU.subtract)
            drain("vector")
            return inc("vector", "vE")

        issue_gathers(0)
        xgT_v = None
        for p, (e, b) in enumerate(pairs):
            r = 8 * b + e
            slot = p % 2
            if p == 0:
                issue_gathers(1)
                xgT_v = emit_xg_trans(0)
            elif p + 1 < len(pairs):
                issue_gathers(p + 1)

            wait("tensor", "vE", xgT_v)
            v_w_h = {}
            for qq in range(32 + 2):
                if qq < 32:
                    slotw = qq % 4
                    wait("sync", "tE", wslotC_free[slotw])
                    d_w = dma("sync", wslot_views(slotw)[0]
                              .rearrange("p (c j) -> p c j", c=8),
                              w1r[e, qq], "dS")
                    v_w_h[qq] = split_wblock(slotw, d_w)
                if qq < 2:
                    continue
                q = qq - 2
                slotw = q % 4
                _, w_hi, w_lo = wslot_views(slotw)
                wait("tensor", "vE", v_w_h[q])
                bk_i = 4 + (q % 4)
                use_banks([bk_i])
                nmm = 0
                for c in range(EC):
                    for wp, xp in PASSES:
                        wt = (w_lo if wp else w_hi)[:, 128 * c:128 * c + 128]
                        xt_ = (XgT_lo if xp else XgT_hi)[:, 512 * c:512 * c + 512]
                        op_matmul(pb[bk_i][:, 0:512], wt, xt_, nmm == 0, nmm == 23)
                        nmm += 1
                wslotC_free[slotw] = inc("tensor", "tE")
                t_h = ctr["tE"]
                gsl = q % 2
                wait("scalar", "tE", t_h)
                wait("scalar", "vE", gel_free[gsl])
                if q < 2:
                    wait("scalar", "tE", last_y_tE)
                op_act(gel[gsl], pb[bk_i][:, 0:512], AF.Gelu)
                drain("scalar")
                op_act(hT_hi[:, 512 * q:512 * q + 512], gel[gsl], AF.Copy)
                drain("scalar")
                s_h = inc("scalar", "sE")
                bank_wait[bk_i] = ("sE", s_h)
                wait("vector", "sE", s_h)
                op_tt(hT_lo[:, 512 * q:512 * q + 512], gel[gsl],
                      hT_hi[:, 512 * q:512 * q + 512], ALU.subtract)
                drain("vector")
                gel_free[gsl] = inc("vector", "vE")
            last_h_tE = ctr["tE"]
            h_done_vE = ctr["vE"]

            use_banks(range(8))
            wait("tensor", "vE", h_done_vE)
            v_y_half = [0, 0]
            for half in range(2):
                v_w_y = {}
                for hq in range(32 + 2):
                    if hq < 32:
                        slotw = hq % 4
                        wait("sync", "tE", wslotC_free[slotw])
                        d_w = dma("sync", wslot_views(slotw)[0],
                                  w2f[e, 128 * hq:128 * hq + 128, :], "dS")
                        v_w_y[hq] = split_wblock(slotw, d_w)
                    if hq < 2:
                        continue
                    hh = hq - 2
                    slotw = hh % 4
                    _, w_hi, w_lo = wslot_views(slotw)
                    wait("tensor", "vE", v_w_y[hh])
                    for cc4 in range(4):
                        cc = 4 * half + cc4
                        for ip, (wp, xp) in enumerate(PASSES):
                            wt = (w_lo if wp else w_hi)[:, 128 * cc:128 * cc + 128]
                            ht_ = (hT_lo if xp else hT_hi)[:, 512 * hh:512 * hh + 512]
                            op_matmul(pb[cc][:, 0:512], wt, ht_,
                                      hh == 0 and ip == 0, hh == 31 and ip == 2)
                    wslotC_free[slotw] = inc("tensor", "tE")
                if half == 0:
                    t_half = ctr["tE"]
                    wait("vector", "tE", t_half)
                    for cc4 in range(4):
                        op_copy("vector", yT[:, 512 * cc4:512 * cc4 + 512], pb[cc4][:, 0:512])
                    v_y_half[0] = inc("vector", "vE")
                    for cc4 in range(4):
                        bank_wait[cc4] = ("vE", v_y_half[0])
            last_y_tE = ctr["tE"]
            next_xgT_v = emit_xg_trans(p + 1) if p + 1 < len(pairs) else None
            wait("vector", "tE", last_y_tE)
            for cc4 in range(4):
                cc = 4 + cc4
                op_copy("vector", yT[:, 512 * cc:512 * cc + 512], pb[cc][:, 0:512])
            v_y_half[1] = inc("vector", "vE")
            for cc4 in range(4):
                bank_wait[4 + cc4] = ("vE", v_y_half[1])
            v_y = v_y_half[1]

            wait("tensor", "vE", v_y)
            for tt in range(4):
                for cc in range(EC):
                    bk_i = (tt * EC + cc) % 4
                    use_banks([bk_i])
                    op_transpose(pb[bk_i][:, 0:128],
                                 yT[:, 512 * cc + 128 * tt:512 * cc + 128 * tt + 128],
                                 ident)
                    t_tr = inc("tensor", "tE")
                    wait("vector", "tE", t_tr)
                    if tt == 0 and cc == 0:
                        wait("vector", "dG", last_pair_scatter)
                    op_ts(y_tok[:, 1024 * tt + 128 * cc:1024 * tt + 128 * cc + 128],
                          pb[bk_i][:, 0:128], wpT[:, 32 * tt + r:32 * tt + r + 1], ALU.mult)
                    v_e3 = inc("vector", "vE")
                    bank_wait[bk_i] = ("vE", v_e3)
            last_ytok_vE = ctr["vE"]
            wait("gpsimd", "vE", last_ytok_vE)
            wait("gpsimd", "dG", scatter_done[b])
            for tt in range(4):
                g = idma_scatter_add(acc_flat, idxT[:, 32 * tt + r:32 * tt + r + 1],
                                     y_tok[:, 1024 * tt:1024 * tt + 1024])
            scatter_done[b] = g
            last_pair_scatter = g
            xgT_v = next_xgT_v

        all_scatter_dG = ctr["dG"]

        if max_phase <= 3:
            wait("gpsimd", "dG", ctr["dG"])
            wait("gpsimd", "dA", ctr["dA"])
            return _finish(nc, prog, block_ctx=None)

        dslot_free = [0, 0]
        store_slot = [0, 0]
        wait("sync", "dG", all_scatter_dG)
        for b in range(b_pc):
            for tl in range(16):
                slot = tl % 2
                row0 = b * S + 128 * tl
                wait("sync", "vE", dslot_free[slot])
                d_l = dma("sync", tokbuf[:, 4096 * slot:4096 * slot + 1024],
                          acc_flat[row0:row0 + 128, :], "dS")
                ld = tokbuf[:, 4096 * slot:4096 * slot + 1024]
                ot = tokbuf[:, 4096 * slot + 1024:4096 * slot + 2048]
                wait("vector", "dS", d_l)
                emit_ln(ld, ot, store_wait=store_slot[slot])
                v_o = ctr["vE"]
                dslot_free[slot] = v_o
                wait("scalar", "vE", v_o)
                store_slot[slot] = dma("scalar", out_d[b, 128 * tl:128 * tl + 128, :], ot, "dA")
        final_dA = ctr["dA"]
        wait("gpsimd", "dA", final_dA)
        if debug:
            wait("gpsimd", "dG", ctr["dG"])

        return _finish(nc, prog, block_ctx=None)


def _finish(nc, prog, block_ctx):
    with nc.Block() as block:
        @block.sync
        def _(q):
            for f in prog["sync"]:
                f(q)

        @block.scalar
        def _(q):
            for f in prog["scalar"]:
                f(q)

        @block.vector
        def _(q):
            for f in prog["vector"]:
                f(q)

        @block.tensor
        def _(q):
            for f in prog["tensor"]:
                f(q)

        @block.gpsimd
        def _(q):
            for f in prog["gpsimd"]:
                f(q)
    return nc


def make_consts(bq, bk, bv, bo, gw):
    c = np.zeros((128, 243), dtype=np.float32)
    c[:, 0:128] = np.eye(128, dtype=np.float32)
    c[:, 128] = 1.0
    c[:, 129:137] = 1.0
    c[::32, 137:145] = 1.0
    r = np.arange(32)
    c[0:32, 145] = (r // 8) * float(S)
    for i, b in enumerate((bq, bk, bv, bo)):
        c[:, 147 + 8 * i:155 + 8 * i] = np.asarray(b, np.float32).reshape(8, 128).T
    c[:, 179:243] = np.asarray(gw, np.float32).reshape(8, 128, 8).transpose(1, 0, 2).reshape(128, 64)
    return c


def prep_inputs(L0_attn_x, L0_attn_Wq, L0_attn_bq, L0_attn_Wk, L0_attn_bk,
                L0_attn_Wv, L0_attn_bv, L0_attn_Wo, L0_attn_bo,
                L0_ffn_gate_w, L0_ffn_w1, L0_ffn_w2, b_pc=4, n_cores=8):
    x = np.ascontiguousarray(np.asarray(L0_attn_x, dtype=np.float32))
    w1 = np.asarray(L0_ffn_w1, dtype=np.float32)
    w2 = np.asarray(L0_ffn_w2, dtype=np.float32)
    w1r = np.ascontiguousarray(
        w1.reshape(NE, 8, 128, 32, 128).transpose(0, 3, 2, 1, 4))
    w2f = np.ascontiguousarray(w2)
    consts = make_consts(L0_attn_bq, L0_attn_bk, L0_attn_bv, L0_attn_bo, L0_ffn_gate_w)
    common = dict(
        Wq=np.ascontiguousarray(L0_attn_Wq, dtype=np.float32),
        Wk=np.ascontiguousarray(L0_attn_Wk, dtype=np.float32),
        Wv=np.ascontiguousarray(L0_attn_Wv, dtype=np.float32),
        Wo=np.ascontiguousarray(L0_attn_Wo, dtype=np.float32),
        bq=np.ascontiguousarray(L0_attn_bq, dtype=np.float32),
        bk=np.ascontiguousarray(L0_attn_bk, dtype=np.float32),
        bv=np.ascontiguousarray(L0_attn_bv, dtype=np.float32),
        bo=np.ascontiguousarray(L0_attn_bo, dtype=np.float32),
        gw=np.ascontiguousarray(L0_ffn_gate_w, dtype=np.float32),
        w1r=w1r, w2f=w2f, consts=consts,
    )
    in_maps = []
    for c in range(n_cores):
        m = dict(common)
        m["x"] = np.ascontiguousarray(x[c * b_pc:(c + 1) * b_pc])
        in_maps.append(m)
    return in_maps


def kernel(**inputs):
    b_pc = 4
    n_cores = 8
    in_maps = prep_inputs(**inputs, b_pc=b_pc, n_cores=n_cores)
    nc = build_program(b_pc=b_pc, debug=False)
    res = run_bass_kernel_spmd(nc, in_maps, core_ids=list(range(n_cores)))
    out = np.concatenate([r["out"] for r in res.results], axis=0)
    return out


def check_deadlock(nc):
    trace = nc._trace
    pos = {e: 0 for e in trace}
    sems = {n: 0 for n in ("dS", "dA", "dG", "tE", "vE", "sE")}
    progress = True
    while progress:
        progress = False
        for e in trace:
            while pos[e] < len(trace[e]):
                kind, sem, val = trace[e][pos[e]]
                if kind == "inc":
                    sems[sem] += val
                    pos[e] += 1
                    progress = True
                else:
                    if sems[sem] >= val:
                        pos[e] += 1
                        progress = True
                    else:
                        break
    stuck = {e: (pos[e], len(trace[e]), trace[e][pos[e]] if pos[e] < len(trace[e]) else None)
             for e in trace if pos[e] < len(trace[e])}
    if stuck:
        print("DEADLOCK:", stuck, "sems:", sems)
        return False
    print("no deadlock;", {e: len(trace[e]) for e in trace})
    return True
